# version 04
# baseline (speedup 1.0000x reference)
import numpy as np
from contextlib import ExitStack

import concourse.bass as bass
import concourse.mybir as mybir
from concourse.bass_utils import run_bass_kernel_spmd

F32 = mybir.dt.float32
BF16 = mybir.dt.bfloat16
F16 = mybir.dt.float16
I32 = mybir.dt.int32
U16 = mybir.dt.uint16
U8 = mybir.dt.uint8
AF = mybir.ActivationFunctionType
ALU = mybir.AluOpType
AX = mybir.AxisListType

ENGINES = ("sync", "vector", "scalar", "tensor", "gpsimd")
T = 2048
D = 2048
NCORES = 8
INV_SQRT_HD = float(1.0 / np.sqrt(128.0))


class P:

    def __init__(self):
        self.steps = {e: [] for e in ENGINES}
        self.sems = {}
        self.order = []

    def sem(self, name):
        if name not in self.sems:
            self.sems[name] = 0
            self.order.append(name)
        return name

    def val(self, name):
        self.sem(name)
        return self.sems[name]

    def emit(self, eng, fn, waits=(), inc=None, inc_n=1):
        waits = tuple((s, v) for (s, v) in waits if v > 0)
        if inc is not None:
            self.sem(inc)
            self.sems[inc] += inc_n
        self.steps[eng].append((fn, waits, inc, inc_n))

    def dma(self, eng, out, in_, waits=(), inc=None, **kw):
        self.emit(eng, lambda e, out=out, in_=in_, kw=kw: e.dma_start(out=out, in_=in_, **kw),
                  waits=waits, inc=inc, inc_n=16)

    def barrier(self):
        bar = self.sem("bar")
        for e in ENGINES:
            self.emit(e, lambda eng: eng.nop(), inc="bar")
        tgt = self.sems["bar"]
        for e in ENGINES:
            self.steps[e].append((None, ((bar, tgt),), None, 0))

    def finalize(self, nc, ctx, final_wait=None):
        handles = {n: ctx.enter_context(nc.semaphore(n)) for n in self.order}
        with nc.Block() as block:
            for eng_name in ENGINES:
                if not self.steps[eng_name] and eng_name != "sync":
                    continue

                def mk(eng_name=eng_name):
                    def body(eng):
                        for fn, waits, inc, inc_n in self.steps[eng_name]:
                            for s, v in waits:
                                eng.wait_ge(handles[s], v)
                            if fn is None:
                                continue
                            inst = fn(eng)
                            if inc:
                                inst.then_inc(handles[inc], inc_n)
                        if eng_name == "sync" and final_wait is not None:
                            eng.wait_ge(handles[final_wait[0]], final_wait[1])
                    return body
                getattr(block, eng_name)(mk())



SBUF_TOP = 229344


def make_carver(nc, base):
    state = {"off": base}

    def carve(name, shape, dtype):
        nbytes = int(np.prod(shape[1:])) * mybir.dt.size(dtype)
        off = (state["off"] + 31) // 32 * 32
        t = nc.alloc_sbuf_tensor_at(name, shape, dtype, offset=off)
        state["off"] = off + nbytes
        assert state["off"] <= SBUF_TOP, f"carve overflow at {name}: {state['off']}"
        return t

    def reset():
        state["off"] = base
    return carve, reset


def build_launch(phase_emitters, dram_decls, aliases=None):
    nc = bass.Bass("TRN2", target_bir_lowering=False, debug=False)
    io = {}
    for name, shape, dtype, kind in dram_decls:
        io[name] = nc.dram_tensor(name, shape, dtype, kind=kind).ap()
    if aliases:
        for k, v in aliases.items():
            io[k] = io[v]
    ctx = ExitStack()
    env = {"nc": nc, "io": io}
    pb = [nc.alloc_psum_tensor(f"pb{i}", [128, 512], F32) for i in range(8)]
    env["pb"] = pb

    def sb_fixed(name, shape, dtype):
        return nc.alloc_sbuf_tensor(name, shape, dtype)
    env["sb"] = sb_fixed
    p = P()
    p.sem("init")
    common_small(p, env)
    common_identf(p, env)
    arena_base = SBUF_TOP - nc.sbuf_bytes_remaining
    carve, reset = make_carver(nc, arena_base)
    p.barrier()
    for i, emit in enumerate(phase_emitters):
        reset()
        pref = f"ph{i}_"
        env["sb"] = (lambda pref=pref: lambda name, shape, dtype:
                     carve(pref + name, shape, dtype))()
        waits = emit(p, env)
        for s, v in waits:
            p.emit("sync", lambda e: e.nop(), waits=[(s, v)])
        p.barrier()
    p.finalize(nc, ctx, final_wait=None)
    env["ctx"] = ctx
    return nc


def common_small(p, env):
    nc = env["nc"]
    sb = env["sb"]
    ident = sb("c_ident", [128, 128], BF16)
    m_incl = sb("c_mincl", [128, 128], BF16)
    m_strict = sb("c_mstr", [128, 128], BF16)
    ones_col = sb("c_ones", [128, 1], BF16)
    ones_row = sb("c_onesr", [1, 128], F32)
    env["ident"], env["m_incl"], env["m_strict"] = ident, m_incl, m_strict
    env["ones_col"], env["ones_row"] = ones_col, ones_row
    p.emit("gpsimd", lambda e: e.memset(ident[:], 0.0))
    p.emit("gpsimd", lambda e: e.affine_select(
        out=ident[:], in_=ident[:], compare_op=ALU.not_equal, fill=1.0,
        base=0, pattern=[[-1, 128]], channel_multiplier=1))
    p.emit("gpsimd", lambda e: e.memset(m_incl[:], 1.0))
    p.emit("gpsimd", lambda e: e.affine_select(
        out=m_incl[:], in_=m_incl[:], compare_op=ALU.is_ge, fill=0.0,
        base=0, pattern=[[1, 128]], channel_multiplier=-1))
    p.emit("gpsimd", lambda e: e.memset(m_strict[:], 1.0))
    p.emit("gpsimd", lambda e: e.affine_select(
        out=m_strict[:], in_=m_strict[:], compare_op=ALU.is_gt, fill=0.0,
        base=0, pattern=[[1, 128]], channel_multiplier=-1))
    p.emit("gpsimd", lambda e: e.memset(ones_col[:], 1.0))
    p.emit("gpsimd", lambda e: e.memset(ones_row[:], 1.0), inc="init")


def emit_ln(p, env, xin, xlt, pref):
    nc = env["nc"]
    sb = env["sb"]
    s_xc = [sb(f"{pref}_xc{i}", [128, T], F32) for i in range(2)]
    s_sq = sb(f"{pref}_sq", [128, T], F32)
    s_tln = [sb(f"{pref}_tln{i}", [128, T], BF16) for i in range(2)]
    sm = {k: sb(f"{pref}_{k}", [128, 1], F32)
          for k in ("ssq", "ssum", "nm", "t2e", "mm", "ve", "lnv", "rstd", "negmr")}
    env[pref + "_xc"] = s_xc
    env[pref + "_sq"] = s_sq
    env[pref + "_tln"] = s_tln
    ld = p.sem(pref + "ld")
    ch = p.sem(pref + "ch")
    app = p.sem(pref + "app")
    ptr = p.sem(pref + "ptr")
    dev = p.sem(pref + "dev")
    ident = env["ident"]
    pT = env["pb"][0].bitcast(BF16)

    def chain(eng, fn, extra=()):
        p.emit(eng, fn, waits=[(ch, p.val(ch))] + list(extra), inc=pref + "ch")

    for ci in range(16):
        b = ci % 2
        p.dma("sync", s_xc[b][:], xin[128 * ci:128 * (ci + 1), :],
              waits=[(ch, p.val(ch) - 10)], inc=pref + "ld")
        ldv = p.val(ld)
        chain("vector", lambda e, b=b: e.reduce_sum(sm["ssum"][:], s_xc[b][:], axis=AX.X),
              extra=[(ld, ldv)])
        chain("scalar", lambda e, b=b: e.activation(s_sq[:], s_xc[b][:], AF.Square,
                                                    accum_out=sm["ssq"][:]))
        chain("vector", lambda e: e.tensor_scalar(sm["nm"][:], sm["ssum"][:], -1.0 / D, None, ALU.mult))
        chain("vector", lambda e: e.tensor_scalar(sm["t2e"][:], sm["ssq"][:], 1.0 / D, 1e-5, ALU.mult, ALU.add))
        chain("vector", lambda e: e.tensor_mul(sm["mm"][:], sm["nm"][:], sm["nm"][:]))
        chain("vector", lambda e: e.tensor_sub(sm["ve"][:], sm["t2e"][:], sm["mm"][:]))
        chain("scalar", lambda e: e.activation(sm["lnv"][:], sm["ve"][:], AF.Ln))
        chain("scalar", lambda e: e.activation(sm["rstd"][:], sm["lnv"][:], AF.Exp, scale=-0.5))
        chain("vector", lambda e: e.tensor_mul(sm["negmr"][:], sm["nm"][:], sm["rstd"][:]))
        chain("scalar", lambda e, b=b: e.activation(s_tln[b][:], s_xc[b][:], AF.Identity,
                                                    bias=sm["negmr"][:], scale=sm["rstd"][:]),
              extra=[(dev, p.val(dev) - 4)])
        for g in range(4):
            for k in range(4):
                fj = 4 * g + k
                p.emit("tensor", lambda e, b=b, fj=fj, k=k: e.transpose(
                    pT[:, 128 * k:128 * (k + 1)], s_tln[b][:, 128 * fj:128 * (fj + 1)], ident[:]),
                    waits=[(ch, p.val(ch)), (dev, p.val(dev))] if k == 0 else [],
                    inc=pref + "ptr" if k == 3 else None)
            dst = xlt[:].rearrange("p (f t) -> p f t", f=16)[:, 4 * g:4 * g + 4,
                                                            128 * ci:128 * (ci + 1)]
            src = pT[:, 0:512].rearrange("p (f t) -> p f t", f=4)
            p.emit("vector", lambda e, dst=dst, src=src: e.tensor_copy(dst, src),
                   waits=[(ptr, p.val(ptr))], inc=pref + "dev")


def emit_k1(p, env):
    nc = env["nc"]; sb = env["sb"]; pb = env["pb"]; io = env["io"]
    xin, wqkv, wout, x1 = io["xin"], io["wqkv"], io["wout"], io["x1"]
    xlt = sb("xlt", [128, 16 * T], BF16)
    emit_ln(p, env, xin, xlt, "ln")
    p.barrier()

    s_w = [sb(f"wslab{i}", [128, 16 * 512], BF16) for i in range(2)]
    s_q = sb("s_q", [128, 16 * 512], BF16)
    s_k = sb("s_k", [128, 16 * 512], BF16)
    s_v = sb("s_v", [128, 4 * D], BF16)
    s_o = sb("s_o", [128, 16 * 512], BF16)
    s_pt = [sb(f"s_pt{i}", [128, 512], BF16) for i in range(4)]
    s_rs = sb("s_rs", [1, 512], F32)
    s_rsb = sb("s_rsb", [128, 512], F32)
    s_xcs = [env["ln_tln"][0][:].bitcast(F32)[:, 512 * i:512 * (i + 1)] for i in range(2)]
    s_ys = [env["ln_tln"][1][:].bitcast(F32)[:, 512 * i:512 * (i + 1)] for i in range(2)]

    ldw = p.sem("ldw")
    pes = p.sem("pes")
    ace = p.sem("ace")
    dvm = p.sem("dvm")
    pev = p.sem("pev")
    aev = p.sem("aev")
    dvn = p.sem("dvn")
    ldx = p.sem("ldx")
    dvy = p.sem("dvy")
    sty = p.sem("sty")

    for hb in range(4):
        th0 = 512 * hb
        aev0 = p.val(aev)
        for sg in range(12):
            wb = sg % 2
            p.dma("gpsimd", s_w[wb][:],
                  wqkv[:, 512 * sg:512 * (sg + 1)].rearrange("(ic p) o -> p ic o", p=128),
                  waits=[(aev, aev0 + 4 * (sg - 1))] if sg >= 2 else [], inc="ldw")
            ldv = p.val(ldw)
            for u in range(4):
                bank = pb[4 + u]
                for ic in range(16):
                    if sg < 8:
                        fn = (lambda e, wb=wb, ic=ic, u=u, th0=th0, bank=bank: e.matmul(
                            bank[:], s_w[wb][:, 512 * ic + 128 * u:512 * ic + 128 * (u + 1)],
                            xlt[:, 2048 * ic + th0:2048 * ic + th0 + 512],
                            start=(ic == 0), stop=(ic == 15)))
                    else:
                        fn = (lambda e, wb=wb, ic=ic, u=u, th0=th0, bank=bank: e.matmul(
                            bank[:], xlt[:, 2048 * ic + th0 + 128 * u:2048 * ic + th0 + 128 * (u + 1)],
                            s_w[wb][:, 512 * ic:512 * (ic + 1)],
                            start=(ic == 0), stop=(ic == 15)))
                    p.emit("tensor", fn,
                           waits=([(ldw, ldv),
                                   (aev, aev0 + 4 * (sg - 1) + u + 1 if sg >= 1 else 0)]
                                  if ic == 0 else []),
                           inc=("pes" if ic == 15 else None))
                if sg < 8:
                    dst, off = (s_q, (sg % 4) * 4 + u) if sg < 4 else (s_k, ((sg - 4) % 4) * 4 + u)
                    dap = dst[:, 512 * off:512 * (off + 1)]
                else:
                    dap = s_v[:, 2048 * u + 512 * (sg - 8):2048 * u + 512 * (sg - 8) + 512]
                p.emit("scalar", lambda e, dap=dap, bank=bank: e.activation(dap, bank[:], AF.Copy),
                       waits=[(pes, p.val(pes))], inc="aev")
        p.barrier()

        sB = [pb[0], pb[1], pb[5], pb[6]]
        LAG = 3
        for j in range(16):
            mask_of = {}
            pv_of = {}

            def emit_pv(c, j=j, mask_of=mask_of, pv_of=pv_of):
                Tp, jp = c // 16, c % 16
                col0 = 128 * Tp
                ptb = s_pt[c % 4]
                p.emit("tensor", lambda e, c=c, Tp=Tp, jp=jp, ptb=ptb, col0=col0: e.matmul(
                    pb[2][:, col0:512],
                    s_v[:, 2048 * Tp + 128 * jp:2048 * Tp + 128 * (jp + 1)],
                    ptb[:, col0:512], start=(c == 0), stop=(c == 63),
                    skip_group_check=True),
                    waits=[(dvm, mask_of[c])] + ([(dvn, p.val(dvn))] if c == 0 else []),
                    inc="pev")
                p.emit("tensor", lambda e, c=c, ptb=ptb, col0=col0: e.matmul(
                    pb[3][0:1, col0:512], env["ones_col"][:], ptb[:, col0:512],
                    start=(c == 0), stop=(c == 63), skip_group_check=True),
                    inc="pev")
                pv_of[c] = p.val(pev)

            for c in range(64):
                Tp, jp = c // 16, c % 16
                col0 = 128 * Tp
                sbank = sB[c % 4]
                ptb = s_pt[c % 4]
                p.emit("tensor", lambda e, j=j, Tp=Tp, jp=jp, col0=col0, sbank=sbank: e.matmul(
                    sbank[:, col0:512],
                    s_k[:, 512 * jp + 128 * Tp:512 * jp + 128 * (Tp + 1)],
                    s_q[:, 512 * j + col0:512 * j + 512],
                    start=True, stop=True, skip_group_check=True),
                    waits=[(ace, p.val(ace) - 4), (dvn, p.val(dvn))] if c <= 3 else
                          [(ace, p.val(ace) - 4)],
                    inc="pes")
                qk_done = p.val(pes)
                p.emit("scalar", lambda e, ptb=ptb, col0=col0, sbank=sbank: e.activation(
                    ptb[:, col0:512], sbank[:, col0:512], AF.Exp, scale=INV_SQRT_HD),
                    waits=[(pes, qk_done), (pev, pv_of.get(c - 4, 0))], inc="ace")
                mask = env["m_incl"] if jp <= j else env["m_strict"]
                p.emit("vector", lambda e, ptb=ptb, col0=col0, mask=mask: e.tensor_mul(
                    ptb[:, col0:col0 + 128], ptb[:, col0:col0 + 128], mask[:]),
                    waits=[(ace, p.val(ace))], inc="dvm")
                mask_of[c] = p.val(dvm)
                if c >= LAG:
                    emit_pv(c - LAG)
            for c in range(64 - LAG, 64):
                emit_pv(c)
            p.emit("vector", lambda e: e.reciprocal(s_rs[:], pb[3][0:1, :]),
                   waits=[(pev, p.val(pev))], inc="dvn")
            p.emit("tensor", lambda e: e.matmul(pb[4][:], env["ones_row"][:], s_rs[:],
                                                start=True, stop=True),
                   waits=[(dvn, p.val(dvn)), (aev, p.val(aev))], inc="pes")
            p.emit("scalar", lambda e: e.activation(s_rsb[:], pb[4][:], AF.Copy),
                   waits=[(pes, p.val(pes))], inc="aev")
            p.emit("vector", lambda e, j=j: e.tensor_mul(
                s_o[:, 512 * j:512 * (j + 1)], pb[2][:], s_rsb[:]),
                   waits=[(aev, p.val(aev))], inc="dvn")
        p.barrier()

        ldw0 = p.val(ldw)
        dvy0 = p.val(dvy)
        for og in range(4):
            wb = og % 2
            p.dma("gpsimd", s_w[wb][:],
                  wout[:, 512 * og:512 * (og + 1)].rearrange("(ic p) o -> p ic o", p=128),
                  waits=[(dvy, dvy0 + 4 * (og - 1))] if og >= 2 else [], inc="ldw")
            ldv = p.val(ldw)
            for tc in range(4):
                i = (og * 4 + tc) % 2
                bank = pb[4 + tc % 4]
                p.dma("sync", s_xcs[i],
                      xin[th0 + 128 * tc:th0 + 128 * (tc + 1), 512 * og:512 * (og + 1)],
                      waits=[(dvy, p.val(dvy) - 1)], inc="ldx")
                ldxv = p.val(ldx)
                for ec in range(16):
                    p.emit("tensor", lambda e, wb=wb, ec=ec, tc=tc, bank=bank: e.matmul(
                        bank[:], s_o[:, 512 * ec + 128 * tc:512 * ec + 128 * (tc + 1)],
                        s_w[wb][:, 512 * ec:512 * (ec + 1)],
                        start=(ec == 0), stop=(ec == 15)),
                        waits=([(ldw, ldv), (dvy, p.val(dvy))] if ec == 0 else []),
                        inc=("pes" if ec == 15 else None))
                p.emit("vector", lambda e, i=i, bank=bank: e.scalar_tensor_tensor(
                    s_ys[i], bank[:], 1.0, s_xcs[i], ALU.mult, ALU.add),
                    waits=[(pes, p.val(pes)), (ldx, ldxv), (sty, p.val(sty) - 16)],
                    inc="dvy")
                p.dma("sync", x1[th0 + 128 * tc:th0 + 128 * (tc + 1), 512 * og:512 * (og + 1)],
                      s_ys[i], waits=[(dvy, p.val(dvy))], inc="sty")
        p.barrier()

    return [("sty", p.val("sty"))]


def emit_dec_fp4(p, env, u16v, nC2, s_W, scT_bcast, wv_maker, waits_first, inc_last,
                 dec):
    k_, sh_, b_, t_, vp_, v_ = (dec[n] for n in ("k", "sh", "b", "t", "vp", "v"))
    for jj in range(4):
        w0 = waits_first if jj == 0 else ()
        if jj == 0:
            p.emit("vector", lambda e: e.tensor_scalar(
                k_[:, :nC2], u16v, 7, None, ALU.bitwise_and), waits=w0)
        else:
            p.emit("vector", lambda e, jj=jj: e.tensor_scalar(
                k_[:, :nC2], u16v, 4 * jj, 7, ALU.logical_shift_right, ALU.bitwise_and))
        if jj == 3:
            p.emit("vector", lambda e: e.tensor_scalar(
                sh_[:, :nC2], u16v, 0x8000, None, ALU.bitwise_and))
        else:
            p.emit("vector", lambda e, jj=jj: e.tensor_scalar(
                sh_[:, :nC2], u16v, 12 - 4 * jj, 0x8000,
                ALU.logical_shift_left, ALU.bitwise_and))
        p.emit("vector", lambda e: e.tensor_scalar(
            b_[:, :nC2], k_[:, :nC2], 64.0, 16128.0, ALU.mult, ALU.add))
        p.emit("vector", lambda e: e.tensor_scalar(
            t_[:, :nC2], b_[:, :nC2].bitcast(BF16), 2.0, -1.0, ALU.mult, ALU.add))
        p.emit("vector", lambda e: e.tensor_tensor(
            vp_[:, :nC2], t_[:, :nC2], b_[:, :nC2].bitcast(BF16), ALU.min))
        p.emit("vector", lambda e: e.tensor_tensor(
            v_[:, :nC2].bitcast(U16), vp_[:, :nC2].bitcast(U16), sh_[:, :nC2],
            ALU.bitwise_xor))
        wv, ng = wv_maker(jj)
        vv = v_[:, :nC2].rearrange("p (g k) -> p g k", g=ng)
        p.emit("vector", lambda e, wv=wv, vv=vv: e.tensor_tensor(wv, vv, scT_bcast, ALU.mult),
               inc=(inc_last if jj == 3 else None))


def emit_scale_T(p, env, dram_slice, s_h, s_b16, pbank, s_out, nrows, waits, inc):
    ident = env["ident"]
    p.dma("sync", s_h[0:nrows, :], dram_slice,
          waits=list(waits) + [("pesc", p.val("pesc"))], inc="ldsc")
    v = p.val("ldsc")
    p.emit("vector", lambda e: e.tensor_copy(s_b16[0:nrows, :], s_h[0:nrows, :]),
           waits=[("ldsc", v)], inc="dvsc")
    p.emit("tensor", lambda e: e.transpose(pbank.bitcast(BF16)[:, 0:nrows],
                                           s_b16[0:nrows, :], ident[0:nrows, 0:nrows]),
           waits=[("dvsc", p.val("dvsc")), (inc, p.val(inc))], inc="pesc")
    p.emit("vector", lambda e: e.tensor_copy(s_out[:, 0:nrows], pbank.bitcast(BF16)[:, 0:nrows]),
           waits=[("pesc", p.val("pesc"))], inc=inc)


def emit_k2(p, env):
    nc = env["nc"]; sb = env["sb"]; pb = env["pb"]; io = env["io"]
    x1, gup, gus, dp, ds, x2, wdT0, yd = (io["x1"], io["gup"], io["gus"], io["dp"],
                                          io["ds"], io["x2"], io["wdT0"], io["yd"])
    wdT = wdT0
    for s in ("ldsc", "dvsc", "pesc"):
        p.sem(s)
    xlt = sb("xlt", [128, 16 * T], BF16)
    emit_ln(p, env, x1, xlt, "ln")

    dec = {n: sb(f"dec_{n}", [128, 1408], U16 if n in ("k", "sh", "b") else BF16)
           for n in ("k", "sh", "b", "t", "vp", "v")}
    s_pk = [sb(f"s_pk{i}", [128, 704], I32) for i in range(2)]
    s_sch = sb("s_sch", [44, 128], F16)
    s_scb = sb("s_scb", [44, 128], BF16)
    s_scT = [sb(f"s_scT{i}", [128, 44], F32) for i in range(2)]
    s_W = [sb(f"s_W{i}", [128, 5632], BF16) for i in range(2)]
    s_wT = [sb(f"s_wT{i}", [128, 2048], BF16) for i in range(2)]
    s_st = [sb(f"s_st{i}", [128, 512], BF16) for i in range(2)]
    s_h = [sb(f"s_h{i}", [128, 2048], BF16) for i in range(4)]
    s_sg = sb("s_sg", [128, 2048], F32)
    s_wd = [sb(f"s_wd{i}", [128, 2048], BF16) for i in range(4)]
    s_yst = [sb(f"s_yst{i}", [128, 512], F32) for i in range(2)]
    ldp = p.sem("ldp")
    dvd = p.sem("dvd")
    pet = p.sem("pet")
    dve = p.sem("dve")
    stw = p.sem("stw")
    pem = p.sem("pem")
    ach = p.sem("ach")
    dvh = p.sem("dvh")
    ldd = p.sem("ldd")
    pdn = p.sem("pdn")
    dvo = p.sem("dvo")
    sta = p.sem("sta")
    ldy = p.sem("ldy")
    dvz = p.sem("dvz")
    acz = p.sem("acz")
    stz = p.sem("stz")
    p.barrier()

    for f in range(16):
        fb = f % 2
        p.dma("sync", s_pk[fb][:], dp[128 * f:128 * (f + 1), :],
              waits=[(dvd, p.val(dvd) - 1)], inc="ldp")
        ldv = p.val(ldp)
        emit_scale_T(p, env, ds[:, 128 * f:128 * (f + 1)], s_sch, s_scb, pb[1],
                     s_scT[fb], 44, [(dvd, p.val(dvd) - 1)], "dvsc2" if False else "dvscT")
        scv = p.val("dvscT")
        scb = s_scT[fb][:, 0:44].unsqueeze(2).broadcast_to([128, 44, 32])

        def wv_mk(jj, fb=fb):
            return s_W[fb][:, jj::4].rearrange("p (g k) -> p g k", g=44), 44
        emit_dec_fp4(p, env, s_pk[fb][:].bitcast(U16), 1408, s_W[fb], scb, wv_mk,
                     [(ldp, ldv), ("dvscT", scv), (pet, p.val(pet))], "dvd", dec)
        dcv = p.val(dvd)
        for g in range(11):
            for k in range(4):
                c = 4 * g + k
                p.emit("tensor", lambda e, fb=fb, c=c, k=k: e.transpose(
                    pb[0].bitcast(BF16)[:, 128 * k:128 * (k + 1)],
                    s_W[fb][:, 128 * c:128 * (c + 1)], env["ident"][:]),
                    waits=([(dvd, dcv), (dve, p.val(dve))] if k == 0 else []),
                    inc=("pet" if k == 3 else None))
            sb_i = (f * 11 + g) % 2
            p.emit("vector", lambda e, sb_i=sb_i: e.tensor_copy(
                s_st[sb_i][:], pb[0].bitcast(BF16)[:, 0:512]),
                waits=[(pet, p.val(pet)), (stw, p.val(stw) - 16)], inc="dve")
            dst = wdT[512 * g:512 * (g + 1), 128 * f:128 * (f + 1)].rearrange(
                "(t p) f -> p t f", p=128)
            src = s_st[sb_i][:].rearrange("p (t f) -> p t f", t=4)
            p.dma("scalar", dst, src, waits=[(dve, p.val(dve))], inc="stw")
    p.barrier()

    s_gsh = sb("s_gsh", [16, 128], F16)
    s_gsb = sb("s_gsb", [16, 128], BF16)
    s_gsT = [sb(f"s_gsT{i}", [128, 16], F32) for i in range(2)]
    stw_total2 = p.val(stw)
    sta_base = p.val(sta)
    for g in range(11):
        for ci in range(4):
            c = 4 * g + ci
            for half, row0 in ((0, 128 * c), (1, 5632 + 128 * c)):
                hb2 = half
                p.dma("sync", s_pk[hb2][:, 0:256], gup[row0:row0 + 128, :],
                      waits=[(dvd, p.val(dvd) - 1)], inc="ldp")
                ldv = p.val(ldp)
                emit_scale_T(p, env, gus[:, row0:row0 + 128], s_gsh, s_gsb, pb[1],
                             s_gsT[hb2], 16, [("dvscT", p.val("dvscT") - 1)], "dvscT")
                scv = p.val("dvscT")
                scb = s_gsT[hb2][:, 0:16].unsqueeze(2).broadcast_to([128, 16, 32])

                def wv_mk(jj, hb2=hb2):
                    return s_W[hb2][:, jj::4][:, 0:512].rearrange(
                        "p (g k) -> p g k", g=16), 16
                emit_dec_fp4(p, env, s_pk[hb2][:, 0:256].bitcast(U16), 512,
                             s_W[hb2], scb, wv_mk,
                             [(ldp, ldv), ("dvscT", scv), (pet, p.val(pet))], "dvd", dec)
                dcv = p.val(dvd)
                for grp in range(4):
                    for k in range(4):
                        ic = 4 * grp + k
                        p.emit("tensor", lambda e, hb2=hb2, ic=ic, k=k: e.transpose(
                            pb[0].bitcast(BF16)[:, 128 * k:128 * (k + 1)],
                            s_W[hb2][:, 128 * ic:128 * (ic + 1)], env["ident"][:]),
                            waits=([(dvd, dcv), (dve, p.val(dve))] if k == 0 else []),
                            inc=("pet" if k == 3 else None))
                    p.emit("vector", lambda e, hb2=hb2, grp=grp: e.tensor_copy(
                        s_wT[hb2][:].rearrange("p (ic q) -> p ic q", ic=16)[:, 4 * grp:4 * grp + 4, :],
                        pb[0].bitcast(BF16)[:, 0:512].rearrange("p (t q) -> p t q", t=4)),
                        waits=[(pet, p.val(pet)), (pem, p.val(pem) - 4)], inc="dve")
                devv = p.val(dve)
                bank = pb[4 + half]
                for tg in range(4):
                    cons = ("dvg", p.val("dvg")) if half == 0 else (dvh, p.val(dvh))
                    for ic in range(16):
                        p.emit("tensor", lambda e, hb2=hb2, ic=ic, tg=tg, bank=bank: e.matmul(
                            bank[:], s_wT[hb2][:, 128 * ic:128 * (ic + 1)],
                            xlt[:, 2048 * ic + 512 * tg:2048 * ic + 512 * (tg + 1)],
                            start=(ic == 0), stop=(ic == 15)),
                            waits=([(dve, devv), cons] if ic == 0 else []),
                            inc=("pem" if ic == 15 else None))
                    if half == 0:
                        p.emit("vector", lambda e, tg=tg: e.tensor_copy(
                            s_sg[:, 512 * tg:512 * (tg + 1)], pb[4][:]),
                            waits=[(pem, p.val(pem)), (dvh, p.val(dvh) - 3 + tg)],
                            inc="dvg")
                        p.emit("scalar", lambda e, tg=tg: e.activation(
                            s_sg[:, 512 * tg:512 * (tg + 1)],
                            s_sg[:, 512 * tg:512 * (tg + 1)], AF.Silu),
                            waits=[("dvg", p.val("dvg"))], inc="ach")
                    else:
                        p.emit("vector", lambda e, ci=ci, tg=tg: e.tensor_mul(
                            s_h[ci][:, 512 * tg:512 * (tg + 1)],
                            s_sg[:, 512 * tg:512 * (tg + 1)], pb[5][:]),
                            waits=[(pem, p.val(pem)), (ach, p.val(ach)),
                                   (pdn, p.val(pdn))],
                            inc="dvh")
        h_done = p.val(dvh)
        for ci in range(4):
            c = 4 * g + ci
            p.dma("gpsimd", s_wd[ci][:], wdT[128 * c:128 * (c + 1), :],
                  waits=[(stw, stw_total2), (pdn, p.val(pdn))], inc="ldd")
        lddv = p.val(ldd)
        for tc in range(16):
            for fg in range(4):
                idx = tc * 4 + fg
                bank = pb[6 + idx % 2]
                for ci in range(4):
                    p.emit("tensor", lambda e, ci=ci, tc=tc, fg=fg, bank=bank: e.matmul(
                        bank[:], s_h[ci][:, 128 * tc:128 * (tc + 1)],
                        s_wd[ci][:, 512 * fg:512 * (fg + 1)],
                        start=(ci == 0), stop=(ci == 3)),
                        waits=([(ldd, lddv), (dvh, h_done), (dvo, p.val(dvo) - 1)]
                               if ci == 0 else []),
                        inc=("pdn" if ci == 3 else None))
                yb = idx % 2
                p.emit("vector", lambda e, yb=yb, bank=bank: e.tensor_copy(
                    s_yst[yb][:], bank[:]),
                    waits=[(pdn, p.val(pdn)), (sta, p.val(sta) - 16)], inc="dvo")
                p.dma("gpsimd", yd[128 * tc:128 * (tc + 1), 512 * fg:512 * (fg + 1)],
                      s_yst[yb][:],
                      waits=[(dvo, p.val(dvo)), (sta, sta_base + 16 * 64 * g)], inc="sta",
                      accum_op=(ALU.add if g > 0 else ALU.bypass))
    p.barrier()

    s_yc = [env["ln_xc"][i][:] for i in range(2)]
    s_xc3 = [s_W[i].bitcast(F32)[:, 0:2048] for i in range(2)]
    s_sq2 = env["ln_sq"]
    xltf = xlt[:].bitcast(F32)
    s_o2 = [xltf[:, 2048 * i:2048 * (i + 1)] for i in range(2)]
    smz = {k: sb(f"fz_{k}", [128, 1], F32) for k in ("ssq", "t2e", "lnv", "rstd")}
    sta_total = p.val(sta)
    for tc in range(16):
        b2 = tc % 2
        p.dma("sync", s_yc[b2], yd[128 * tc:128 * (tc + 1), :],
              waits=[("fch", p.val("fch") - 5), (sta, sta_total)], inc="ldy")
        p.dma("sync", s_xc3[b2], x1[128 * tc:128 * (tc + 1), :],
              waits=[("fch", p.val("fch") - 5)], inc="ldy")
        ldv = p.val(ldy)
        p.emit("scalar", lambda e, b2=b2: e.activation(s_sq2[:], s_yc[b2], AF.Square,
                                                       accum_out=smz["ssq"][:]),
               waits=[(ldy, ldv), ("fch", p.val("fch"))], inc="fch")
        p.emit("vector", lambda e: e.tensor_scalar(smz["t2e"][:], smz["ssq"][:],
                                                   1.0 / D, 1e-6, ALU.mult, ALU.add),
               waits=[("fch", p.val("fch"))], inc="fch")
        p.emit("scalar", lambda e: e.activation(smz["lnv"][:], smz["t2e"][:], AF.Ln),
               waits=[("fch", p.val("fch"))], inc="fch")
        p.emit("scalar", lambda e: e.activation(smz["rstd"][:], smz["lnv"][:], AF.Exp,
                                                scale=-0.5),
               waits=[("fch", p.val("fch"))], inc="fch")
        p.emit("vector", lambda e, b2=b2: e.scalar_tensor_tensor(
            s_o2[b2], s_yc[b2], smz["rstd"][:], s_xc3[b2], ALU.mult, ALU.add),
            waits=[("fch", p.val("fch")), (stz, p.val(stz) - 16)], inc="fch")
        p.dma("sync", x2[128 * tc:128 * (tc + 1), :], s_o2[b2],
              waits=[("fch", p.val("fch"))], inc="stz")

    return [("stz", p.val("stz"))]


def common_identf(p, env):
    sb = env["sb"]
    identf = sb("c_identf", [128, 128], F32)
    env["identf"] = identf
    p.emit("gpsimd", lambda e: e.memset(identf[:], 0.0))
    p.emit("gpsimd", lambda e: e.affine_select(
        out=identf[:], in_=identf[:], compare_op=ALU.not_equal, fill=1.0,
        base=0, pattern=[[-1, 128]], channel_multiplier=1))


def emit_bias_T(p, env, bvec, s_bh, s_out, tag):
    pbk = env["pb"][1]
    p.dma("sync", s_bh[:], bvec.rearrange("(c p) -> c p", p=128),
          waits=[(tag + "bt", p.val(tag + "bt"))], inc=tag + "bl")
    p.emit("tensor", lambda e: e.transpose(pbk[:, 0:16], s_bh[:], env["identf"][0:16, 0:16]),
           waits=[(tag + "bl", p.val(tag + "bl")), (tag + "bt", p.val(tag + "bt"))],
           inc=tag + "bp")
    p.emit("vector", lambda e: e.tensor_copy(s_out[:], pbk[:, 0:16]),
           waits=[(tag + "bp", p.val(tag + "bp"))], inc=tag + "bt")


def emit_rope_tables(p, env, pos, s_sinf, s_cosf):
    sb = env["sb"]
    pb = env["pb"]
    s_dr = sb("rp_dr", [1, 128], I32)
    s_jr = sb("rp_jr", [1, 128], I32)
    s_jf = sb("rp_jf", [1, 128], F32)
    s_if = sb("rp_if", [1, 128], F32)
    s_pos = sb("rp_pos", [1, T], F32)
    s_u = sb("rp_u", [128, 512], F32)
    s_i = sb("rp_i", [128, 512], I32)
    s_f = sb("rp_f", [128, 512], F32)
    s_f2 = sb("rp_f2", [128, 512], F32)
    rc = p.sem("rc")

    def chain(eng, fn, extra=()):
        p.emit(eng, fn, waits=[("rc", p.val("rc"))] + list(extra), inc="rc")

    p.dma("sync", s_pos[:], pos, inc="rl")
    rl = p.val("rl")
    chain("gpsimd", lambda e: e.iota(s_dr[:], pattern=[[1, 128]], base=0, channel_multiplier=0))
    chain("vector", lambda e: e.tensor_scalar(s_jr[:], s_dr[:], 1, None, ALU.logical_shift_right))
    chain("vector", lambda e: e.tensor_copy(s_jf[:], s_jr[:]))
    LN1E4 = float(np.log(10000.0))
    chain("scalar", lambda e: e.activation(s_if[:], s_jf[:], AF.Exp, scale=-2.0 * LN1E4 / 128.0))
    INV2PI = float(1.0 / (2.0 * np.pi))
    TWOPI = float(2.0 * np.pi)
    for tg in range(4):
        chain("tensor", lambda e, tg=tg: e.matmul(
            pb[2][:], s_if[:], s_pos[:, 512 * tg:512 * (tg + 1)], start=True, stop=True),
            extra=[("rl", rl)])
        for phase, dst in ((0.0, s_sinf), (0.25, s_cosf)):
            chain("vector", lambda e, ph=phase: e.tensor_scalar(
                s_u[:], pb[2][:], INV2PI, ph, ALU.mult, ALU.add))
            chain("vector", lambda e: e.tensor_copy(s_i[:], s_u[:]))
            chain("vector", lambda e: e.tensor_copy(s_f[:], s_i[:]))
            chain("vector", lambda e: e.tensor_sub(s_f2[:], s_u[:], s_f[:]))
            chain("scalar", lambda e, dst=dst, tg=tg: e.activation(
                dst[:, 512 * tg:512 * (tg + 1)], s_f2[:], AF.Sin, scale=TWOPI))


def emit_psw(p, env, s_psw):
    sb = env["sb"]
    s_i1 = sb("pw_i1", [128, 128], F32)
    s_c1 = sb("pw_c1", [128, 128], F32)
    s_c2 = sb("pw_c2", [128, 128], F32)
    s_pi = sb("pw_pi", [128, 1], I32)
    s_pk = sb("pw_pk", [128, 1], I32)
    s_pkf = sb("pw_pkf", [128, 1], F32)
    s_ev = sb("pw_ev", [128, 1], F32)

    def chain(eng, fn):
        p.emit(eng, fn, waits=[("rc", p.val("rc"))], inc="rc")
    chain("gpsimd", lambda e: e.iota(s_i1[:], pattern=[[1, 128]], base=0, channel_multiplier=-1, allow_small_or_imprecise_dtypes=True))
    chain("vector", lambda e: e.tensor_single_scalar(s_c1[:], s_i1[:], 1.0, ALU.is_equal))
    chain("vector", lambda e: e.tensor_single_scalar(s_c2[:], s_i1[:], -1.0, ALU.is_equal))
    chain("gpsimd", lambda e: e.iota(s_pi[:], pattern=[[0, 1]], base=0, channel_multiplier=1))
    chain("vector", lambda e: e.tensor_single_scalar(s_pk[:], s_pi[:], 1, ALU.bitwise_and))
    chain("vector", lambda e: e.tensor_copy(s_pkf[:], s_pk[:]))
    chain("vector", lambda e: e.tensor_scalar(s_ev[:], s_pkf[:], -1.0, 1.0, ALU.mult, ALU.add))
    chain("vector", lambda e: e.tensor_scalar(s_c1[:], s_c1[:], s_ev[:], None, ALU.mult))
    chain("vector", lambda e: e.tensor_scalar(s_c2[:], s_c2[:], s_pkf[:], None, ALU.mult))
    chain("vector", lambda e: e.tensor_sub(s_psw[:], s_c1[:], s_c2[:]))


def emit_k3a(p, env):
    nc = env["nc"]; sb = env["sb"]; pb = env["pb"]; io = env["io"]
    x2, wq, wk, wv = io["x2"], io["wq"], io["wk"], io["wv"]
    bq, bk, bv, pos = io["bq"], io["bk"], io["bv"], io["pos"]
    qT, kT, vv = io["qT"], io["kT"], io["vv"]
    xlt = sb("xlt", [128, 16 * T], BF16)
    emit_ln(p, env, x2, xlt, "ln")
    s_sinf = sb("s_sinf", [128, T], F32)
    s_cosf = sb("s_cosf", [128, T], F32)
    emit_rope_tables(p, env, pos, s_sinf, s_cosf)
    s_psw = sb("s_psw", [128, 128], F32)
    emit_psw(p, env, s_psw)
    s_pswb = sb("s_pswb", [128, 128], BF16)
    p.emit("vector", lambda e: e.tensor_copy(s_pswb[:], s_psw[:]),
           waits=[("rc", p.val("rc"))], inc="rc")
    s_bqT = sb("s_bqT", [128, 16], F32)
    s_bkT = sb("s_bkT", [128, 16], F32)
    s_bvR = sb("s_bvR", [1, D], F32)
    s_bvB = sb("s_bvB", [128, D], F32)
    s_bh = sb("s_bh", [16, 128], F32)
    emit_bias_T(p, env, bq, s_bh, s_bqT, "q")
    emit_bias_T(p, env, bk, s_bh, s_bkT, "k")
    p.dma("sync", s_bvR[:], bv.rearrange("(a d) -> a d", a=1), inc="rl")
    rlv = p.val("rl")
    for tg in range(4):
        p.emit("tensor", lambda e, tg=tg: e.matmul(
            pb[3][:], env["ones_row"][:], s_bvR[:, 512 * tg:512 * (tg + 1)],
            start=True, stop=True),
            waits=[("rl", rlv), ("rc", p.val("rc"))], inc="rc")
        p.emit("vector", lambda e, tg=tg: e.tensor_copy(
            s_bvB[:, 512 * tg:512 * (tg + 1)], pb[3][:]),
            waits=[("rc", p.val("rc"))], inc="rc")
    p.barrier()

    s_w = [sb(f"wslab{i}", [128, 16 * 512], BF16) for i in range(2)]
    s_qs = [sb(f"s_qs{i}", [128, 512], BF16) for i in range(2)]
    s_m1 = sb("s_m1", [128, 512], F32)
    s_m2 = sb("s_m2", [128, 512], F32)
    s_qo = [sb(f"s_qo{i}", [128, 512], BF16) for i in range(2)]
    ldw = p.sem("ldw")
    pes = p.sem("pes")
    aev = p.sem("aev")
    per = p.sem("per")
    dvr = p.sem("dvr")
    stq = p.sem("stq")

    for mat, (wmat, bT, dst) in enumerate(((wq, s_bqT, qT), (wk, s_bkT, kT))):
        for og in range(4):
            wb = og % 2
            p.dma("gpsimd", s_w[wb][:],
                  wmat[:, 512 * og:512 * (og + 1)].rearrange("(ic p) o -> p ic o", p=128),
                  waits=[(pes, p.val(pes) - 16)], inc="ldw")
            ldv = p.val(ldw)
            for oc in range(4):
                h = 4 * og + oc
                for tg in range(4):
                    i = (h * 4 + tg) % 2
                    bank = pb[4 + tg % 2]
                    for ic in range(16):
                        p.emit("tensor", lambda e, wb=wb, ic=ic, oc=oc, tg=tg, bank=bank: e.matmul(
                            bank[:], s_w[wb][:, 512 * ic + 128 * oc:512 * ic + 128 * (oc + 1)],
                            xlt[:, 2048 * ic + 512 * tg:2048 * ic + 512 * (tg + 1)],
                            start=(ic == 0), stop=(ic == 15)),
                            waits=([(ldw, ldv), (aev, p.val(aev) - 1)] if ic == 0 else []),
                            inc=("pes" if ic == 15 else None))
                    p.emit("scalar", lambda e, i=i, h=h, bank=bank, bT=bT: e.activation(
                        s_qs[i][:], bank[:], AF.Identity, bias=bT[:, h:h + 1]),
                        waits=[(pes, p.val(pes)), (per, p.val(per) - 1),
                               (dvr, p.val(dvr) - 3)], inc="aev")
                    p.emit("tensor", lambda e, i=i: e.matmul(
                        pb[6][:], s_pswb[:], s_qs[i][:], start=True, stop=True),
                        waits=[(aev, p.val(aev)), (dvr, p.val(dvr) - 1)], inc="per")
                    p.emit("vector", lambda e, i=i, tg=tg: e.tensor_mul(
                        s_m1[:], s_qs[i][:], s_cosf[:, 512 * tg:512 * (tg + 1)]),
                        waits=[(aev, p.val(aev))])
                    p.emit("vector", lambda e, tg=tg: e.tensor_mul(
                        s_m2[:], pb[6][:], s_sinf[:, 512 * tg:512 * (tg + 1)]),
                        waits=[(per, p.val(per))], inc="dvr")
                    p.emit("vector", lambda e, i=i: e.tensor_add(s_qo[i][:], s_m1[:], s_m2[:]),
                           waits=[(stq, p.val(stq) - 16)], inc="dvr")
                    p.dma("scalar", dst[128 * h:128 * (h + 1), 512 * tg:512 * (tg + 1)],
                          s_qo[i][:], waits=[(dvr, p.val(dvr))], inc="stq")
    p.barrier()

    s_vo = [sb(f"s_vo{i}", [128, 512], BF16) for i in range(2)]
    for og in range(4):
        wb = og % 2
        p.dma("gpsimd", s_w[wb][:],
              wv[:, 512 * og:512 * (og + 1)].rearrange("(ic p) o -> p ic o", p=128),
              waits=[(pes, p.val(pes) - 16)], inc="ldw")
        ldv = p.val(ldw)
        for tc in range(16):
            i = (og * 16 + tc) % 2
            bank = pb[4 + tc % 2]
            for ic in range(16):
                p.emit("tensor", lambda e, wb=wb, ic=ic, tc=tc, bank=bank: e.matmul(
                    bank[:], xlt[:, 2048 * ic + 128 * tc:2048 * ic + 128 * (tc + 1)],
                    s_w[wb][:, 512 * ic:512 * (ic + 1)],
                    start=(ic == 0), stop=(ic == 15)),
                    waits=([(ldw, ldv), ("dvv", p.val("dvv") - 1)] if ic == 0 else []),
                    inc=("pes" if ic == 15 else None))
            p.emit("vector", lambda e, i=i, og=og, bank=bank: e.tensor_add(
                s_vo[i][:], bank[:], s_bvB[:, 512 * og:512 * (og + 1)]),
                waits=[(pes, p.val(pes)), (stq, p.val(stq) - 16)], inc="dvv")
            p.dma("scalar", vv[128 * tc:128 * (tc + 1), 512 * og:512 * (og + 1)],
                  s_vo[i][:], waits=[("dvv", p.val("dvv"))], inc="stq")

    return [("stq", p.val("stq"))]


def emit_k3b(p, env):
    nc = env["nc"]; sb = env["sb"]; pb = env["pb"]; io = env["io"]
    qTd, kg, vg = io["qT2"], io["kg"], io["vg"]
    wo, bo, x2, x3 = io["wo"], io["bo"], io["x2b"], io["x3"]
    s_boR = sb("s_boR", [1, D], F32)
    s_boB = sb("s_boB", [128, D], F32)
    p.dma("sync", s_boR[:], bo.rearrange("(a d) -> a d", a=1), inc="rl")
    rlv = p.val("rl")
    for tg in range(4):
        p.emit("tensor", lambda e, tg=tg: e.matmul(
            pb[3][:], env["ones_row"][:], s_boR[:, 512 * tg:512 * (tg + 1)],
            start=True, stop=True),
            waits=[("rl", rlv), ("rc", p.val("rc"))], inc="rc")
        p.emit("vector", lambda e, tg=tg: e.tensor_copy(
            s_boB[:, 512 * tg:512 * (tg + 1)], pb[3][:]),
            waits=[("rc", p.val("rc"))], inc="rc")
    p.barrier()

    s_kh = [sb(f"s_kh{i}", [128, 4 * T], BF16) for i in range(2)]
    s_vh = [sb(f"s_vh{i}", [128, 4 * T], BF16) for i in range(2)]
    s_qt = [sb(f"s_qt{i}", [128, 512], BF16) for i in range(2)]
    s_pt = [sb(f"s_pt{i}", [128, 512], BF16) for i in range(4)]
    s_rs = sb("s_rs", [1, 512], F32)
    s_rsb = sb("s_rsb", [128, 512], F32)
    s_oall = sb("s_oall", [128, 16 * T], BF16)
    ldk = p.sem("ldk")
    ldq = p.sem("ldq")
    pes = p.sem("pes")
    ace = p.sem("ace")
    pev = p.sem("pev")
    aev = p.sem("aev")
    dvn = p.sem("dvn")

    for h in range(16):
        hb = h % 2
        for r in range(4):
            p.dma("gpsimd", s_kh[hb][:, 2048 * r:2048 * (r + 1)],
                  kg[2048 * r + 128 * h:2048 * r + 128 * (h + 1), :],
                  waits=[(pev, p.val(pev) - 512)] if h >= 2 else [], inc="ldk")
            p.dma("gpsimd", s_vh[hb][:, 2048 * r:2048 * (r + 1)],
                  vg[2048 * r:2048 * (r + 1), 128 * h:128 * (h + 1)].rearrange(
                      "(tc p) d -> p tc d", p=128),
                  inc="ldk")
        ldkv = p.val(ldk)
        sB = [pb[0], pb[1], pb[5], pb[6]]
        LAG = 3
        for tg in range(4):
            qb = (h * 4 + tg) % 2
            p.dma("sync", s_qt[qb][:], qTd[128 * h:128 * (h + 1), 512 * tg:512 * (tg + 1)],
                  waits=[(pes, p.val(pes) - 66)], inc="ldq")
            ldqv = p.val(ldq)
            exp_of = {}
            pv_of = {}

            def emit_pv(kc, hb=hb):
                ptb = s_pt[kc % 4]
                p.emit("tensor", lambda e, kc=kc, hb=hb, ptb=ptb: e.matmul(
                    pb[2][:], s_vh[hb][:, 128 * kc:128 * (kc + 1)], ptb[:],
                    start=(kc == 0), stop=(kc == 63), skip_group_check=True),
                    waits=[(ace, exp_of[kc])] + ([(dvn, p.val(dvn))] if kc == 0 else []),
                    inc="pev")
                p.emit("tensor", lambda e, kc=kc, ptb=ptb: e.matmul(
                    pb[3][0:1, :], env["ones_col"][:], ptb[:],
                    start=(kc == 0), stop=(kc == 63), skip_group_check=True),
                    inc="pev")
                pv_of[kc] = p.val(pev)

            for kc in range(64):
                sbank = sB[kc % 4]
                ptb = s_pt[kc % 4]
                p.emit("tensor", lambda e, kc=kc, hb=hb, qb=qb, sbank=sbank: e.matmul(
                    sbank[:], s_kh[hb][:, 128 * kc:128 * (kc + 1)], s_qt[qb][:],
                    start=True, stop=True, skip_group_check=True),
                    waits=([(ldk, ldkv), (ldq, ldqv), (ace, p.val(ace) - 1),
                            (dvn, p.val(dvn))] if kc <= 3 else
                           [(ace, exp_of.get(kc - 4, 0))]),
                    inc="pes")
                qk_done = p.val(pes)
                p.emit("scalar", lambda e, ptb=ptb, sbank=sbank: e.activation(
                    ptb[:], sbank[:], AF.Exp, scale=INV_SQRT_HD),
                    waits=[(pes, qk_done), (pev, pv_of.get(kc - 4, 0))], inc="ace")
                exp_of[kc] = p.val(ace)
                if kc >= LAG:
                    emit_pv(kc - LAG)
            for kc in range(64 - LAG, 64):
                emit_pv(kc)
            p.emit("vector", lambda e: e.reciprocal(s_rs[:], pb[3][0:1, :]),
                   waits=[(pev, p.val(pev))], inc="dvn")
            p.emit("tensor", lambda e: e.matmul(pb[4][:], env["ones_row"][:], s_rs[:],
                                                start=True, stop=True),
                   waits=[(dvn, p.val(dvn)), (aev, p.val(aev))], inc="pes")
            p.emit("scalar", lambda e: e.activation(s_rsb[:], pb[4][:], AF.Copy),
                   waits=[(pes, p.val(pes))], inc="aev")
            p.emit("vector", lambda e, h=h, tg=tg: e.tensor_mul(
                s_oall[:, 2048 * h + 512 * tg:2048 * h + 512 * (tg + 1)], pb[2][:], s_rsb[:]),
                   waits=[(aev, p.val(aev))], inc="dvn")
    p.barrier()

    s_w = [sb(f"wslab{i}", [128, 16 * 512], BF16) for i in range(2)]
    s_xcs = [sb(f"s_xcs{i}", [128, 512], F32) for i in range(2)]
    s_t1 = sb("s_t1", [128, 512], F32)
    s_ys = [sb(f"s_ys{i}", [128, 512], F32) for i in range(2)]
    ldx = p.sem("ldx")
    dvy = p.sem("dvy")
    sty = p.sem("sty")
    ldw = p.sem("ldw")
    for og in range(4):
        wb = og % 2
        p.dma("gpsimd", s_w[wb][:],
              wo[:, 512 * og:512 * (og + 1)].rearrange("(ic p) o -> p ic o", p=128),
              waits=[(dvy, p.val(dvy) - 16)] if og >= 2 else [], inc="ldw")
        ldv = p.val(ldw)
        for tc in range(16):
            i = (og * 16 + tc) % 2
            bank = pb[4 + tc % 2]
            p.dma("sync", s_xcs[i][:],
                  x2[128 * tc:128 * (tc + 1), 512 * og:512 * (og + 1)],
                  waits=[(dvy, p.val(dvy) - 1)], inc="ldx")
            ldxv = p.val(ldx)
            for ec in range(16):
                p.emit("tensor", lambda e, wb=wb, ec=ec, tc=tc, bank=bank: e.matmul(
                    bank[:], s_oall[:, 2048 * ec + 128 * tc:2048 * ec + 128 * (tc + 1)],
                    s_w[wb][:, 512 * ec:512 * (ec + 1)],
                    start=(ec == 0), stop=(ec == 15)),
                    waits=([(ldw, ldv), (dvy, p.val(dvy))] if ec == 0 else []),
                    inc=("pes" if ec == 15 else None))
            p.emit("vector", lambda e, og=og, bank=bank: e.tensor_add(
                s_t1[:], bank[:], s_boB[:, 512 * og:512 * (og + 1)]),
                waits=[(pes, p.val(pes)), (ldx, ldxv), (sty, p.val(sty) - 16)])
            p.emit("vector", lambda e, i=i: e.tensor_add(s_ys[i][:], s_t1[:], s_xcs[i][:]),
                   inc="dvy")
            p.dma("sync", x3[128 * tc:128 * (tc + 1), 512 * og:512 * (og + 1)],
                  s_ys[i][:], waits=[(dvy, p.val(dvy))], inc="sty")

    return [("sty", p.val("sty"))]


def emit_dec_nvfp4(p, env, u8v4, nC, s_W, scb, wv_maker, waits_first, inc_last, dec):
    u8c, u16, k_, sh_, b_, t_, vp_, v_ = (dec[n] for n in
                                          ("u8c", "u16", "k", "sh", "b", "t", "vp", "v"))
    p.emit("vector", lambda e: e.tensor_copy(u8c[:, :nC], u8v4[:, 0::4]), waits=waits_first)
    p.emit("vector", lambda e: e.tensor_copy(u16[:, :nC], u8c[:, :nC]))
    for (kshift, sshift), par in (((0, 12), 0), ((4, 8), 1)):
        if kshift == 0:
            p.emit("vector", lambda e: e.tensor_scalar(k_[:, :nC], u16[:, :nC], 7, None, ALU.bitwise_and))
        else:
            p.emit("vector", lambda e, ks=kshift: e.tensor_scalar(
                k_[:, :nC], u16[:, :nC], ks, 7, ALU.logical_shift_right, ALU.bitwise_and))
        p.emit("vector", lambda e, ss=sshift: e.tensor_scalar(
            sh_[:, :nC], u16[:, :nC], ss, 0x8000, ALU.logical_shift_left, ALU.bitwise_and))
        p.emit("vector", lambda e: e.tensor_scalar(b_[:, :nC], k_[:, :nC], 64.0, 16128.0, ALU.mult, ALU.add))
        p.emit("vector", lambda e: e.tensor_scalar(t_[:, :nC], b_[:, :nC].bitcast(BF16), 2.0, -1.0, ALU.mult, ALU.add))
        p.emit("vector", lambda e: e.tensor_tensor(vp_[:, :nC], t_[:, :nC], b_[:, :nC].bitcast(BF16), ALU.min))
        p.emit("vector", lambda e: e.tensor_tensor(v_[:, :nC].bitcast(U16), vp_[:, :nC].bitcast(U16), sh_[:, :nC], ALU.bitwise_xor))
        wv, ng = wv_maker(par)
        vv_ = v_[:, :nC].rearrange("p (g k) -> p g k", g=ng)
        p.emit("vector", lambda e, wv=wv, vv_=vv_: e.tensor_tensor(wv, vv_, scb, ALU.mult),
               inc=(inc_last if par == 1 else None))


def emit_k5(p, env):
    nc = env["nc"]; sb = env["sb"]; pb = env["pb"]; io = env["io"]
    x3, out, wdT = io["x3b"], io["out"], io["wdT1"]
    gWp, gs, uWp, us, dWp, dsc = (io["gWp"], io["gs"], io["uWp"], io["us"],
                                  io["dWp"], io["dsc"])
    xlt = sb("xlt", [128, 16 * T], BF16)
    emit_ln(p, env, x3, xlt, "ln")
    p.dma("gpsimd", out, x3, inc="sti")
    sti_v = p.val("sti")
    sta_base = p.val("sta") if "sta" in p.sems else 0

    dec = {}
    for n in ("u8c",):
        dec[n] = sb(f"dec_{n}", [128, 1024], U8)
    for n in ("u16", "k", "sh", "b"):
        dec[n] = sb(f"dec_{n}", [128, 1024], U16)
    for n in ("t", "vp", "v"):
        dec[n] = sb(f"dec_{n}", [128, 1024], BF16)
    s_pk = [sb(f"s_pk{i}", [128, 1024], I32) for i in range(2)]
    s_sc = [sb(f"s_sc{i}", [128, 64], F32) for i in range(2)]
    s_W = [sb(f"s_W{i}", [128, 2048], BF16) for i in range(2)]
    s_wT = [sb(f"s_wT{i}", [128, 2048], BF16) for i in range(2)]
    s_st = [sb(f"s_st{i}", [128, 512], BF16) for i in range(2)]
    s_h = [sb(f"s_h{i}", [128, 2048], BF16) for i in range(8)]
    s_gg = sb("s_gg", [128, 2048], F32)
    _xc = env["ln_xc"]
    _sq = env["ln_sq"]
    _tl0 = nc.lookup_handle("ln_tln0") if False else None
    s_wd = [
        _xc[0][:].bitcast(BF16)[:, 0:2048], _xc[0][:].bitcast(BF16)[:, 2048:4096],
        _xc[1][:].bitcast(BF16)[:, 0:2048], _xc[1][:].bitcast(BF16)[:, 2048:4096],
        _sq[:].bitcast(BF16)[:, 0:2048], _sq[:].bitcast(BF16)[:, 2048:4096],
        env["ln_tln"][0][:], env["ln_tln"][1][:],
    ]
    s_yst = [sb(f"s_yst{i}", [128, 512], F32) for i in range(2)]
    ldp = p.sem("ldp")
    ldsc = p.sem("ldsc")
    dvd = p.sem("dvd")
    pet = p.sem("pet")
    dve = p.sem("dve")
    stw = p.sem("stw")
    pem = p.sem("pem")
    dvg = p.sem("dvg")
    ach = p.sem("ach")
    dvh = p.sem("dvh")
    ldd = p.sem("ldd")
    pdn = p.sem("pdn")
    dvo = p.sem("dvo")
    sta = p.sem("sta")
    p.barrier()

    for f in range(16):
        for cc in range(8):
            fb = (f * 8 + cc) % 2
            p.dma("sync", s_pk[fb][:], dWp[128 * f:128 * (f + 1), 1024 * cc:1024 * (cc + 1)],
                  waits=[(dvd, p.val(dvd) - 1)], inc="ldp")
            p.dma("sync", s_sc[fb][:], dsc[128 * f:128 * (f + 1), 64 * cc:64 * (cc + 1)],
                  waits=[(dvd, p.val(dvd) - 1)], inc="ldp")
            ldv = p.val(ldp)
            scb = s_sc[fb][:].unsqueeze(2).broadcast_to([128, 64, 16])

            def wv_mk(par, fb=fb):
                return s_W[fb][:, par::2].rearrange("p (g k) -> p g k", g=64), 64
            emit_dec_nvfp4(p, env, s_pk[fb][:].bitcast(U8), 1024, s_W[fb], scb, wv_mk,
                           [(ldp, ldv), (pet, p.val(pet))], "dvd", dec)
            dcv = p.val(dvd)
            for g in range(4):
                for k in range(4):
                    c = 4 * g + k
                    p.emit("tensor", lambda e, fb=fb, c=c, k=k: e.transpose(
                        pb[0].bitcast(BF16)[:, 128 * k:128 * (k + 1)],
                        s_W[fb][:, 128 * c:128 * (c + 1)], env["ident"][:]),
                        waits=([(dvd, dcv), (dve, p.val(dve))] if k == 0 else []),
                        inc=("pet" if k == 3 else None))
                sb_i = g % 2
                p.emit("vector", lambda e, sb_i=sb_i: e.tensor_copy(
                    s_st[sb_i][:], pb[0].bitcast(BF16)[:, 0:512]),
                    waits=[(pet, p.val(pet)), (stw, p.val(stw) - 16)], inc="dve")
                dst = wdT[2048 * cc + 512 * g:2048 * cc + 512 * (g + 1),
                          128 * f:128 * (f + 1)].rearrange("(t p) q -> p t q", p=128)
                srcv = s_st[sb_i][:].rearrange("p (t q) -> p t q", t=4)
                p.dma("scalar", dst, srcv, waits=[(dve, p.val(dve))], inc="stw")
    stw_total = p.val(stw)
    p.barrier()

    for g in range(16):
        for ci in range(8):
            c = 8 * g + ci
            for half, (Wp, sc) in enumerate(((gWp, gs), (uWp, us))):
                hb2 = half
                p.dma("sync", s_pk[hb2][:, 0:1024], Wp[128 * c:128 * (c + 1), :],
                      waits=[(dvd, p.val(dvd) - 1)], inc="ldp")
                p.dma("sync", s_sc[hb2][:, 0:64], sc[128 * c:128 * (c + 1), :],
                      waits=[(dvd, p.val(dvd) - 1)], inc="ldp")
                ldv = p.val(ldp)
                scb = s_sc[hb2][:, 0:64].unsqueeze(2).broadcast_to([128, 64, 16])

                def wv_mk(par, hb2=hb2):
                    return s_W[hb2][:, par::2][:, 0:1024].rearrange(
                        "p (g k) -> p g k", g=64), 64
                emit_dec_nvfp4(p, env, s_pk[hb2][:, 0:1024].bitcast(U8), 1024,
                               s_W[hb2], scb, wv_mk,
                               [(ldp, ldv), (pet, p.val(pet))], "dvd", dec)
                dcv = p.val(dvd)
                for grp in range(4):
                    for k in range(4):
                        ic = 4 * grp + k
                        p.emit("tensor", lambda e, hb2=hb2, ic=ic, k=k: e.transpose(
                            pb[0].bitcast(BF16)[:, 128 * k:128 * (k + 1)],
                            s_W[hb2][:, 128 * ic:128 * (ic + 1)], env["ident"][:]),
                            waits=([(dvd, dcv), (dve, p.val(dve))] if k == 0 else []),
                            inc=("pet" if k == 3 else None))
                    p.emit("vector", lambda e, hb2=hb2, grp=grp: e.tensor_copy(
                        s_wT[hb2][:].rearrange("p (ic q) -> p ic q", ic=16)[:, 4 * grp:4 * grp + 4, :],
                        pb[0].bitcast(BF16)[:, 0:512].rearrange("p (t q) -> p t q", t=4)),
                        waits=[(pet, p.val(pet)), (pem, p.val(pem) - 4)], inc="dve")
                devv = p.val(dve)
                bank = pb[4 + half]
                for tg in range(4):
                    cons = ("dvg", p.val("dvg")) if half == 0 else (dvh, p.val(dvh))
                    for ic in range(16):
                        p.emit("tensor", lambda e, hb2=hb2, ic=ic, tg=tg, bank=bank: e.matmul(
                            bank[:], s_wT[hb2][:, 128 * ic:128 * (ic + 1)],
                            xlt[:, 2048 * ic + 512 * tg:2048 * ic + 512 * (tg + 1)],
                            start=(ic == 0), stop=(ic == 15)),
                            waits=([(dve, devv), cons] if ic == 0 else []),
                            inc=("pem" if ic == 15 else None))
                    if half == 0:
                        p.emit("vector", lambda e, tg=tg: e.tensor_copy(
                            s_gg[:, 512 * tg:512 * (tg + 1)], pb[4][:]),
                            waits=[(pem, p.val(pem)), (dvh, p.val(dvh) - 3 + tg)],
                            inc="dvg")
                        p.emit("scalar", lambda e, tg=tg: e.activation(
                            s_gg[:, 512 * tg:512 * (tg + 1)],
                            s_gg[:, 512 * tg:512 * (tg + 1)], AF.Gelu_apprx_tanh),
                            waits=[("dvg", p.val("dvg"))], inc="ach")
                    else:
                        p.emit("vector", lambda e, ci=ci, tg=tg: e.tensor_mul(
                            s_h[ci][:, 512 * tg:512 * (tg + 1)],
                            s_gg[:, 512 * tg:512 * (tg + 1)], pb[5][:]),
                            waits=[(pem, p.val(pem)), (ach, p.val(ach)),
                                   (pdn, p.val(pdn))],
                            inc="dvh")
        h_done = p.val(dvh)
        for ci in range(8):
            c = 8 * g + ci
            p.dma("gpsimd", s_wd[ci], wdT[128 * c:128 * (c + 1), :],
                  waits=[(stw, stw_total), (pdn, p.val(pdn))], inc="ldd")
        lddv = p.val(ldd)
        for tc in range(16):
            for fg in range(4):
                idx = tc * 4 + fg
                bank = pb[6 + idx % 2]
                for ci in range(8):
                    p.emit("tensor", lambda e, ci=ci, tc=tc, fg=fg, bank=bank: e.matmul(
                        bank[:], s_h[ci][:, 128 * tc:128 * (tc + 1)],
                        s_wd[ci][:, 512 * fg:512 * (fg + 1)],
                        start=(ci == 0), stop=(ci == 7)),
                        waits=([(ldd, lddv), (dvh, h_done), (dvo, p.val(dvo) - 1)]
                               if ci == 0 else []),
                        inc=("pdn" if ci == 7 else None))
                yb = idx % 2
                p.emit("vector", lambda e, yb=yb, bank=bank: e.tensor_copy(
                    s_yst[yb][:], bank[:]),
                    waits=[(pdn, p.val(pdn)), (sta, p.val(sta) - 16)], inc="dvo")
                p.dma("gpsimd", out[128 * tc:128 * (tc + 1), 512 * fg:512 * (fg + 1)],
                      s_yst[yb][:],
                      waits=[(dvo, p.val(dvo)), (sta, sta_base + 16 * 64 * g),
                             ("sti", sti_v)], inc="sta",
                      accum_op=ALU.add)

    return [("sta", p.val("sta"))]


EI, EO, IN = "ExternalInput", "ExternalOutput", "Internal"

DECLS_K1 = [("xin", [T, D], F32, EI), ("wqkv", [D, 3 * D], F32, EI),
            ("wout", [D, D], F32, EI)]
DECLS_K2 = [("gup", [11264, 256], I32, EI), ("gus", [16, 11264], F16, EI),
            ("dp", [D, 704], I32, EI), ("ds", [44, D], F16, EI),
            ("wdT0", [5632, D], BF16, IN), ("yd", [T, D], F32, IN)]
DECLS_K3A = [("wq", [D, D], F32, EI), ("wk", [D, D], F32, EI), ("wv", [D, D], F32, EI),
             ("bq", [D], F32, EI), ("bk", [D], F32, EI), ("bv", [D], F32, EI),
             ("pos", [1, T], F32, EI), ("qT", [D, T], BF16, EO),
             ("kT", [D, T], BF16, EO), ("vv", [T, D], BF16, EO)]
DECLS_K3B = [("qT2", [D, T], BF16, EI), ("kg", [4 * D, T], BF16, EI),
             ("vg", [4 * T, D], BF16, EI), ("wo", [D, D], F32, EI), ("bo", [D], F32, EI)]
DECLS_K5 = [("gWp", [16384, 1024], I32, EI), ("gs", [16384, 64], F32, EI),
            ("uWp", [16384, 1024], I32, EI), ("us", [16384, 64], F32, EI),
            ("dWp", [D, 8192], I32, EI), ("dsc", [D, 512], F32, EI),
            ("wdT1", [16384, D], BF16, IN), ("out", [T, D], F32, EO)]


def build_launchA():
    decls = (DECLS_K1 + [("x1", [T, D], F32, IN)] + DECLS_K2 +
             [("x2", [T, D], F32, EO)] + DECLS_K3A)
    return build_launch([emit_k1, emit_k2, emit_k3a], decls)


def build_launchB():
    decls = (DECLS_K3B + [("x2b", [T, D], F32, EI), ("x3", [T, D], F32, IN)] + DECLS_K5)
    return build_launch([emit_k3b, emit_k5], decls, aliases={"x3b": "x3"})


def build_k1():
    return build_launch([emit_k1], DECLS_K1 + [("x1", [T, D], F32, EO)])


def build_k2():
    return build_launch([emit_k2], [("x1", [T, D], F32, EI)] + DECLS_K2 +
                        [("x2", [T, D], F32, EO)])


def build_k3a():
    return build_launch([emit_k3a], [("x2", [T, D], F32, EI)] + DECLS_K3A)


def build_k3b():
    return build_launch([emit_k3b], DECLS_K3B +
                        [("x2b", [T, D], F32, EI), ("x3", [T, D], F32, EO)])


def build_k5():
    return build_launch([emit_k5], [("x3b", [T, D], F32, EI)] + DECLS_K5)


def _shard_x(x):
    out = []
    for c in range(NCORES):
        b, s = c // 4, 2048 * (c % 4)
        out.append(np.ascontiguousarray(x[b, s:s + 2048, :], dtype=np.float32))
    return out


def kernel(**inputs):
    f32 = lambda k: np.ascontiguousarray(np.asarray(inputs[k], np.float32))
    i32 = lambda k: np.ascontiguousarray(np.asarray(inputs[k], np.int32))
    f16 = lambda k: np.ascontiguousarray(np.asarray(inputs[k], np.float16))

    xs = _shard_x(f32("L0_attn_x"))
    cores = list(range(NCORES))

    ncA = build_launchA()
    in_a = []
    for c in cores:
        s = 2048 * (c % 4)
        in_a.append({
            "xin": xs[c],
            "wqkv": f32("L0_attn_w_qkv"), "wout": f32("L0_attn_w_out"),
            "gup": i32("L0_ffn_gate_up_packed"), "gus": f16("L0_ffn_gate_up_scales"),
            "dp": i32("L0_ffn_down_packed"), "ds": f16("L0_ffn_down_scales"),
            "wq": f32("L1_attn_Wq"), "wk": f32("L1_attn_Wk"), "wv": f32("L1_attn_Wv"),
            "bq": f32("L1_attn_bq"), "bk": f32("L1_attn_bk"), "bv": f32("L1_attn_bv"),
            "pos": np.arange(s, s + 2048, dtype=np.float32).reshape(1, 2048),
        })
    rA = run_bass_kernel_spmd(ncA, in_a, core_ids=cores)
    x2 = [np.asarray(rA.results[c]["x2"]) for c in cores]
    qT = [np.asarray(rA.results[c]["qT"]) for c in cores]
    kT = [np.asarray(rA.results[c]["kT"]) for c in cores]
    vv = [np.asarray(rA.results[c]["vv"]) for c in cores]

    ncB = build_launchB()
    in_b = []
    for c in cores:
        b = c // 4
        grp = [4 * b + i for i in range(4)]
        in_b.append({
            "qT2": qT[c],
            "kg": np.ascontiguousarray(np.concatenate([kT[r] for r in grp], axis=0)),
            "vg": np.ascontiguousarray(np.concatenate([vv[r] for r in grp], axis=0)),
            "wo": f32("L1_attn_Wo"), "bo": f32("L1_attn_bo"), "x2b": x2[c],
            "gWp": i32("L1_ffn_gate_W_packed"), "gs": f32("L1_ffn_gate_scales"),
            "uWp": i32("L1_ffn_up_W_packed"), "us": f32("L1_ffn_up_scales"),
            "dWp": i32("L1_ffn_down_W_packed"), "dsc": f32("L1_ffn_down_scales"),
        })
    rB = run_bass_kernel_spmd(ncB, in_b, core_ids=cores)

    out = np.empty((2, 8192, 2048), np.float32)
    for c in cores:
        b, s = c // 4, 2048 * (c % 4)
        out[b, s:s + 2048, :] = rB.results[c]["out"]
    return out


# version 05
# speedup vs baseline: 1.2740x; 1.2740x over previous
import numpy as np
from contextlib import ExitStack

import concourse.bass as bass
import concourse.mybir as mybir
from concourse.bass_utils import run_bass_kernel_spmd

F32 = mybir.dt.float32
BF16 = mybir.dt.bfloat16
F16 = mybir.dt.float16
I32 = mybir.dt.int32
U16 = mybir.dt.uint16
U8 = mybir.dt.uint8
AF = mybir.ActivationFunctionType
ALU = mybir.AluOpType
AX = mybir.AxisListType

ENGINES = ("sync", "vector", "scalar", "tensor", "gpsimd")
T = 2048
D = 2048
NCORES = 8
INV_SQRT_HD = float(1.0 / np.sqrt(128.0))


class P:

    def __init__(self):
        self.steps = {e: [] for e in ENGINES}
        self.sems = {}
        self.order = []

    def sem(self, name):
        if name not in self.sems:
            self.sems[name] = 0
            self.order.append(name)
        return name

    def val(self, name):
        self.sem(name)
        return self.sems[name]

    def emit(self, eng, fn, waits=(), inc=None, inc_n=1):
        waits = tuple((s, v) for (s, v) in waits if v > 0)
        if inc is not None:
            self.sem(inc)
            self.sems[inc] += inc_n
        self.steps[eng].append((fn, waits, inc, inc_n))

    def dma(self, eng, out, in_, waits=(), inc=None, **kw):
        self.emit(eng, lambda e, out=out, in_=in_, kw=kw: e.dma_start(out=out, in_=in_, **kw),
                  waits=waits, inc=inc, inc_n=16)

    def barrier(self, engines=ENGINES):
        bar = self.sem("bar")
        for e in engines:
            self.emit(e, lambda eng: eng.nop(), inc="bar")
        tgt = self.sems["bar"]
        for e in engines:
            self.steps[e].append((None, ((bar, tgt),), None, 0))

    def finalize(self, nc, ctx, final_wait=None):
        handles = {n: ctx.enter_context(nc.semaphore(n)) for n in self.order}
        with nc.Block() as block:
            for eng_name in ENGINES:
                if not self.steps[eng_name] and eng_name != "sync":
                    continue

                def mk(eng_name=eng_name):
                    def body(eng):
                        for fn, waits, inc, inc_n in self.steps[eng_name]:
                            for s, v in waits:
                                eng.wait_ge(handles[s], v)
                            if fn is None:
                                continue
                            inst = fn(eng)
                            if inc:
                                inst.then_inc(handles[inc], inc_n)
                        if eng_name == "sync" and final_wait is not None:
                            eng.wait_ge(handles[final_wait[0]], final_wait[1])
                    return body
                getattr(block, eng_name)(mk())



SBUF_TOP = 229344


def make_carver(nc, base):
    state = {"off": base}

    def carve(name, shape, dtype):
        nbytes = int(np.prod(shape[1:])) * mybir.dt.size(dtype)
        off = (state["off"] + 31) // 32 * 32
        t = nc.alloc_sbuf_tensor_at(name, shape, dtype, offset=off)
        state["off"] = off + nbytes
        assert state["off"] <= SBUF_TOP, f"carve overflow at {name}: {state['off']}"
        return t

    def reset():
        state["off"] = base
    return carve, reset


def build_launch(phase_emitters, dram_decls, aliases=None):
    nc = bass.Bass("TRN2", target_bir_lowering=False, debug=False)
    io = {}
    for name, shape, dtype, kind in dram_decls:
        io[name] = nc.dram_tensor(name, shape, dtype, kind=kind).ap()
    if aliases:
        for k, v in aliases.items():
            io[k] = io[v]
    ctx = ExitStack()
    env = {"nc": nc, "io": io}
    pb = [nc.alloc_psum_tensor(f"pb{i}", [128, 512], F32) for i in range(8)]
    env["pb"] = pb

    def sb_fixed(name, shape, dtype):
        return nc.alloc_sbuf_tensor(name, shape, dtype)
    env["sb"] = sb_fixed
    p = P()
    p.sem("init")
    common_small(p, env)
    common_identf(p, env)
    arena_base = SBUF_TOP - nc.sbuf_bytes_remaining
    carve, reset = make_carver(nc, arena_base)
    p.barrier()
    for i, emit in enumerate(phase_emitters):
        reset()
        pref = f"ph{i}_"
        env["sb"] = (lambda pref=pref: lambda name, shape, dtype:
                     carve(pref + name, shape, dtype))()
        waits = emit(p, env)
        for s, v in waits:
            p.emit("sync", lambda e: e.nop(), waits=[(s, v)])
        p.barrier()
    p.finalize(nc, ctx, final_wait=None)
    env["ctx"] = ctx
    return nc


def common_small(p, env):
    nc = env["nc"]
    sb = env["sb"]
    ident = sb("c_ident", [128, 128], BF16)
    m_incl = sb("c_mincl", [128, 128], BF16)
    m_strict = sb("c_mstr", [128, 128], BF16)
    ones_col = sb("c_ones", [128, 1], BF16)
    ones_row = sb("c_onesr", [1, 128], F32)
    env["ident"], env["m_incl"], env["m_strict"] = ident, m_incl, m_strict
    env["ones_col"], env["ones_row"] = ones_col, ones_row

    def ch(fn):
        p.emit("gpsimd", fn, waits=[("init", p.val("init"))], inc="init")
    ch(lambda e: e.memset(ident[:], 0.0))
    ch(lambda e: e.affine_select(
        out=ident[:], in_=ident[:], compare_op=ALU.not_equal, fill=1.0,
        base=0, pattern=[[-1, 128]], channel_multiplier=1))
    ch(lambda e: e.memset(m_incl[:], 1.0))
    ch(lambda e: e.affine_select(
        out=m_incl[:], in_=m_incl[:], compare_op=ALU.is_ge, fill=0.0,
        base=0, pattern=[[1, 128]], channel_multiplier=-1))
    ch(lambda e: e.memset(m_strict[:], 1.0))
    ch(lambda e: e.affine_select(
        out=m_strict[:], in_=m_strict[:], compare_op=ALU.is_gt, fill=0.0,
        base=0, pattern=[[1, 128]], channel_multiplier=-1))
    ch(lambda e: e.memset(ones_col[:], 1.0))
    ch(lambda e: e.memset(ones_row[:], 1.0))


def emit_ln(p, env, xin, xlt, pref):
    nc = env["nc"]
    sb = env["sb"]
    s_xc = [sb(f"{pref}_xc{i}", [128, T], F32) for i in range(2)]
    s_sq = sb(f"{pref}_sq", [128, T], F32)
    s_tln = [sb(f"{pref}_tln{i}", [128, T], BF16) for i in range(2)]
    sm = {k: sb(f"{pref}_{k}", [128, 1], F32)
          for k in ("ssq", "ssum", "nm", "t2e", "mm", "ve", "lnv", "rstd", "negmr")}
    env[pref + "_xc"] = s_xc
    env[pref + "_sq"] = s_sq
    env[pref + "_tln"] = s_tln
    ch = p.sem(pref + "ch")
    app = p.sem(pref + "app")
    ptr = p.sem(pref + "ptr")
    dev = p.sem(pref + "dev")
    ident = env["ident"]
    pT = env["pb"][0].bitcast(BF16)

    def chain(eng, fn, extra=()):
        p.emit(eng, fn, waits=[(ch, p.val(ch))] + list(extra), inc=pref + "ch")

    for ci in range(16):
        b = ci % 2
        p.dma("sync", s_xc[b][:], xin[128 * ci:128 * (ci + 1), :],
              waits=[(ch, p.val(ch) - 10)], inc=pref + f"ld{b}")
        ldv = p.val(pref + f"ld{b}")
        chain("vector", lambda e, b=b: e.reduce_sum(sm["ssum"][:], s_xc[b][:], axis=AX.X),
              extra=[(pref + f"ld{b}", ldv)])
        chain("scalar", lambda e, b=b: e.activation(s_sq[:], s_xc[b][:], AF.Square,
                                                    accum_out=sm["ssq"][:]))
        chain("vector", lambda e: e.tensor_scalar(sm["nm"][:], sm["ssum"][:], -1.0 / D, None, ALU.mult))
        chain("vector", lambda e: e.tensor_scalar(sm["t2e"][:], sm["ssq"][:], 1.0 / D, 1e-5, ALU.mult, ALU.add))
        chain("vector", lambda e: e.tensor_mul(sm["mm"][:], sm["nm"][:], sm["nm"][:]))
        chain("vector", lambda e: e.tensor_sub(sm["ve"][:], sm["t2e"][:], sm["mm"][:]))
        chain("scalar", lambda e: e.activation(sm["lnv"][:], sm["ve"][:], AF.Ln))
        chain("scalar", lambda e: e.activation(sm["rstd"][:], sm["lnv"][:], AF.Exp, scale=-0.5))
        chain("vector", lambda e: e.tensor_mul(sm["negmr"][:], sm["nm"][:], sm["rstd"][:]))
        chain("scalar", lambda e, b=b: e.activation(s_tln[b][:], s_xc[b][:], AF.Identity,
                                                    bias=sm["negmr"][:], scale=sm["rstd"][:]),
              extra=[(dev, p.val(dev) - 4)])
        for g in range(4):
            for k in range(4):
                fj = 4 * g + k
                p.emit("tensor", lambda e, b=b, fj=fj, k=k: e.transpose(
                    pT[:, 128 * k:128 * (k + 1)], s_tln[b][:, 128 * fj:128 * (fj + 1)], ident[:]),
                    waits=[(ch, p.val(ch)), (dev, p.val(dev))] if k == 0 else [],
                    inc=pref + "ptr" if k == 3 else None)
            dst = xlt[:].rearrange("p (f t) -> p f t", f=16)[:, 4 * g:4 * g + 4,
                                                            128 * ci:128 * (ci + 1)]
            src = pT[:, 0:512].rearrange("p (f t) -> p f t", f=4)
            p.emit("vector", lambda e, dst=dst, src=src: e.tensor_copy(dst, src),
                   waits=[(ptr, p.val(ptr))], inc=pref + "dev")


def emit_k1(p, env):
    nc = env["nc"]; sb = env["sb"]; pb = env["pb"]; io = env["io"]
    xin, wqkv, wout, x1 = io["xin"], io["wqkv"], io["wout"], io["x1"]
    xlt = sb("xlt", [128, 16 * T], BF16)
    emit_ln(p, env, xin, xlt, "ln")
    p.emit("tensor", lambda e: e.nop(), waits=[("lndev", p.val("lndev"))])
    p.barrier()

    s_w = [sb(f"wslab{i}", [128, 16 * 512], BF16) for i in range(2)]
    s_q = sb("s_q", [128, 16 * 512], BF16)
    s_k = sb("s_k", [128, 16 * 512], BF16)
    s_v = sb("s_v", [128, 4 * D], BF16)
    s_o = sb("s_o", [128, 16 * 512], BF16)
    s_pt = [sb(f"s_pt{i}", [128, 512], BF16) for i in range(4)]
    s_rs = sb("s_rs", [1, 512], F32)
    s_rsb = sb("s_rsb", [128, 512], F32)
    s_xcs = [env["ln_tln"][0][:].bitcast(F32)[:, 512 * i:512 * (i + 1)] for i in range(2)]
    s_ys = [env["ln_tln"][1][:].bitcast(F32)[:, 512 * i:512 * (i + 1)] for i in range(2)]

    pes = p.sem("pes")
    ace = p.sem("ace")
    dvm = p.sem("dvm")
    pev = p.sem("pev")
    aev = p.sem("aev")
    dvn = p.sem("dvn")
    dvy = p.sem("dvy")

    for hb in range(4):
        th0 = 512 * hb
        aev0 = p.val(aev)
        for sg in range(12):
            wb = sg % 2
            p.dma("gpsimd", s_w[wb][:],
                  wqkv[:, 512 * sg:512 * (sg + 1)].rearrange("(ic p) o -> p ic o", p=128),
                  waits=[(aev, aev0 + 4 * (sg - 1))] if sg >= 2 else [], inc=f"ldw{wb}")
            ldv = p.val(f"ldw{wb}")
            for u in range(4):
                bank = pb[4 + u]
                for ic in range(16):
                    if sg < 8:
                        fn = (lambda e, wb=wb, ic=ic, u=u, th0=th0, bank=bank: e.matmul(
                            bank[:], s_w[wb][:, 512 * ic + 128 * u:512 * ic + 128 * (u + 1)],
                            xlt[:, 2048 * ic + th0:2048 * ic + th0 + 512],
                            start=(ic == 0), stop=(ic == 15)))
                    else:
                        fn = (lambda e, wb=wb, ic=ic, u=u, th0=th0, bank=bank: e.matmul(
                            bank[:], xlt[:, 2048 * ic + th0 + 128 * u:2048 * ic + th0 + 128 * (u + 1)],
                            s_w[wb][:, 512 * ic:512 * (ic + 1)],
                            start=(ic == 0), stop=(ic == 15)))
                    p.emit("tensor", fn,
                           waits=([(f"ldw{wb}", ldv),
                                   (aev, aev0 + 4 * (sg - 1) + u + 1 if sg >= 1 else 0)]
                                  if ic == 0 else []),
                           inc=("pes" if ic == 15 else None))
                if sg < 8:
                    dst, off = (s_q, (sg % 4) * 4 + u) if sg < 4 else (s_k, ((sg - 4) % 4) * 4 + u)
                    dap = dst[:, 512 * off:512 * (off + 1)]
                else:
                    dap = s_v[:, 2048 * u + 512 * (sg - 8):2048 * u + 512 * (sg - 8) + 512]
                p.emit("scalar", lambda e, dap=dap, bank=bank: e.activation(dap, bank[:], AF.Copy),
                       waits=[(pes, p.val(pes))], inc="aev")
        p.barrier()

        sB = [pb[0], pb[1], pb[5], pb[6]]
        LAG = 3
        for j in range(16):
            mask_of = {}
            pv_of = {}

            def emit_pv(c, j=j, mask_of=mask_of, pv_of=pv_of):
                Tp, jp = c // 16, c % 16
                col0 = 128 * Tp
                ptb = s_pt[c % 4]
                p.emit("tensor", lambda e, c=c, Tp=Tp, jp=jp, ptb=ptb, col0=col0: e.matmul(
                    pb[2][:, col0:512],
                    s_v[:, 2048 * Tp + 128 * jp:2048 * Tp + 128 * (jp + 1)],
                    ptb[:, col0:512], start=(c == 0), stop=(c == 63),
                    skip_group_check=True),
                    waits=[(dvm, mask_of[c])] + ([(dvn, p.val(dvn))] if c == 0 else []),
                    inc="pev")
                p.emit("tensor", lambda e, c=c, ptb=ptb, col0=col0: e.matmul(
                    pb[3][0:1, col0:512], env["ones_col"][:], ptb[:, col0:512],
                    start=(c == 0), stop=(c == 63), skip_group_check=True),
                    inc="pev")
                pv_of[c] = p.val(pev)

            for c in range(64):
                Tp, jp = c // 16, c % 16
                col0 = 128 * Tp
                sbank = sB[c % 4]
                ptb = s_pt[c % 4]
                p.emit("tensor", lambda e, j=j, Tp=Tp, jp=jp, col0=col0, sbank=sbank: e.matmul(
                    sbank[:, col0:512],
                    s_k[:, 512 * jp + 128 * Tp:512 * jp + 128 * (Tp + 1)],
                    s_q[:, 512 * j + col0:512 * j + 512],
                    start=True, stop=True, skip_group_check=True),
                    waits=[(ace, p.val(ace) - 4), (dvn, p.val(dvn))] if c <= 3 else
                          [(ace, p.val(ace) - 4)],
                    inc="pes")
                qk_done = p.val(pes)
                p.emit("scalar", lambda e, ptb=ptb, col0=col0, sbank=sbank: e.activation(
                    ptb[:, col0:512], sbank[:, col0:512], AF.Exp, scale=INV_SQRT_HD),
                    waits=[(pes, qk_done), (pev, pv_of.get(c - 4, 0))], inc="ace")
                mask = env["m_incl"] if jp <= j else env["m_strict"]
                p.emit("vector", lambda e, ptb=ptb, col0=col0, mask=mask: e.tensor_mul(
                    ptb[:, col0:col0 + 128], ptb[:, col0:col0 + 128], mask[:]),
                    waits=[(ace, p.val(ace))], inc="dvm")
                mask_of[c] = p.val(dvm)
                if c >= LAG:
                    emit_pv(c - LAG)
            for c in range(64 - LAG, 64):
                emit_pv(c)
            p.emit("vector", lambda e: e.reciprocal(s_rs[:], pb[3][0:1, :]),
                   waits=[(pev, p.val(pev))], inc="dvn")
            p.emit("tensor", lambda e: e.matmul(pb[4][:], env["ones_row"][:], s_rs[:],
                                                start=True, stop=True),
                   waits=[(dvn, p.val(dvn)), (aev, p.val(aev))], inc="pes")
            p.emit("scalar", lambda e: e.activation(s_rsb[:], pb[4][:], AF.Copy),
                   waits=[(pes, p.val(pes))], inc="aev")
            p.emit("vector", lambda e, j=j: e.tensor_mul(
                s_o[:, 512 * j:512 * (j + 1)], pb[2][:], s_rsb[:]),
                   waits=[(aev, p.val(aev))], inc="dvn")
        p.barrier()

        dvy0 = p.val(dvy)
        for og in range(4):
            wb = og % 2
            p.dma("gpsimd", s_w[wb][:],
                  wout[:, 512 * og:512 * (og + 1)].rearrange("(ic p) o -> p ic o", p=128),
                  waits=[(dvy, dvy0 + 4 * (og - 1))] if og >= 2 else [], inc=f"ldw{wb}")
            ldv = p.val(f"ldw{wb}")
            for tc in range(4):
                i = (og * 4 + tc) % 2
                bank = pb[4 + tc % 4]
                p.dma("sync", s_xcs[i],
                      xin[th0 + 128 * tc:th0 + 128 * (tc + 1), 512 * og:512 * (og + 1)],
                      waits=[(dvy, p.val(dvy) - 1)], inc=f"ldx{i}")
                ldxv = p.val(f"ldx{i}")
                for ec in range(16):
                    p.emit("tensor", lambda e, wb=wb, ec=ec, tc=tc, bank=bank: e.matmul(
                        bank[:], s_o[:, 512 * ec + 128 * tc:512 * ec + 128 * (tc + 1)],
                        s_w[wb][:, 512 * ec:512 * (ec + 1)],
                        start=(ec == 0), stop=(ec == 15)),
                        waits=([(f"ldw{wb}", ldv), (dvy, p.val(dvy))] if ec == 0 else []),
                        inc=("pes" if ec == 15 else None))
                p.emit("vector", lambda e, i=i, bank=bank: e.scalar_tensor_tensor(
                    s_ys[i], bank[:], 1.0, s_xcs[i], ALU.mult, ALU.add),
                    waits=[(pes, p.val(pes)), (f"ldx{i}", ldxv),
                           (f"sty{i}", p.val(f"sty{i}"))],
                    inc="dvy")
                p.dma("sync", x1[th0 + 128 * tc:th0 + 128 * (tc + 1), 512 * og:512 * (og + 1)],
                      s_ys[i], waits=[(dvy, p.val(dvy))], inc=f"sty{i}")
        p.barrier()

    return [("sty0", p.val("sty0")), ("sty1", p.val("sty1"))]


def emit_dec_fp4(p, env, u16v, nC2, s_W, scT_bcast, wv_maker, waits_first, inc_last,
                 dec):
    k_, sh_, b_, t_, vp_, v_ = (dec[n] for n in ("k", "sh", "b", "t", "vp", "v"))
    for jj in range(4):
        w0 = waits_first if jj == 0 else ()
        if jj == 0:
            p.emit("vector", lambda e: e.tensor_scalar(
                k_[:, :nC2], u16v, 7, None, ALU.bitwise_and), waits=w0)
        else:
            p.emit("vector", lambda e, jj=jj: e.tensor_scalar(
                k_[:, :nC2], u16v, 4 * jj, 7, ALU.logical_shift_right, ALU.bitwise_and))
        if jj == 3:
            p.emit("vector", lambda e: e.tensor_scalar(
                sh_[:, :nC2], u16v, 0x8000, None, ALU.bitwise_and))
        else:
            p.emit("vector", lambda e, jj=jj: e.tensor_scalar(
                sh_[:, :nC2], u16v, 12 - 4 * jj, 0x8000,
                ALU.logical_shift_left, ALU.bitwise_and))
        p.emit("vector", lambda e: e.tensor_scalar(
            b_[:, :nC2], k_[:, :nC2], 64.0, 16128.0, ALU.mult, ALU.add))
        p.emit("vector", lambda e: e.tensor_scalar(
            t_[:, :nC2], b_[:, :nC2].bitcast(BF16), 2.0, -1.0, ALU.mult, ALU.add))
        p.emit("vector", lambda e: e.tensor_tensor(
            vp_[:, :nC2], t_[:, :nC2], b_[:, :nC2].bitcast(BF16), ALU.min))
        p.emit("vector", lambda e: e.tensor_tensor(
            v_[:, :nC2].bitcast(U16), vp_[:, :nC2].bitcast(U16), sh_[:, :nC2],
            ALU.bitwise_xor))
        wv, ng = wv_maker(jj)
        vv = v_[:, :nC2].rearrange("p (g k) -> p g k", g=ng)
        p.emit("vector", lambda e, wv=wv, vv=vv: e.tensor_tensor(wv, vv, scT_bcast, ALU.mult),
               inc=(inc_last if jj == 3 else None))


def emit_scale_T(p, env, dram_slice, s_h, s_b16, pbank, s_out, nrows, waits, inc):
    ident = env["ident"]
    p.dma("sync", s_h[0:nrows, :], dram_slice,
          waits=list(waits) + [("pesc", p.val("pesc"))], inc="ldsc")
    v = p.val("ldsc")
    p.emit("vector", lambda e: e.tensor_copy(s_b16[0:nrows, :], s_h[0:nrows, :]),
           waits=[("ldsc", v)], inc="dvsc")
    p.emit("tensor", lambda e: e.transpose(pbank.bitcast(BF16)[:, 0:nrows],
                                           s_b16[0:nrows, :], ident[0:nrows, 0:nrows]),
           waits=[("dvsc", p.val("dvsc")), (inc, p.val(inc))], inc="pesc")
    p.emit("vector", lambda e: e.tensor_copy(s_out[:, 0:nrows], pbank.bitcast(BF16)[:, 0:nrows]),
           waits=[("pesc", p.val("pesc"))], inc=inc)


def emit_k2(p, env):
    nc = env["nc"]; sb = env["sb"]; pb = env["pb"]; io = env["io"]
    x1, gup, gus, dp, ds, x2, wdT0, yd = (io["x1"], io["gup"], io["gus"], io["dp"],
                                          io["ds"], io["x2"], io["wdT0"], io["yd"])
    wdT = wdT0
    for s in ("ldsc", "dvsc", "pesc"):
        p.sem(s)
    xlt = sb("xlt", [128, 16 * T], BF16)
    emit_ln(p, env, x1, xlt, "ln")
    p.emit("tensor", lambda e: e.nop(), waits=[("lndev", p.val("lndev"))])

    dec = {n: sb(f"dec_{n}", [128, 1408], U16 if n in ("k", "sh", "b") else BF16)
           for n in ("k", "sh", "b", "t", "vp", "v")}
    s_pk = [sb(f"s_pk{i}", [128, 704], I32) for i in range(2)]
    s_sch = sb("s_sch", [44, 128], F16)
    s_scb = sb("s_scb", [44, 128], BF16)
    s_scT = [sb(f"s_scT{i}", [128, 44], F32) for i in range(2)]
    s_W = [sb(f"s_W{i}", [128, 5632], BF16) for i in range(2)]
    s_wT = [sb(f"s_wT{i}", [128, 2048], BF16) for i in range(2)]
    s_st = [sb(f"s_st{i}", [128, 512], BF16) for i in range(2)]
    s_h = [sb(f"s_h{i}", [128, 2048], BF16) for i in range(4)]
    s_sg = sb("s_sg", [128, 2048], F32)
    s_wd = [sb(f"s_wd{i}", [128, 2048], BF16) for i in range(4)]
    s_yst = [sb(f"s_yst{i}", [128, 512], F32) for i in range(2)]
    dvd = p.sem("dvd")
    pet = p.sem("pet")
    dve = p.sem("dve")
    pem = p.sem("pem")
    ach = p.sem("ach")
    dvh = p.sem("dvh")
    ldd = p.sem("ldd")
    pdn = p.sem("pdn")
    dvo = p.sem("dvo")
    p.barrier()

    for f in range(16):
        fb = f % 2
        p.dma("sync", s_pk[fb][:], dp[128 * f:128 * (f + 1), :],
              waits=[(dvd, p.val(dvd) - 1)], inc=f"ldp{fb}")
        ldv = p.val(f"ldp{fb}")
        emit_scale_T(p, env, ds[:, 128 * f:128 * (f + 1)], s_sch, s_scb, pb[1],
                     s_scT[fb], 44, [(dvd, p.val(dvd) - 1)], "dvsc2" if False else "dvscT")
        scv = p.val("dvscT")
        scb = s_scT[fb][:, 0:44].unsqueeze(2).broadcast_to([128, 44, 32])

        def wv_mk(jj, fb=fb):
            return s_W[fb][:, jj::4].rearrange("p (g k) -> p g k", g=44), 44
        emit_dec_fp4(p, env, s_pk[fb][:].bitcast(U16), 1408, s_W[fb], scb, wv_mk,
                     [(f"ldp{fb}", ldv), ("dvscT", scv), (pet, p.val(pet))], "dvd", dec)
        dcv = p.val(dvd)
        for g in range(11):
            for k in range(4):
                c = 4 * g + k
                p.emit("tensor", lambda e, fb=fb, c=c, k=k: e.transpose(
                    pb[0].bitcast(BF16)[:, 128 * k:128 * (k + 1)],
                    s_W[fb][:, 128 * c:128 * (c + 1)], env["ident"][:]),
                    waits=([(dvd, dcv), (dve, p.val(dve))] if k == 0 else []),
                    inc=("pet" if k == 3 else None))
            sb_i = (f * 11 + g) % 2
            p.emit("vector", lambda e, sb_i=sb_i: e.tensor_copy(
                s_st[sb_i][:], pb[0].bitcast(BF16)[:, 0:512]),
                waits=[(pet, p.val(pet)), (f"stw{sb_i}", p.val(f"stw{sb_i}"))], inc="dve")
            dst = wdT[512 * g:512 * (g + 1), 128 * f:128 * (f + 1)].rearrange(
                "(t p) f -> p t f", p=128)
            src = s_st[sb_i][:].rearrange("p (t f) -> p t f", t=4)
            p.dma("scalar", dst, src, waits=[(dve, p.val(dve))], inc=f"stw{sb_i}")
    p.barrier()

    s_gsh = sb("s_gsh", [16, 128], F16)
    s_gsb = sb("s_gsb", [16, 128], BF16)
    s_gsT = [sb(f"s_gsT{i}", [128, 16], F32) for i in range(2)]
    stw_totals = [p.val("stw0"), p.val("stw1")]
    sta_grp = {}
    for g in range(11):
        for ci in range(4):
            c = 4 * g + ci
            for half, row0 in ((0, 128 * c), (1, 5632 + 128 * c)):
                hb2 = half
                p.dma("sync", s_pk[hb2][:, 0:256], gup[row0:row0 + 128, :],
                      waits=[(dvd, p.val(dvd) - 1)], inc=f"ldp{hb2}")
                ldv = p.val(f"ldp{hb2}")
                emit_scale_T(p, env, gus[:, row0:row0 + 128], s_gsh, s_gsb, pb[1],
                             s_gsT[hb2], 16, [("dvscT", p.val("dvscT") - 1)], "dvscT")
                scv = p.val("dvscT")
                scb = s_gsT[hb2][:, 0:16].unsqueeze(2).broadcast_to([128, 16, 32])

                def wv_mk(jj, hb2=hb2):
                    return s_W[hb2][:, jj::4][:, 0:512].rearrange(
                        "p (g k) -> p g k", g=16), 16
                emit_dec_fp4(p, env, s_pk[hb2][:, 0:256].bitcast(U16), 512,
                             s_W[hb2], scb, wv_mk,
                             [(f"ldp{hb2}", ldv), ("dvscT", scv), (pet, p.val(pet))], "dvd", dec)
                dcv = p.val(dvd)
                for grp in range(4):
                    for k in range(4):
                        ic = 4 * grp + k
                        p.emit("tensor", lambda e, hb2=hb2, ic=ic, k=k: e.transpose(
                            pb[0].bitcast(BF16)[:, 128 * k:128 * (k + 1)],
                            s_W[hb2][:, 128 * ic:128 * (ic + 1)], env["ident"][:]),
                            waits=([(dvd, dcv), (dve, p.val(dve))] if k == 0 else []),
                            inc=("pet" if k == 3 else None))
                    p.emit("vector", lambda e, hb2=hb2, grp=grp: e.tensor_copy(
                        s_wT[hb2][:].rearrange("p (ic q) -> p ic q", ic=16)[:, 4 * grp:4 * grp + 4, :],
                        pb[0].bitcast(BF16)[:, 0:512].rearrange("p (t q) -> p t q", t=4)),
                        waits=[(pet, p.val(pet)), (pem, p.val(pem) - 4)], inc="dve")
                devv = p.val(dve)
                bank = pb[4 + half]
                for tg in range(4):
                    cons = ("dvg", p.val("dvg")) if half == 0 else (dvh, p.val(dvh))
                    for ic in range(16):
                        p.emit("tensor", lambda e, hb2=hb2, ic=ic, tg=tg, bank=bank: e.matmul(
                            bank[:], s_wT[hb2][:, 128 * ic:128 * (ic + 1)],
                            xlt[:, 2048 * ic + 512 * tg:2048 * ic + 512 * (tg + 1)],
                            start=(ic == 0), stop=(ic == 15)),
                            waits=([(dve, devv), cons] if ic == 0 else []),
                            inc=("pem" if ic == 15 else None))
                    if half == 0:
                        p.emit("vector", lambda e, tg=tg: e.tensor_copy(
                            s_sg[:, 512 * tg:512 * (tg + 1)], pb[4][:]),
                            waits=[(pem, p.val(pem)), (dvh, p.val(dvh) - 3 + tg)],
                            inc="dvg")
                        p.emit("scalar", lambda e, tg=tg: e.activation(
                            s_sg[:, 512 * tg:512 * (tg + 1)],
                            s_sg[:, 512 * tg:512 * (tg + 1)], AF.Silu),
                            waits=[("dvg", p.val("dvg"))], inc="ach")
                    else:
                        p.emit("vector", lambda e, ci=ci, tg=tg: e.tensor_mul(
                            s_h[ci][:, 512 * tg:512 * (tg + 1)],
                            s_sg[:, 512 * tg:512 * (tg + 1)], pb[5][:]),
                            waits=[(pem, p.val(pem)), (ach, p.val(ach)),
                                   (pdn, p.val(pdn))],
                            inc="dvh")
        h_done = p.val(dvh)
        for ci in range(4):
            c = 4 * g + ci
            p.dma("gpsimd", s_wd[ci][:], wdT[128 * c:128 * (c + 1), :],
                  waits=[("stw0", stw_totals[0]), ("stw1", stw_totals[1]),
                         (pdn, p.val(pdn))], inc="ldd")
        lddv = p.val(ldd)
        for tc in range(16):
            for fg in range(4):
                idx = tc * 4 + fg
                bank = pb[6 + idx % 2]
                for ci in range(4):
                    p.emit("tensor", lambda e, ci=ci, tc=tc, fg=fg, bank=bank: e.matmul(
                        bank[:], s_h[ci][:, 128 * tc:128 * (tc + 1)],
                        s_wd[ci][:, 512 * fg:512 * (fg + 1)],
                        start=(ci == 0), stop=(ci == 3)),
                        waits=([(ldd, lddv), (dvh, h_done), (dvo, p.val(dvo) - 1)]
                               if ci == 0 else []),
                        inc=("pdn" if ci == 3 else None))
                yb = idx % 2
                p.emit("vector", lambda e, yb=yb, bank=bank: e.tensor_copy(
                    s_yst[yb][:], bank[:]),
                    waits=[(pdn, p.val(pdn)), (f"sta{yb}", p.val(f"sta{yb}"))], inc="dvo")
                gw = [(f"sta{yb}", sta_grp[g - 1][yb])] if g > 0 else []
                p.dma("gpsimd", yd[128 * tc:128 * (tc + 1), 512 * fg:512 * (fg + 1)],
                      s_yst[yb][:],
                      waits=[(dvo, p.val(dvo))] + gw, inc=f"sta{yb}",
                      accum_op=(ALU.add if g > 0 else ALU.bypass))
        sta_grp[g] = (p.val("sta0"), p.val("sta1"))
    p.barrier()

    s_yc = [env["ln_xc"][i][:] for i in range(2)]
    s_xc3 = [s_W[i].bitcast(F32)[:, 0:2048] for i in range(2)]
    s_sq2 = env["ln_sq"]
    xltf = xlt[:].bitcast(F32)
    s_o2 = [xltf[:, 2048 * i:2048 * (i + 1)] for i in range(2)]
    smz = {k: sb(f"fz_{k}", [128, 1], F32) for k in ("ssq", "t2e", "lnv", "rstd")}
    sta_totals = [p.val("sta0"), p.val("sta1")]
    for tc in range(16):
        b2 = tc % 2
        p.dma("sync", s_yc[b2], yd[128 * tc:128 * (tc + 1), :],
              waits=[("fch", p.val("fch") - 5), ("sta0", sta_totals[0]),
                     ("sta1", sta_totals[1])], inc=f"ldy{b2}")
        p.dma("sync", s_xc3[b2], x1[128 * tc:128 * (tc + 1), :],
              waits=[("fch", p.val("fch") - 5)], inc=f"ldy{b2}")
        ldv = p.val(f"ldy{b2}")
        p.emit("scalar", lambda e, b2=b2: e.activation(s_sq2[:], s_yc[b2], AF.Square,
                                                       accum_out=smz["ssq"][:]),
               waits=[(f"ldy{b2}", ldv), ("fch", p.val("fch"))], inc="fch")
        p.emit("vector", lambda e: e.tensor_scalar(smz["t2e"][:], smz["ssq"][:],
                                                   1.0 / D, 1e-6, ALU.mult, ALU.add),
               waits=[("fch", p.val("fch"))], inc="fch")
        p.emit("scalar", lambda e: e.activation(smz["lnv"][:], smz["t2e"][:], AF.Ln),
               waits=[("fch", p.val("fch"))], inc="fch")
        p.emit("scalar", lambda e: e.activation(smz["rstd"][:], smz["lnv"][:], AF.Exp,
                                                scale=-0.5),
               waits=[("fch", p.val("fch"))], inc="fch")
        p.emit("vector", lambda e, b2=b2: e.scalar_tensor_tensor(
            s_o2[b2], s_yc[b2], smz["rstd"][:], s_xc3[b2], ALU.mult, ALU.add),
            waits=[("fch", p.val("fch")), (f"stz{b2}", p.val(f"stz{b2}"))], inc="fch")
        p.dma("sync", x2[128 * tc:128 * (tc + 1), :], s_o2[b2],
              waits=[("fch", p.val("fch"))], inc=f"stz{b2}")

    return [("stz0", p.val("stz0")), ("stz1", p.val("stz1"))]


def common_identf(p, env):
    sb = env["sb"]
    identf = sb("c_identf", [128, 128], F32)
    env["identf"] = identf
    p.emit("gpsimd", lambda e: e.memset(identf[:], 0.0),
           waits=[("init", p.val("init"))], inc="init")
    p.emit("gpsimd", lambda e: e.affine_select(
        out=identf[:], in_=identf[:], compare_op=ALU.not_equal, fill=1.0,
        base=0, pattern=[[-1, 128]], channel_multiplier=1),
        waits=[("init", p.val("init"))], inc="init")


def emit_bias_T(p, env, bvec, s_bh, s_out, tag):
    pbk = env["pb"][1]
    p.dma("sync", s_bh[:], bvec.rearrange("(c p) -> c p", p=128),
          waits=[(tag + "bt", p.val(tag + "bt"))], inc=tag + "bl")
    p.emit("tensor", lambda e: e.transpose(pbk[:, 0:16], s_bh[:], env["identf"][0:16, 0:16]),
           waits=[(tag + "bl", p.val(tag + "bl")), (tag + "bt", p.val(tag + "bt"))],
           inc=tag + "bp")
    p.emit("vector", lambda e: e.tensor_copy(s_out[:], pbk[:, 0:16]),
           waits=[(tag + "bp", p.val(tag + "bp"))], inc=tag + "bt")


def emit_rope_tables(p, env, pos, s_sinf, s_cosf):
    sb = env["sb"]
    pb = env["pb"]
    s_dr = sb("rp_dr", [1, 128], I32)
    s_jr = sb("rp_jr", [1, 128], I32)
    s_jf = sb("rp_jf", [1, 128], F32)
    s_if = sb("rp_if", [1, 128], F32)
    s_pos = sb("rp_pos", [1, T], F32)
    s_u = sb("rp_u", [128, 512], F32)
    s_i = sb("rp_i", [128, 512], I32)
    s_f = sb("rp_f", [128, 512], F32)
    s_f2 = sb("rp_f2", [128, 512], F32)
    rc = p.sem("rc")

    def chain(eng, fn, extra=()):
        p.emit(eng, fn, waits=[("rc", p.val("rc"))] + list(extra), inc="rc")

    p.dma("sync", s_pos[:], pos, inc="rl")
    rl = p.val("rl")
    chain("gpsimd", lambda e: e.iota(s_dr[:], pattern=[[1, 128]], base=0, channel_multiplier=0))
    chain("vector", lambda e: e.tensor_scalar(s_jr[:], s_dr[:], 1, None, ALU.logical_shift_right))
    chain("vector", lambda e: e.tensor_copy(s_jf[:], s_jr[:]))
    LN1E4 = float(np.log(10000.0))
    chain("scalar", lambda e: e.activation(s_if[:], s_jf[:], AF.Exp, scale=-2.0 * LN1E4 / 128.0))
    INV2PI = float(1.0 / (2.0 * np.pi))
    TWOPI = float(2.0 * np.pi)
    for tg in range(4):
        chain("tensor", lambda e, tg=tg: e.matmul(
            pb[2][:], s_if[:], s_pos[:, 512 * tg:512 * (tg + 1)], start=True, stop=True),
            extra=[("rl", rl)])
        for phase, dst in ((0.0, s_sinf), (0.25, s_cosf)):
            chain("vector", lambda e, ph=phase: e.tensor_scalar(
                s_u[:], pb[2][:], INV2PI, ph, ALU.mult, ALU.add))
            chain("vector", lambda e: e.tensor_copy(s_i[:], s_u[:]))
            chain("vector", lambda e: e.tensor_copy(s_f[:], s_i[:]))
            chain("vector", lambda e: e.tensor_sub(s_f2[:], s_u[:], s_f[:]))
            chain("scalar", lambda e, dst=dst, tg=tg: e.activation(
                dst[:, 512 * tg:512 * (tg + 1)], s_f2[:], AF.Sin, scale=TWOPI))


def emit_psw(p, env, s_psw):
    sb = env["sb"]
    s_i1 = sb("pw_i1", [128, 128], F32)
    s_c1 = sb("pw_c1", [128, 128], F32)
    s_c2 = sb("pw_c2", [128, 128], F32)
    s_pi = sb("pw_pi", [128, 1], I32)
    s_pk = sb("pw_pk", [128, 1], I32)
    s_pkf = sb("pw_pkf", [128, 1], F32)
    s_ev = sb("pw_ev", [128, 1], F32)

    def chain(eng, fn):
        p.emit(eng, fn, waits=[("rc", p.val("rc"))], inc="rc")
    chain("gpsimd", lambda e: e.iota(s_i1[:], pattern=[[1, 128]], base=0, channel_multiplier=-1, allow_small_or_imprecise_dtypes=True))
    chain("vector", lambda e: e.tensor_single_scalar(s_c1[:], s_i1[:], 1.0, ALU.is_equal))
    chain("vector", lambda e: e.tensor_single_scalar(s_c2[:], s_i1[:], -1.0, ALU.is_equal))
    chain("gpsimd", lambda e: e.iota(s_pi[:], pattern=[[0, 1]], base=0, channel_multiplier=1))
    chain("vector", lambda e: e.tensor_single_scalar(s_pk[:], s_pi[:], 1, ALU.bitwise_and))
    chain("vector", lambda e: e.tensor_copy(s_pkf[:], s_pk[:]))
    chain("vector", lambda e: e.tensor_scalar(s_ev[:], s_pkf[:], -1.0, 1.0, ALU.mult, ALU.add))
    chain("vector", lambda e: e.tensor_scalar(s_c1[:], s_c1[:], s_ev[:], None, ALU.mult))
    chain("vector", lambda e: e.tensor_scalar(s_c2[:], s_c2[:], s_pkf[:], None, ALU.mult))
    chain("vector", lambda e: e.tensor_sub(s_psw[:], s_c1[:], s_c2[:]))


def emit_k3a(p, env):
    nc = env["nc"]; sb = env["sb"]; pb = env["pb"]; io = env["io"]
    x2, wq, wk, wv = io["x2"], io["wq"], io["wk"], io["wv"]
    bq, bk, bv, pos = io["bq"], io["bk"], io["bv"], io["pos"]
    qT, kT, vv = io["qT"], io["kT"], io["vv"]
    xlt = sb("xlt", [128, 16 * T], BF16)
    emit_ln(p, env, x2, xlt, "ln")
    p.emit("tensor", lambda e: e.nop(), waits=[("lndev", p.val("lndev"))])
    s_sinf = sb("s_sinf", [128, T], F32)
    s_cosf = sb("s_cosf", [128, T], F32)
    emit_rope_tables(p, env, pos, s_sinf, s_cosf)
    s_psw = sb("s_psw", [128, 128], F32)
    emit_psw(p, env, s_psw)
    s_pswb = sb("s_pswb", [128, 128], BF16)
    p.emit("vector", lambda e: e.tensor_copy(s_pswb[:], s_psw[:]),
           waits=[("rc", p.val("rc"))], inc="rc")
    s_bqT = sb("s_bqT", [128, 16], F32)
    s_bkT = sb("s_bkT", [128, 16], F32)
    s_bvR = sb("s_bvR", [1, D], F32)
    s_bvB = sb("s_bvB", [128, D], F32)
    s_bh = sb("s_bh", [16, 128], F32)
    emit_bias_T(p, env, bq, s_bh, s_bqT, "q")
    emit_bias_T(p, env, bk, s_bh, s_bkT, "k")
    p.dma("sync", s_bvR[:], bv.rearrange("(a d) -> a d", a=1), inc="rlb")
    rlv = p.val("rlb")
    for tg in range(4):
        p.emit("tensor", lambda e, tg=tg: e.matmul(
            pb[3][:], env["ones_row"][:], s_bvR[:, 512 * tg:512 * (tg + 1)],
            start=True, stop=True),
            waits=[("rlb", rlv), ("rc", p.val("rc"))], inc="rc")
        p.emit("vector", lambda e, tg=tg: e.tensor_copy(
            s_bvB[:, 512 * tg:512 * (tg + 1)], pb[3][:]),
            waits=[("rc", p.val("rc"))], inc="rc")
    p.barrier()

    s_w = [sb(f"wslab{i}", [128, 16 * 512], BF16) for i in range(2)]
    s_qs = [sb(f"s_qs{i}", [128, 512], BF16) for i in range(2)]
    s_m1 = sb("s_m1", [128, 512], F32)
    s_m2 = sb("s_m2", [128, 512], F32)
    s_qo = [sb(f"s_qo{i}", [128, 512], BF16) for i in range(2)]
    pes = p.sem("pes")
    aev = p.sem("aev")
    per = p.sem("per")
    dvr = p.sem("dvr")

    for mat, (wmat, bT, dst) in enumerate(((wq, s_bqT, qT), (wk, s_bkT, kT))):
        for og in range(4):
            wb = og % 2
            p.dma("gpsimd", s_w[wb][:],
                  wmat[:, 512 * og:512 * (og + 1)].rearrange("(ic p) o -> p ic o", p=128),
                  waits=[(pes, p.val(pes) - 16)], inc=f"ldw{wb}")
            ldv = p.val(f"ldw{wb}")
            for oc in range(4):
                h = 4 * og + oc
                for tg in range(4):
                    i = (h * 4 + tg) % 2
                    bank = pb[4 + tg % 2]
                    for ic in range(16):
                        p.emit("tensor", lambda e, wb=wb, ic=ic, oc=oc, tg=tg, bank=bank: e.matmul(
                            bank[:], s_w[wb][:, 512 * ic + 128 * oc:512 * ic + 128 * (oc + 1)],
                            xlt[:, 2048 * ic + 512 * tg:2048 * ic + 512 * (tg + 1)],
                            start=(ic == 0), stop=(ic == 15)),
                            waits=([(f"ldw{wb}", ldv), (aev, p.val(aev) - 1)] if ic == 0 else []),
                            inc=("pes" if ic == 15 else None))
                    p.emit("scalar", lambda e, i=i, h=h, bank=bank, bT=bT: e.activation(
                        s_qs[i][:], bank[:], AF.Identity, bias=bT[:, h:h + 1]),
                        waits=[(pes, p.val(pes)), (per, p.val(per) - 1),
                               (dvr, p.val(dvr) - 3)], inc="aev")
                    p.emit("tensor", lambda e, i=i: e.matmul(
                        pb[6][:], s_pswb[:], s_qs[i][:], start=True, stop=True),
                        waits=[(aev, p.val(aev)), (dvr, p.val(dvr) - 1)], inc="per")
                    p.emit("vector", lambda e, i=i, tg=tg: e.tensor_mul(
                        s_m1[:], s_qs[i][:], s_cosf[:, 512 * tg:512 * (tg + 1)]),
                        waits=[(aev, p.val(aev))])
                    p.emit("vector", lambda e, tg=tg: e.tensor_mul(
                        s_m2[:], pb[6][:], s_sinf[:, 512 * tg:512 * (tg + 1)]),
                        waits=[(per, p.val(per))], inc="dvr")
                    p.emit("vector", lambda e, i=i: e.tensor_add(s_qo[i][:], s_m1[:], s_m2[:]),
                           waits=[(f"stq{i}", p.val(f"stq{i}"))], inc="dvr")
                    p.dma("scalar", dst[128 * h:128 * (h + 1), 512 * tg:512 * (tg + 1)],
                          s_qo[i][:], waits=[(dvr, p.val(dvr))], inc=f"stq{i}")
    p.barrier()

    s_vo = [sb(f"s_vo{i}", [128, 512], BF16) for i in range(2)]
    for og in range(4):
        wb = og % 2
        p.dma("gpsimd", s_w[wb][:],
              wv[:, 512 * og:512 * (og + 1)].rearrange("(ic p) o -> p ic o", p=128),
              waits=[(pes, p.val(pes) - 16)], inc=f"ldw{wb}")
        ldv = p.val(f"ldw{wb}")
        for tc in range(16):
            i = (og * 16 + tc) % 2
            bank = pb[4 + tc % 2]
            for ic in range(16):
                p.emit("tensor", lambda e, wb=wb, ic=ic, tc=tc, bank=bank: e.matmul(
                    bank[:], xlt[:, 2048 * ic + 128 * tc:2048 * ic + 128 * (tc + 1)],
                    s_w[wb][:, 512 * ic:512 * (ic + 1)],
                    start=(ic == 0), stop=(ic == 15)),
                    waits=([(f"ldw{wb}", ldv), ("dvv", p.val("dvv") - 1)] if ic == 0 else []),
                    inc=("pes" if ic == 15 else None))
            p.emit("vector", lambda e, i=i, og=og, bank=bank: e.tensor_add(
                s_vo[i][:], bank[:], s_bvB[:, 512 * og:512 * (og + 1)]),
                waits=[(pes, p.val(pes)), (f"stq{i}", p.val(f"stq{i}"))], inc="dvv")
            p.dma("scalar", vv[128 * tc:128 * (tc + 1), 512 * og:512 * (og + 1)],
                  s_vo[i][:], waits=[("dvv", p.val("dvv"))], inc=f"stq{i}")

    return [("stq0", p.val("stq0")), ("stq1", p.val("stq1"))]


def emit_k3b(p, env):
    nc = env["nc"]; sb = env["sb"]; pb = env["pb"]; io = env["io"]
    qTd, kg, vg = io["qT2"], io["kg"], io["vg"]
    wo, bo, x2, x3 = io["wo"], io["bo"], io["x2b"], io["x3"]
    s_boR = sb("s_boR", [1, D], F32)
    s_boB = sb("s_boB", [128, D], F32)
    p.dma("sync", s_boR[:], bo.rearrange("(a d) -> a d", a=1), inc="rl")
    rlv = p.val("rl")
    for tg in range(4):
        p.emit("tensor", lambda e, tg=tg: e.matmul(
            pb[3][:], env["ones_row"][:], s_boR[:, 512 * tg:512 * (tg + 1)],
            start=True, stop=True),
            waits=[("rl", rlv), ("rc", p.val("rc"))], inc="rc")
        p.emit("vector", lambda e, tg=tg: e.tensor_copy(
            s_boB[:, 512 * tg:512 * (tg + 1)], pb[3][:]),
            waits=[("rc", p.val("rc"))], inc="rc")
    p.barrier()

    s_kh = [sb(f"s_kh{i}", [128, 4 * T], BF16) for i in range(2)]
    s_vh = [sb(f"s_vh{i}", [128, 4 * T], BF16) for i in range(2)]
    s_qt = [sb(f"s_qt{i}", [128, 512], BF16) for i in range(2)]
    s_pt = [sb(f"s_pt{i}", [128, 512], BF16) for i in range(4)]
    s_rs = sb("s_rs", [1, 512], F32)
    s_rsb = sb("s_rsb", [128, 512], F32)
    s_oall = sb("s_oall", [128, 16 * T], BF16)
    pes = p.sem("pes")
    ace = p.sem("ace")
    pev = p.sem("pev")
    aev = p.sem("aev")
    dvn = p.sem("dvn")

    for h in range(16):
        hb = h % 2
        for r in range(4):
            p.dma("gpsimd", s_kh[hb][:, 2048 * r:2048 * (r + 1)],
                  kg[2048 * r + 128 * h:2048 * r + 128 * (h + 1), :],
                  waits=[(pev, p.val(pev) - 512)] if h >= 2 else [], inc=f"ldk{hb}")
            p.dma("gpsimd", s_vh[hb][:, 2048 * r:2048 * (r + 1)],
                  vg[2048 * r:2048 * (r + 1), 128 * h:128 * (h + 1)].rearrange(
                      "(tc p) d -> p tc d", p=128),
                  inc=f"ldk{hb}")
        ldkv = p.val(f"ldk{hb}")
        sB = [pb[0], pb[1], pb[5], pb[6]]
        LAG = 3
        for tg in range(4):
            qb = (h * 4 + tg) % 2
            p.dma("sync", s_qt[qb][:], qTd[128 * h:128 * (h + 1), 512 * tg:512 * (tg + 1)],
                  waits=[(pes, p.val(pes) - 66)], inc=f"ldq{qb}")
            ldqv = p.val(f"ldq{qb}")
            exp_of = {}
            pv_of = {}

            def emit_pv(kc, hb=hb):
                ptb = s_pt[kc % 4]
                p.emit("tensor", lambda e, kc=kc, hb=hb, ptb=ptb: e.matmul(
                    pb[2][:], s_vh[hb][:, 128 * kc:128 * (kc + 1)], ptb[:],
                    start=(kc == 0), stop=(kc == 63), skip_group_check=True),
                    waits=[(ace, exp_of[kc])] + ([(dvn, p.val(dvn))] if kc == 0 else []),
                    inc="pev")
                p.emit("tensor", lambda e, kc=kc, ptb=ptb: e.matmul(
                    pb[3][0:1, :], env["ones_col"][:], ptb[:],
                    start=(kc == 0), stop=(kc == 63), skip_group_check=True),
                    inc="pev")
                pv_of[kc] = p.val(pev)

            for kc in range(64):
                sbank = sB[kc % 4]
                ptb = s_pt[kc % 4]
                p.emit("tensor", lambda e, kc=kc, hb=hb, qb=qb, sbank=sbank: e.matmul(
                    sbank[:], s_kh[hb][:, 128 * kc:128 * (kc + 1)], s_qt[qb][:],
                    start=True, stop=True, skip_group_check=True),
                    waits=([(f"ldk{hb}", ldkv), (f"ldq{qb}", ldqv), (ace, p.val(ace) - 1),
                            (dvn, p.val(dvn))] if kc <= 3 else
                           [(ace, exp_of.get(kc - 4, 0))]),
                    inc="pes")
                qk_done = p.val(pes)
                p.emit("scalar", lambda e, ptb=ptb, sbank=sbank: e.activation(
                    ptb[:], sbank[:], AF.Exp, scale=INV_SQRT_HD),
                    waits=[(pes, qk_done), (pev, pv_of.get(kc - 4, 0))], inc="ace")
                exp_of[kc] = p.val(ace)
                if kc >= LAG:
                    emit_pv(kc - LAG)
            for kc in range(64 - LAG, 64):
                emit_pv(kc)
            p.emit("vector", lambda e: e.reciprocal(s_rs[:], pb[3][0:1, :]),
                   waits=[(pev, p.val(pev))], inc="dvn")
            p.emit("tensor", lambda e: e.matmul(pb[4][:], env["ones_row"][:], s_rs[:],
                                                start=True, stop=True),
                   waits=[(dvn, p.val(dvn)), (aev, p.val(aev))], inc="pes")
            p.emit("scalar", lambda e: e.activation(s_rsb[:], pb[4][:], AF.Copy),
                   waits=[(pes, p.val(pes))], inc="aev")
            p.emit("vector", lambda e, h=h, tg=tg: e.tensor_mul(
                s_oall[:, 2048 * h + 512 * tg:2048 * h + 512 * (tg + 1)], pb[2][:], s_rsb[:]),
                   waits=[(aev, p.val(aev))], inc="dvn")
    p.barrier()

    s_w = [sb(f"wslab{i}", [128, 16 * 512], BF16) for i in range(2)]
    s_xcs = [sb(f"s_xcs{i}", [128, 512], F32) for i in range(2)]
    s_t1 = sb("s_t1", [128, 512], F32)
    s_ys = [sb(f"s_ys{i}", [128, 512], F32) for i in range(2)]
    dvy = p.sem("dvy")
    for og in range(4):
        wb = og % 2
        p.dma("gpsimd", s_w[wb][:],
              wo[:, 512 * og:512 * (og + 1)].rearrange("(ic p) o -> p ic o", p=128),
              waits=[(dvy, p.val(dvy) - 16)] if og >= 2 else [], inc=f"ldw{wb}")
        ldv = p.val(f"ldw{wb}")
        for tc in range(16):
            i = (og * 16 + tc) % 2
            bank = pb[4 + tc % 2]
            p.dma("sync", s_xcs[i][:],
                  x2[128 * tc:128 * (tc + 1), 512 * og:512 * (og + 1)],
                  waits=[(dvy, p.val(dvy) - 1)], inc=f"ldx{i}")
            ldxv = p.val(f"ldx{i}")
            for ec in range(16):
                p.emit("tensor", lambda e, wb=wb, ec=ec, tc=tc, bank=bank: e.matmul(
                    bank[:], s_oall[:, 2048 * ec + 128 * tc:2048 * ec + 128 * (tc + 1)],
                    s_w[wb][:, 512 * ec:512 * (ec + 1)],
                    start=(ec == 0), stop=(ec == 15)),
                    waits=([(f"ldw{wb}", ldv), (dvy, p.val(dvy))] if ec == 0 else []),
                    inc=("pes" if ec == 15 else None))
            p.emit("vector", lambda e, og=og, bank=bank: e.tensor_add(
                s_t1[:], bank[:], s_boB[:, 512 * og:512 * (og + 1)]),
                waits=[(pes, p.val(pes)), (f"ldx{i}", ldxv),
                       (f"sty{i}", p.val(f"sty{i}"))])
            p.emit("vector", lambda e, i=i: e.tensor_add(s_ys[i][:], s_t1[:], s_xcs[i][:]),
                   inc="dvy")
            p.dma("sync", x3[128 * tc:128 * (tc + 1), 512 * og:512 * (og + 1)],
                  s_ys[i][:], waits=[(dvy, p.val(dvy))], inc=f"sty{i}")

    return [("sty0", p.val("sty0")), ("sty1", p.val("sty1"))]


def emit_dec_nvfp4(p, env, u8v4, nC, s_W, scb, wv_maker, waits_first, inc_last, dec):
    u8c, u16, k_, sh_, b_, t_, vp_, v_ = (dec[n] for n in
                                          ("u8c", "u16", "k", "sh", "b", "t", "vp", "v"))
    p.emit("vector", lambda e: e.tensor_copy(u8c[:, :nC], u8v4[:, 0::4]), waits=waits_first)
    p.emit("vector", lambda e: e.tensor_copy(u16[:, :nC], u8c[:, :nC]))
    for (kshift, sshift), par in (((0, 12), 0), ((4, 8), 1)):
        if kshift == 0:
            p.emit("vector", lambda e: e.tensor_scalar(k_[:, :nC], u16[:, :nC], 7, None, ALU.bitwise_and))
        else:
            p.emit("vector", lambda e, ks=kshift: e.tensor_scalar(
                k_[:, :nC], u16[:, :nC], ks, 7, ALU.logical_shift_right, ALU.bitwise_and))
        p.emit("vector", lambda e, ss=sshift: e.tensor_scalar(
            sh_[:, :nC], u16[:, :nC], ss, 0x8000, ALU.logical_shift_left, ALU.bitwise_and))
        p.emit("vector", lambda e: e.tensor_scalar(b_[:, :nC], k_[:, :nC], 64.0, 16128.0, ALU.mult, ALU.add))
        p.emit("vector", lambda e: e.tensor_scalar(t_[:, :nC], b_[:, :nC].bitcast(BF16), 2.0, -1.0, ALU.mult, ALU.add))
        p.emit("vector", lambda e: e.tensor_tensor(vp_[:, :nC], t_[:, :nC], b_[:, :nC].bitcast(BF16), ALU.min))
        p.emit("vector", lambda e: e.tensor_tensor(v_[:, :nC].bitcast(U16), vp_[:, :nC].bitcast(U16), sh_[:, :nC], ALU.bitwise_xor))
        wv, ng = wv_maker(par)
        vv_ = v_[:, :nC].rearrange("p (g k) -> p g k", g=ng)
        p.emit("vector", lambda e, wv=wv, vv_=vv_: e.tensor_tensor(wv, vv_, scb, ALU.mult),
               inc=(inc_last if par == 1 else None))


def emit_k5(p, env):
    nc = env["nc"]; sb = env["sb"]; pb = env["pb"]; io = env["io"]
    x3, out, wdT = io["x3b"], io["out"], io["wdT1"]
    gWp, gs, uWp, us, dWp, dsc = (io["gWp"], io["gs"], io["uWp"], io["us"],
                                  io["dWp"], io["dsc"])
    xlt = sb("xlt", [128, 16 * T], BF16)
    emit_ln(p, env, x3, xlt, "ln")
    p.emit("tensor", lambda e: e.nop(), waits=[("lndev", p.val("lndev"))])
    p.dma("gpsimd", out, x3, inc="sti")
    sti_v = p.val("sti")

    dec = {}
    dec["u8c"] = sb("dec_u8c", [128, 1024], U8)
    for n_ in ("u16", "k", "sh", "b"):
        dec[n_] = sb(f"dec_{n_}", [128, 1024], U16)
    for n_ in ("t", "vp", "v"):
        dec[n_] = sb(f"dec_{n_}", [128, 1024], BF16)
    s_pk = [sb(f"s_pk{i}", [128, 1024], I32) for i in range(2)]
    s_sc = [sb(f"s_sc{i}", [128, 64], F32) for i in range(2)]
    s_W = [sb(f"s_W{i}", [128, 2048], BF16) for i in range(2)]
    s_wT = [sb(f"s_wT{i}", [128, 2048], BF16) for i in range(2)]
    s_gg = sb("s_gg", [128, 2048], BF16)
    s_yst = [sb(f"s_yst{i}", [128, 512], F32) for i in range(2)]
    pk5 = [sb("pk5_0", [128, 1024], I32)]
    sc5 = [sb("sc5_0", [128, 64], F32)]
    W5 = [sb("W5_0", [128, 2048], BF16)]
    st5 = [sb(f"st5_{i}", [128, 512], BF16) for i in range(2)]
    s_h = [sb(f"s_h{i}", [128, 2048], BF16) for i in range(12)]
    _xc = env["ln_xc"]; _sq = env["ln_sq"]; _tl = env["ln_tln"]
    s_wd = [
        _xc[0][:].bitcast(BF16)[:, 0:2048], _xc[0][:].bitcast(BF16)[:, 2048:4096],
        _xc[1][:].bitcast(BF16)[:, 0:2048], _xc[1][:].bitcast(BF16)[:, 2048:4096],
        _sq[:].bitcast(BF16)[:, 0:2048], _sq[:].bitcast(BF16)[:, 2048:4096],
        _tl[0][:], _tl[1][:],
    ]
    dvd = p.sem("dvd")
    pet = p.sem("pet")
    dve = p.sem("dve")
    ld5 = p.sem("ld5")
    dvd5 = p.sem("dvd5")
    pet5 = p.sem("pet5")
    dve5 = p.sem("dve5")
    pem = p.sem("pem")
    dvg = p.sem("dvg")
    ach = p.sem("ach")
    dvh = p.sem("dvh")
    ldd = p.sem("ldd")
    pdn = p.sem("pdn")
    dvo = p.sem("dvo")

    NCH = 256
    HRING = 12

    sa_stw = {}

    def emit_5a_load(i):
        cc, f = divmod(i, 16)
        p.dma("sync", pk5[0][:], dWp[128 * f:128 * (f + 1), 1024 * cc:1024 * (cc + 1)],
              waits=[(dvd5, p.val(dvd5))], inc="ld5")
        p.dma("sync", sc5[0][:], dsc[128 * f:128 * (f + 1), 64 * cc:64 * (cc + 1)],
              inc="ld5")

    def emit_5a_chunk(i):
        cc, f = divmod(i, 16)
        ldv = p.val(ld5)
        scb = sc5[0][:].unsqueeze(2).broadcast_to([128, 64, 16])

        def wv_mk(par):
            return W5[0][:, par::2].rearrange("p (g k) -> p g k", g=64), 64
        emit_dec_nvfp4(p, env, pk5[0][:].bitcast(U8), 1024, W5[0], scb, wv_mk,
                       [(ld5, ldv), (pet5, p.val(pet5))], "dvd5", dec)
        dcv = p.val(dvd5)
        if i + 1 < 128:
            emit_5a_load(i + 1)
        for g in range(4):
            for k in range(4):
                c = 4 * g + k
                p.emit("tensor", lambda e, c=c, k=k: e.transpose(
                    pb[1].bitcast(BF16)[:, 128 * k:128 * (k + 1)],
                    W5[0][:, 128 * c:128 * (c + 1)], env["ident"][:]),
                    waits=([(dvd5, dcv), (dve5, p.val(dve5))] if k == 0 else []),
                    inc=("pet5" if k == 3 else None))
            sb_i = g % 2
            p.emit("vector", lambda e, sb_i=sb_i: e.tensor_copy(
                st5[sb_i][:], pb[1].bitcast(BF16)[:, 0:512]),
                waits=[(pet5, p.val(pet5)), (f"stw{sb_i}", p.val(f"stw{sb_i}"))],
                inc="dve5")
            dst = wdT[2048 * cc + 512 * g:2048 * cc + 512 * (g + 1),
                      128 * f:128 * (f + 1)].rearrange("(t p) q -> p t q", p=128)
            srcv = st5[sb_i][:].rearrange("p (t q) -> p t q", t=4)
            p.dma("scalar", dst, srcv, waits=[(dve5, p.val(dve5))], inc=f"stw{sb_i}")
        if f == 15:
            sa_stw[cc] = (p.val("stw0"), p.val("stw1"))

    def emit_pk_load(n):
        c, half = divmod(n, 2)
        Wp, sc = (gWp, gs) if half == 0 else (uWp, us)
        b = n % 2
        p.dma("gpsimd", s_pk[b][:], Wp[128 * c:128 * (c + 1), :],
              waits=[(dvd, p.val(dvd) - 1)], inc=f"ldp{b}")
        p.dma("gpsimd", s_sc[b][:], sc[128 * c:128 * (c + 1), :], inc=f"ldp{b}")

    def emit_dec(n):
        b = n % 2
        ldv = ld_of[n]
        scb = s_sc[b][:].unsqueeze(2).broadcast_to([128, 64, 16])

        def wv_mk(par, b=b):
            return s_W[b][:, par::2].rearrange("p (g k) -> p g k", g=64), 64
        emit_dec_nvfp4(p, env, s_pk[b][:].bitcast(U8), 1024, s_W[b], scb, wv_mk,
                       [(f"ldp{b}", ldv), (pet, p.val(pet))], "dvd", dec)
        dec_of[n] = p.val(dvd)

    def emit_T(n):
        b = n % 2
        for grp in range(4):
            for k in range(4):
                ic = 4 * grp + k
                p.emit("tensor", lambda e, b=b, ic=ic, k=k: e.transpose(
                    pb[0].bitcast(BF16)[:, 128 * k:128 * (k + 1)],
                    s_W[b][:, 128 * ic:128 * (ic + 1)], env["ident"][:]),
                    waits=([(dvd, dec_of[n]), (dve, p.val(dve))] if k == 0 else []),
                    inc=("pet" if k == 3 else None))
            p.emit("vector", lambda e, b=b, grp=grp: e.tensor_copy(
                s_wT[b][:].rearrange("p (ic q) -> p ic q", ic=16)[:, 4 * grp:4 * grp + 4, :],
                pb[0].bitcast(BF16)[:, 0:512].rearrange("p (t q) -> p t q", t=4)),
                waits=[(pet, p.val(pet)), (pem, p.val(pem) - 4)], inc="dve")
        T_of[n] = p.val(dve)

    down_done = {-1: 0}
    h_of = {}
    bank_free = {tg: 0 for tg in range(4)}

    def emit_M(n):
        c, half = divmod(n, 2)
        b = n % 2
        slot = c % HRING
        for tg in range(4):
            bank = pb[4 + tg]
            for ic in range(16):
                w = []
                if ic == 0:
                    w = ([(dve, T_of[n]), bank_free[tg]] if bank_free[tg]
                         else [(dve, T_of[n])])
                p.emit("tensor", lambda e, b=b, ic=ic, tg=tg, bank=bank: e.matmul(
                    bank[:], s_wT[b][:, 128 * ic:128 * (ic + 1)],
                    xlt[:, 2048 * ic + 512 * tg:2048 * ic + 512 * (tg + 1)],
                    start=(ic == 0), stop=(ic == 15)),
                    waits=w, inc=("pem" if ic == 15 else None))
            pemv = p.val(pem)
            if half == 0:
                p.emit("vector", lambda e, tg=tg, bank=bank: e.tensor_copy(
                    s_gg[:, 512 * tg:512 * (tg + 1)], bank[:]),
                    waits=[(pem, pemv)], inc="dvg")
                bank_free[tg] = ("dvg", p.val(dvg))
                p.emit("scalar", lambda e, tg=tg: e.activation(
                    s_gg[:, 512 * tg:512 * (tg + 1)],
                    s_gg[:, 512 * tg:512 * (tg + 1)], AF.Gelu_apprx_tanh),
                    waits=[("dvg", p.val(dvg))], inc="ach")
            else:
                p.emit("vector", lambda e, tg=tg, slot=slot, bank=bank: e.tensor_mul(
                    s_h[slot][:, 512 * tg:512 * (tg + 1)],
                    s_gg[:, 512 * tg:512 * (tg + 1)], bank[:]),
                    waits=[(pem, pemv), (ach, p.val(ach)),
                           (pdn, down_done.get((c - HRING) // 8, 0))],
                    inc="dvh")
                bank_free[tg] = ("dvh", p.val(dvh))
        if half == 1:
            h_of[c] = p.val(dvh)

    wd_loaded = {}

    def emit_wd_loads(g):
        for ci in range(8):
            c = 8 * g + ci
            p.dma("gpsimd", s_wd[ci], wdT[128 * c:128 * (c + 1), :],
                  waits=[("stw0", sa_stw[g // 2][0]), ("stw1", sa_stw[g // 2][1]),
                         (pdn, down_done[g - 1]),
                         ("lnch", p.val("lnch"))], inc="ldd")
        wd_loaded[g] = p.val(ldd)

    def emit_down_regions(g, rr):
        hg_done = h_of[8 * g + 7]
        for r in rr:
            tc, fg = divmod(r, 4)
            bank = pb[2 + r % 2]
            for ci in range(8):
                slot = (8 * g + ci) % HRING
                p.emit("tensor", lambda e, ci=ci, tc=tc, fg=fg, bank=bank, slot=slot: e.matmul(
                    bank[:], s_h[slot][:, 128 * tc:128 * (tc + 1)],
                    s_wd[ci][:, 512 * fg:512 * (fg + 1)],
                    start=(ci == 0), stop=(ci == 7)),
                    waits=([(ldd, wd_loaded[g]), (dvh, hg_done),
                            (dvo, p.val(dvo) - 1)] if ci == 0 else []),
                    inc=("pdn" if ci == 7 else None))
            yb = r % 2
            p.emit("vector", lambda e, yb=yb, bank=bank: e.tensor_copy(
                s_yst[yb][:], bank[:]),
                waits=[(pdn, p.val(pdn)), (f"sta{yb}", p.val(f"sta{yb}"))], inc="dvo")
            gw = [(f"sta{yb}", sta_grp[g - 1][yb])] if g > 0 else []
            p.dma("gpsimd", out[128 * tc:128 * (tc + 1), 512 * fg:512 * (fg + 1)],
                  s_yst[yb][:],
                  waits=[(dvo, p.val(dvo)), ("sti", sti_v)] + gw, inc=f"sta{yb}",
                  accum_op=ALU.add)

    ld_of = {}
    dec_of = {}
    T_of = {}
    sta_grp = {}
    emit_5a_load(0)
    for i in range(32):
        emit_5a_chunk(i)
    sa_next = 32
    emit_pk_load(0)
    ld_of[0] = p.val("ldp0")
    emit_dec(0)
    for n in range(NCH):
        if n + 1 < NCH:
            emit_pk_load(n + 1)
            ld_of[n + 1] = p.val(f"ldp{(n + 1) % 2}")
        emit_T(n)
        if n + 1 < NCH:
            emit_dec(n + 1)

        if n % 2 == 0 and sa_next < 128:
            emit_5a_chunk(sa_next)
            sa_next += 1
        emit_M(n)
        g = n // 16
        if n >= 16 and (n % 16) < 8:
            gprev = g - 1
            if n % 16 == 0:
                emit_wd_loads(gprev)
            emit_down_regions(gprev, range((n % 16) * 8, (n % 16) * 8 + 8))
            if n % 16 == 7:
                down_done[gprev] = p.val(pdn)
                sta_grp[gprev] = (p.val("sta0"), p.val("sta1"))
    emit_wd_loads(15)
    emit_down_regions(15, range(64))
    down_done[15] = p.val(pdn)
    sta_grp[15] = (p.val("sta0"), p.val("sta1"))
    return [("sta0", p.val("sta0")), ("sta1", p.val("sta1"))]


EI, EO, IN = "ExternalInput", "ExternalOutput", "Internal"

DECLS_K1 = [("xin", [T, D], F32, EI), ("wqkv", [D, 3 * D], F32, EI),
            ("wout", [D, D], F32, EI)]
DECLS_K2 = [("gup", [11264, 256], I32, EI), ("gus", [16, 11264], F16, EI),
            ("dp", [D, 704], I32, EI), ("ds", [44, D], F16, EI),
            ("wdT0", [5632, D], BF16, IN), ("yd", [T, D], F32, IN)]
DECLS_K3A = [("wq", [D, D], F32, EI), ("wk", [D, D], F32, EI), ("wv", [D, D], F32, EI),
             ("bq", [D], F32, EI), ("bk", [D], F32, EI), ("bv", [D], F32, EI),
             ("pos", [1, T], F32, EI), ("qT", [D, T], BF16, EO),
             ("kT", [D, T], BF16, EO), ("vv", [T, D], BF16, EO)]
DECLS_K3B = [("qT2", [D, T], BF16, EI), ("kg", [4 * D, T], BF16, EI),
             ("vg", [4 * T, D], BF16, EI), ("wo", [D, D], F32, EI), ("bo", [D], F32, EI)]
DECLS_K5 = [("gWp", [16384, 1024], I32, EI), ("gs", [16384, 64], F32, EI),
            ("uWp", [16384, 1024], I32, EI), ("us", [16384, 64], F32, EI),
            ("dWp", [D, 8192], I32, EI), ("dsc", [D, 512], F32, EI),
            ("wdT1", [16384, D], BF16, IN), ("out", [T, D], F32, EO)]


def build_launchA():
    decls = (DECLS_K1 + [("x1", [T, D], F32, IN)] + DECLS_K2 +
             [("x2", [T, D], F32, EO)] + DECLS_K3A)
    return build_launch([emit_k1, emit_k2, emit_k3a], decls)


def build_launchB():
    decls = (DECLS_K3B + [("x2b", [T, D], F32, EI), ("x3", [T, D], F32, IN)] + DECLS_K5)
    return build_launch([emit_k3b, emit_k5], decls, aliases={"x3b": "x3"})


def build_k1():
    return build_launch([emit_k1], DECLS_K1 + [("x1", [T, D], F32, EO)])


def build_k2():
    return build_launch([emit_k2], [("x1", [T, D], F32, EI)] + DECLS_K2 +
                        [("x2", [T, D], F32, EO)])


def build_k3a():
    return build_launch([emit_k3a], [("x2", [T, D], F32, EI)] + DECLS_K3A)


def build_k3b():
    return build_launch([emit_k3b], DECLS_K3B +
                        [("x2b", [T, D], F32, EI), ("x3", [T, D], F32, EO)])


def build_k5():
    return build_launch([emit_k5], [("x3b", [T, D], F32, EI)] + DECLS_K5)


def _shard_x(x):
    out = []
    for c in range(NCORES):
        b, s = c // 4, 2048 * (c % 4)
        out.append(np.ascontiguousarray(x[b, s:s + 2048, :], dtype=np.float32))
    return out


def kernel(**inputs):
    f32 = lambda k: np.ascontiguousarray(np.asarray(inputs[k], np.float32))
    i32 = lambda k: np.ascontiguousarray(np.asarray(inputs[k], np.int32))
    f16 = lambda k: np.ascontiguousarray(np.asarray(inputs[k], np.float16))

    xs = _shard_x(f32("L0_attn_x"))
    cores = list(range(NCORES))

    ncA = build_launchA()
    in_a = []
    for c in cores:
        s = 2048 * (c % 4)
        in_a.append({
            "xin": xs[c],
            "wqkv": f32("L0_attn_w_qkv"), "wout": f32("L0_attn_w_out"),
            "gup": i32("L0_ffn_gate_up_packed"), "gus": f16("L0_ffn_gate_up_scales"),
            "dp": i32("L0_ffn_down_packed"), "ds": f16("L0_ffn_down_scales"),
            "wq": f32("L1_attn_Wq"), "wk": f32("L1_attn_Wk"), "wv": f32("L1_attn_Wv"),
            "bq": f32("L1_attn_bq"), "bk": f32("L1_attn_bk"), "bv": f32("L1_attn_bv"),
            "pos": np.arange(s, s + 2048, dtype=np.float32).reshape(1, 2048),
        })
    rA = run_bass_kernel_spmd(ncA, in_a, core_ids=cores)
    x2 = [np.asarray(rA.results[c]["x2"]) for c in cores]
    qT = [np.asarray(rA.results[c]["qT"]) for c in cores]
    kT = [np.asarray(rA.results[c]["kT"]) for c in cores]
    vv = [np.asarray(rA.results[c]["vv"]) for c in cores]

    ncB = build_launchB()
    in_b = []
    for c in cores:
        b = c // 4
        grp = [4 * b + i for i in range(4)]
        in_b.append({
            "qT2": qT[c],
            "kg": np.ascontiguousarray(np.concatenate([kT[r] for r in grp], axis=0)),
            "vg": np.ascontiguousarray(np.concatenate([vv[r] for r in grp], axis=0)),
            "wo": f32("L1_attn_Wo"), "bo": f32("L1_attn_bo"), "x2b": x2[c],
            "gWp": i32("L1_ffn_gate_W_packed"), "gs": f32("L1_ffn_gate_scales"),
            "uWp": i32("L1_ffn_up_W_packed"), "us": f32("L1_ffn_up_scales"),
            "dWp": i32("L1_ffn_down_W_packed"), "dsc": f32("L1_ffn_down_scales"),
        })
    rB = run_bass_kernel_spmd(ncB, in_b, core_ids=cores)

    out = np.empty((2, 8192, 2048), np.float32)
    for c in cores:
        b, s = c // 4, 2048 * (c % 4)
        out[b, s:s + 2048, :] = rB.results[c]["out"]
    return out


# version 06
# speedup vs baseline: 1.3599x; 1.0675x over previous
import numpy as np
from contextlib import ExitStack

import concourse.bass as bass
import concourse.mybir as mybir
from concourse.bass_utils import run_bass_kernel_spmd

F32 = mybir.dt.float32
BF16 = mybir.dt.bfloat16
F16 = mybir.dt.float16
I32 = mybir.dt.int32
U16 = mybir.dt.uint16
U8 = mybir.dt.uint8
AF = mybir.ActivationFunctionType
ALU = mybir.AluOpType
AX = mybir.AxisListType

ENGINES = ("sync", "vector", "scalar", "tensor", "gpsimd")
T = 2048
D = 2048
NCORES = 8
INV_SQRT_HD = float(1.0 / np.sqrt(128.0))


class P:

    def __init__(self):
        self.steps = {e: [] for e in ENGINES}
        self.sems = {}
        self.order = []

    def sem(self, name):
        if name not in self.sems:
            self.sems[name] = 0
            self.order.append(name)
        return name

    def val(self, name):
        self.sem(name)
        return self.sems[name]

    def emit(self, eng, fn, waits=(), inc=None, inc_n=1):
        waits = tuple((s, v) for (s, v) in waits if v > 0)
        if inc is not None:
            self.sem(inc)
            self.sems[inc] += inc_n
        self.steps[eng].append((fn, waits, inc, inc_n))

    def dma(self, eng, out, in_, waits=(), inc=None, **kw):
        self.emit(eng, lambda e, out=out, in_=in_, kw=kw: e.dma_start(out=out, in_=in_, **kw),
                  waits=waits, inc=inc, inc_n=16)

    def barrier(self, engines=ENGINES):
        bar = self.sem("bar")
        for e in engines:
            self.emit(e, lambda eng: eng.nop(), inc="bar")
        tgt = self.sems["bar"]
        for e in engines:
            self.steps[e].append((None, ((bar, tgt),), None, 0))

    def finalize(self, nc, ctx, final_wait=None):
        handles = {n: ctx.enter_context(nc.semaphore(n)) for n in self.order}
        with nc.Block() as block:
            for eng_name in ENGINES:
                if not self.steps[eng_name] and eng_name != "sync":
                    continue

                def mk(eng_name=eng_name):
                    def body(eng):
                        for fn, waits, inc, inc_n in self.steps[eng_name]:
                            for s, v in waits:
                                eng.wait_ge(handles[s], v)
                            if fn is None:
                                continue
                            inst = fn(eng)
                            if inc:
                                inst.then_inc(handles[inc], inc_n)
                        if eng_name == "sync" and final_wait is not None:
                            eng.wait_ge(handles[final_wait[0]], final_wait[1])
                    return body
                getattr(block, eng_name)(mk())



SBUF_TOP = 229344


def make_carver(nc, base):
    state = {"off": base}

    def carve(name, shape, dtype):
        nbytes = int(np.prod(shape[1:])) * mybir.dt.size(dtype)
        off = (state["off"] + 31) // 32 * 32
        t = nc.alloc_sbuf_tensor_at(name, shape, dtype, offset=off)
        state["off"] = off + nbytes
        assert state["off"] <= SBUF_TOP, f"carve overflow at {name}: {state['off']}"
        return t

    def reset():
        state["off"] = base
    return carve, reset


def build_launch(phase_emitters, dram_decls, aliases=None):
    nc = bass.Bass("TRN2", target_bir_lowering=False, debug=False)
    io = {}
    for name, shape, dtype, kind in dram_decls:
        io[name] = nc.dram_tensor(name, shape, dtype, kind=kind).ap()
    if aliases:
        for k, v in aliases.items():
            io[k] = io[v]
    ctx = ExitStack()
    env = {"nc": nc, "io": io}
    pb = [nc.alloc_psum_tensor(f"pb{i}", [128, 512], F32) for i in range(8)]
    env["pb"] = pb

    def sb_fixed(name, shape, dtype):
        return nc.alloc_sbuf_tensor(name, shape, dtype)
    env["sb"] = sb_fixed
    p = P()
    p.sem("init")
    common_small(p, env)
    common_identf(p, env)
    arena_base = SBUF_TOP - nc.sbuf_bytes_remaining
    carve, reset = make_carver(nc, arena_base)
    p.barrier()
    for i, emit in enumerate(phase_emitters):
        reset()
        pref = f"ph{i}_"
        env["sb"] = (lambda pref=pref: lambda name, shape, dtype:
                     carve(pref + name, shape, dtype))()
        waits = emit(p, env)
        for s, v in waits:
            p.emit("sync", lambda e: e.nop(), waits=[(s, v)])
        p.barrier()
    p.finalize(nc, ctx, final_wait=None)
    env["ctx"] = ctx
    return nc


def common_small(p, env):
    nc = env["nc"]
    sb = env["sb"]
    ident = sb("c_ident", [128, 128], BF16)
    m_incl = sb("c_mincl", [128, 128], BF16)
    m_strict = sb("c_mstr", [128, 128], BF16)
    ones_col = sb("c_ones", [128, 1], BF16)
    ones_row = sb("c_onesr", [1, 128], F32)
    env["ident"], env["m_incl"], env["m_strict"] = ident, m_incl, m_strict
    env["ones_col"], env["ones_row"] = ones_col, ones_row

    def ch(fn):
        p.emit("gpsimd", fn, waits=[("init", p.val("init"))], inc="init")
    ch(lambda e: e.memset(ident[:], 0.0))
    ch(lambda e: e.affine_select(
        out=ident[:], in_=ident[:], compare_op=ALU.not_equal, fill=1.0,
        base=0, pattern=[[-1, 128]], channel_multiplier=1))
    ch(lambda e: e.memset(m_incl[:], 1.0))
    ch(lambda e: e.affine_select(
        out=m_incl[:], in_=m_incl[:], compare_op=ALU.is_ge, fill=0.0,
        base=0, pattern=[[1, 128]], channel_multiplier=-1))
    ch(lambda e: e.memset(m_strict[:], 1.0))
    ch(lambda e: e.affine_select(
        out=m_strict[:], in_=m_strict[:], compare_op=ALU.is_gt, fill=0.0,
        base=0, pattern=[[1, 128]], channel_multiplier=-1))
    ch(lambda e: e.memset(ones_col[:], 1.0))
    ch(lambda e: e.memset(ones_row[:], 1.0))


def emit_ln(p, env, xin, xlt, pref):
    nc = env["nc"]
    sb = env["sb"]
    s_xc = [sb(f"{pref}_xc{i}", [128, T], F32) for i in range(2)]
    s_sq = sb(f"{pref}_sq", [128, T], F32)
    s_tln = [sb(f"{pref}_tln{i}", [128, T], BF16) for i in range(2)]
    sm = {k: sb(f"{pref}_{k}", [128, 1], F32)
          for k in ("ssq", "ssum", "nm", "t2e", "mm", "ve", "lnv", "rstd", "negmr")}
    env[pref + "_xc"] = s_xc
    env[pref + "_sq"] = s_sq
    env[pref + "_tln"] = s_tln
    ch = p.sem(pref + "ch")
    app = p.sem(pref + "app")
    ptr = p.sem(pref + "ptr")
    dev = p.sem(pref + "dev")
    ident = env["ident"]
    pT = env["pb"][0].bitcast(BF16)

    def chain(eng, fn, extra=()):
        p.emit(eng, fn, waits=[(ch, p.val(ch))] + list(extra), inc=pref + "ch")

    for ci in range(16):
        b = ci % 2
        p.dma("sync", s_xc[b][:], xin[128 * ci:128 * (ci + 1), :],
              waits=[(ch, p.val(ch) - 10)], inc=pref + f"ld{b}")
        ldv = p.val(pref + f"ld{b}")
        chain("vector", lambda e, b=b: e.reduce_sum(sm["ssum"][:], s_xc[b][:], axis=AX.X),
              extra=[(pref + f"ld{b}", ldv)])
        chain("scalar", lambda e, b=b: e.activation(s_sq[:], s_xc[b][:], AF.Square,
                                                    accum_out=sm["ssq"][:]))
        chain("vector", lambda e: e.tensor_scalar(sm["nm"][:], sm["ssum"][:], -1.0 / D, None, ALU.mult))
        chain("vector", lambda e: e.tensor_scalar(sm["t2e"][:], sm["ssq"][:], 1.0 / D, 1e-5, ALU.mult, ALU.add))
        chain("vector", lambda e: e.tensor_mul(sm["mm"][:], sm["nm"][:], sm["nm"][:]))
        chain("vector", lambda e: e.tensor_sub(sm["ve"][:], sm["t2e"][:], sm["mm"][:]))
        chain("scalar", lambda e: e.activation(sm["lnv"][:], sm["ve"][:], AF.Ln))
        chain("scalar", lambda e: e.activation(sm["rstd"][:], sm["lnv"][:], AF.Exp, scale=-0.5))
        chain("vector", lambda e: e.tensor_mul(sm["negmr"][:], sm["nm"][:], sm["rstd"][:]))
        chain("scalar", lambda e, b=b: e.activation(s_tln[b][:], s_xc[b][:], AF.Identity,
                                                    bias=sm["negmr"][:], scale=sm["rstd"][:]),
              extra=[(dev, p.val(dev) - 4)])
        for g in range(4):
            for k in range(4):
                fj = 4 * g + k
                p.emit("tensor", lambda e, b=b, fj=fj, k=k: e.transpose(
                    pT[:, 128 * k:128 * (k + 1)], s_tln[b][:, 128 * fj:128 * (fj + 1)], ident[:]),
                    waits=[(ch, p.val(ch)), (dev, p.val(dev))] if k == 0 else [],
                    inc=pref + "ptr" if k == 3 else None)
            dst = xlt[:].rearrange("p (f t) -> p f t", f=16)[:, 4 * g:4 * g + 4,
                                                            128 * ci:128 * (ci + 1)]
            src = pT[:, 0:512].rearrange("p (f t) -> p f t", f=4)
            p.emit("vector", lambda e, dst=dst, src=src: e.tensor_copy(dst, src),
                   waits=[(ptr, p.val(ptr))], inc=pref + "dev")


def emit_k1(p, env):
    nc = env["nc"]; sb = env["sb"]; pb = env["pb"]; io = env["io"]
    xin, wqkv, wout, x1 = io["xin"], io["wqkv"], io["wout"], io["x1"]
    xlt = sb("xlt", [128, 16 * T], BF16)
    emit_ln(p, env, xin, xlt, "ln")
    p.emit("tensor", lambda e: e.nop(), waits=[("lndev", p.val("lndev"))])
    p.barrier()

    s_w = [sb(f"wslab{i}", [128, 16 * 512], BF16) for i in range(2)]
    s_q = sb("s_q", [128, 16 * 512], BF16)
    s_k = sb("s_k", [128, 16 * 512], BF16)
    s_v = sb("s_v", [128, 4 * D], BF16)
    s_o = sb("s_o", [128, 16 * 512], BF16)
    s_pt = [sb(f"s_pt{i}", [128, 512], BF16) for i in range(4)]
    s_rs = sb("s_rs", [1, 512], F32)
    s_rsb = sb("s_rsb", [128, 512], F32)
    s_xcs = [env["ln_tln"][0][:].bitcast(F32)[:, 512 * i:512 * (i + 1)] for i in range(2)]
    s_ys = [env["ln_tln"][1][:].bitcast(F32)[:, 512 * i:512 * (i + 1)] for i in range(2)]

    pes = p.sem("pes")
    ace = p.sem("ace")
    dvm = p.sem("dvm")
    pev = p.sem("pev")
    aev = p.sem("aev")
    dvn = p.sem("dvn")
    dvy = p.sem("dvy")

    for hb in range(4):
        th0 = 512 * hb
        aev0 = p.val(aev)
        for sg in range(12):
            wb = sg % 2
            p.dma("gpsimd", s_w[wb][:],
                  wqkv[:, 512 * sg:512 * (sg + 1)].rearrange("(ic p) o -> p ic o", p=128),
                  waits=[(aev, aev0 + 4 * (sg - 1))] if sg >= 2 else [], inc=f"ldw{wb}")
            ldv = p.val(f"ldw{wb}")
            for u in range(4):
                bank = pb[4 + u]
                for ic in range(16):
                    if sg < 8:
                        fn = (lambda e, wb=wb, ic=ic, u=u, th0=th0, bank=bank: e.matmul(
                            bank[:], s_w[wb][:, 512 * ic + 128 * u:512 * ic + 128 * (u + 1)],
                            xlt[:, 2048 * ic + th0:2048 * ic + th0 + 512],
                            start=(ic == 0), stop=(ic == 15)))
                    else:
                        fn = (lambda e, wb=wb, ic=ic, u=u, th0=th0, bank=bank: e.matmul(
                            bank[:], xlt[:, 2048 * ic + th0 + 128 * u:2048 * ic + th0 + 128 * (u + 1)],
                            s_w[wb][:, 512 * ic:512 * (ic + 1)],
                            start=(ic == 0), stop=(ic == 15)))
                    p.emit("tensor", fn,
                           waits=([(f"ldw{wb}", ldv),
                                   (aev, aev0 + 4 * (sg - 1) + u + 1 if sg >= 1 else 0)]
                                  if ic == 0 else []),
                           inc=("pes" if ic == 15 else None))
                if sg < 8:
                    dst, off = (s_q, (sg % 4) * 4 + u) if sg < 4 else (s_k, ((sg - 4) % 4) * 4 + u)
                    dap = dst[:, 512 * off:512 * (off + 1)]
                else:
                    dap = s_v[:, 2048 * u + 512 * (sg - 8):2048 * u + 512 * (sg - 8) + 512]
                p.emit("scalar", lambda e, dap=dap, bank=bank: e.activation(dap, bank[:], AF.Copy),
                       waits=[(pes, p.val(pes))], inc="aev")
        p.barrier()

        sB = [pb[0], pb[1], pb[5], pb[6]]
        LAG = 3
        for j in range(16):
            mask_of = {}
            pv_of = {}

            def emit_pv(c, j=j, mask_of=mask_of, pv_of=pv_of):
                Tp, jp = c // 16, c % 16
                col0 = 128 * Tp
                ptb = s_pt[c % 4]
                p.emit("tensor", lambda e, c=c, Tp=Tp, jp=jp, ptb=ptb, col0=col0: e.matmul(
                    pb[2][:, col0:512],
                    s_v[:, 2048 * Tp + 128 * jp:2048 * Tp + 128 * (jp + 1)],
                    ptb[:, col0:512], start=(c == 0), stop=(c == 63),
                    skip_group_check=True),
                    waits=[(dvm, mask_of[c])] + ([(dvn, p.val(dvn))] if c == 0 else []),
                    inc="pev")
                p.emit("tensor", lambda e, c=c, ptb=ptb, col0=col0: e.matmul(
                    pb[3][0:1, col0:512], env["ones_col"][:], ptb[:, col0:512],
                    start=(c == 0), stop=(c == 63), skip_group_check=True),
                    inc="pev")
                pv_of[c] = p.val(pev)

            for c in range(64):
                Tp, jp = c // 16, c % 16
                col0 = 128 * Tp
                sbank = sB[c % 4]
                ptb = s_pt[c % 4]
                p.emit("tensor", lambda e, j=j, Tp=Tp, jp=jp, col0=col0, sbank=sbank: e.matmul(
                    sbank[:, col0:512],
                    s_k[:, 512 * jp + 128 * Tp:512 * jp + 128 * (Tp + 1)],
                    s_q[:, 512 * j + col0:512 * j + 512],
                    start=True, stop=True, skip_group_check=True),
                    waits=[(ace, p.val(ace) - 4), (dvn, p.val(dvn))] if c <= 3 else
                          [(ace, p.val(ace) - 4)],
                    inc="pes")
                qk_done = p.val(pes)
                p.emit("scalar", lambda e, ptb=ptb, col0=col0, sbank=sbank: e.activation(
                    ptb[:, col0:512], sbank[:, col0:512], AF.Exp, scale=INV_SQRT_HD),
                    waits=[(pes, qk_done), (pev, pv_of.get(c - 4, 0))], inc="ace")
                mask = env["m_incl"] if jp <= j else env["m_strict"]
                p.emit("vector", lambda e, ptb=ptb, col0=col0, mask=mask: e.tensor_mul(
                    ptb[:, col0:col0 + 128], ptb[:, col0:col0 + 128], mask[:]),
                    waits=[(ace, p.val(ace))], inc="dvm")
                mask_of[c] = p.val(dvm)
                if c >= LAG:
                    emit_pv(c - LAG)
            for c in range(64 - LAG, 64):
                emit_pv(c)
            p.emit("vector", lambda e: e.reciprocal(s_rs[:], pb[3][0:1, :]),
                   waits=[(pev, p.val(pev))], inc="dvn")
            p.emit("tensor", lambda e: e.matmul(pb[4][:], env["ones_row"][:], s_rs[:],
                                                start=True, stop=True),
                   waits=[(dvn, p.val(dvn)), (aev, p.val(aev))], inc="pes")
            p.emit("scalar", lambda e: e.activation(s_rsb[:], pb[4][:], AF.Copy),
                   waits=[(pes, p.val(pes))], inc="aev")
            p.emit("vector", lambda e, j=j: e.tensor_mul(
                s_o[:, 512 * j:512 * (j + 1)], pb[2][:], s_rsb[:]),
                   waits=[(aev, p.val(aev))], inc="dvn")
        p.barrier()

        dvy0 = p.val(dvy)
        for og in range(4):
            wb = og % 2
            p.dma("gpsimd", s_w[wb][:],
                  wout[:, 512 * og:512 * (og + 1)].rearrange("(ic p) o -> p ic o", p=128),
                  waits=[(dvy, dvy0 + 4 * (og - 1))] if og >= 2 else [], inc=f"ldw{wb}")
            ldv = p.val(f"ldw{wb}")
            for tc in range(4):
                i = (og * 4 + tc) % 2
                bank = pb[4 + tc % 4]
                p.dma("sync", s_xcs[i],
                      xin[th0 + 128 * tc:th0 + 128 * (tc + 1), 512 * og:512 * (og + 1)],
                      waits=[(dvy, p.val(dvy) - 1)], inc=f"ldx{i}")
                ldxv = p.val(f"ldx{i}")
                for ec in range(16):
                    p.emit("tensor", lambda e, wb=wb, ec=ec, tc=tc, bank=bank: e.matmul(
                        bank[:], s_o[:, 512 * ec + 128 * tc:512 * ec + 128 * (tc + 1)],
                        s_w[wb][:, 512 * ec:512 * (ec + 1)],
                        start=(ec == 0), stop=(ec == 15)),
                        waits=([(f"ldw{wb}", ldv), (dvy, p.val(dvy))] if ec == 0 else []),
                        inc=("pes" if ec == 15 else None))
                p.emit("vector", lambda e, i=i, bank=bank: e.scalar_tensor_tensor(
                    s_ys[i], bank[:], 1.0, s_xcs[i], ALU.mult, ALU.add),
                    waits=[(pes, p.val(pes)), (f"ldx{i}", ldxv),
                           (f"sty{i}", p.val(f"sty{i}"))],
                    inc="dvy")
                p.dma("sync", x1[th0 + 128 * tc:th0 + 128 * (tc + 1), 512 * og:512 * (og + 1)],
                      s_ys[i], waits=[(dvy, p.val(dvy))], inc=f"sty{i}")
        p.barrier()

    return [("sty0", p.val("sty0")), ("sty1", p.val("sty1"))]


def emit_dec_fp4(p, env, u16v, nC2, s_W, scT_bcast, wv_maker, waits_first, inc_last,
                 dec):
    k_, sh_, b_, t_, vp_, v_ = (dec[n] for n in ("k", "sh", "b", "t", "vp", "v"))
    for jj in range(4):
        w0 = waits_first if jj == 0 else ()
        if jj == 0:
            p.emit("vector", lambda e: e.tensor_scalar(
                k_[:, :nC2], u16v, 7, None, ALU.bitwise_and), waits=w0)
        else:
            p.emit("vector", lambda e, jj=jj: e.tensor_scalar(
                k_[:, :nC2], u16v, 4 * jj, 7, ALU.logical_shift_right, ALU.bitwise_and))
        if jj == 3:
            p.emit("vector", lambda e: e.tensor_scalar(
                sh_[:, :nC2], u16v, 0x8000, None, ALU.bitwise_and))
        else:
            p.emit("vector", lambda e, jj=jj: e.tensor_scalar(
                sh_[:, :nC2], u16v, 12 - 4 * jj, 0x8000,
                ALU.logical_shift_left, ALU.bitwise_and))
        p.emit("vector", lambda e: e.tensor_scalar(
            b_[:, :nC2], k_[:, :nC2], 64.0, 16128.0, ALU.mult, ALU.add))
        p.emit("vector", lambda e: e.tensor_scalar(
            t_[:, :nC2], b_[:, :nC2].bitcast(BF16), 2.0, -1.0, ALU.mult, ALU.add))
        p.emit("vector", lambda e: e.tensor_tensor(
            vp_[:, :nC2], t_[:, :nC2], b_[:, :nC2].bitcast(BF16), ALU.min))
        p.emit("vector", lambda e: e.tensor_tensor(
            v_[:, :nC2].bitcast(U16), vp_[:, :nC2].bitcast(U16), sh_[:, :nC2],
            ALU.bitwise_xor))
        wv, ng = wv_maker(jj)
        vv = v_[:, :nC2].rearrange("p (g k) -> p g k", g=ng)
        p.emit("vector", lambda e, wv=wv, vv=vv: e.tensor_tensor(wv, vv, scT_bcast, ALU.mult),
               inc=(inc_last if jj == 3 else None))


def emit_scale_T(p, env, dram_slice, s_h, s_b16, pbank, s_out, nrows, waits, inc):
    ident = env["ident"]
    p.dma("sync", s_h[0:nrows, :], dram_slice,
          waits=list(waits) + [("pesc", p.val("pesc"))], inc="ldsc")
    v = p.val("ldsc")
    p.emit("vector", lambda e: e.tensor_copy(s_b16[0:nrows, :], s_h[0:nrows, :]),
           waits=[("ldsc", v)], inc="dvsc")
    p.emit("tensor", lambda e: e.transpose(pbank.bitcast(BF16)[:, 0:nrows],
                                           s_b16[0:nrows, :], ident[0:nrows, 0:nrows]),
           waits=[("dvsc", p.val("dvsc")), (inc, p.val(inc))], inc="pesc")
    p.emit("vector", lambda e: e.tensor_copy(s_out[:, 0:nrows], pbank.bitcast(BF16)[:, 0:nrows]),
           waits=[("pesc", p.val("pesc"))], inc=inc)


def emit_k2(p, env):
    nc = env["nc"]; sb = env["sb"]; pb = env["pb"]; io = env["io"]
    x1, gup, gus, dp, ds, x2, wdT0, yd = (io["x1"], io["gup"], io["gus"], io["dp"],
                                          io["ds"], io["x2"], io["wdT0"], io["yd"])
    wdT = wdT0
    for s in ("ldsc", "dvsc", "pesc"):
        p.sem(s)
    xlt = sb("xlt", [128, 16 * T], BF16)
    emit_ln(p, env, x1, xlt, "ln")
    p.emit("tensor", lambda e: e.nop(), waits=[("lndev", p.val("lndev"))])

    dec = {n: sb(f"dec_{n}", [128, 1408], U16 if n in ("k", "sh", "b") else BF16)
           for n in ("k", "sh", "b", "t", "vp", "v")}
    s_pk = [sb(f"s_pk{i}", [128, 704], I32) for i in range(2)]
    s_sch = sb("s_sch", [44, 128], F16)
    s_scb = sb("s_scb", [44, 128], BF16)
    s_scT = [sb(f"s_scT{i}", [128, 44], F32) for i in range(2)]
    s_W = [sb(f"s_W{i}", [128, 5632], BF16) for i in range(2)]
    s_wT = [sb(f"s_wT{i}", [128, 2048], BF16) for i in range(2)]
    s_st = [sb(f"s_st{i}", [128, 512], BF16) for i in range(2)]
    s_h = [sb(f"s_h{i}", [128, 2048], BF16) for i in range(4)]
    s_sg = sb("s_sg", [128, 2048], F32)
    s_wd = [sb(f"s_wd{i}", [128, 2048], BF16) for i in range(4)]
    s_yst = [sb(f"s_yst{i}", [128, 512], F32) for i in range(2)]
    dvd = p.sem("dvd")
    pet = p.sem("pet")
    dve = p.sem("dve")
    pem = p.sem("pem")
    ach = p.sem("ach")
    dvh = p.sem("dvh")
    ldd = p.sem("ldd")
    pdn = p.sem("pdn")
    dvo = p.sem("dvo")
    p.barrier()

    for f in range(16):
        fb = f % 2
        p.dma("sync", s_pk[fb][:], dp[128 * f:128 * (f + 1), :],
              waits=[(dvd, p.val(dvd) - 1)], inc=f"ldp{fb}")
        ldv = p.val(f"ldp{fb}")
        emit_scale_T(p, env, ds[:, 128 * f:128 * (f + 1)], s_sch, s_scb, pb[1],
                     s_scT[fb], 44, [(dvd, p.val(dvd) - 1)], "dvsc2" if False else "dvscT")
        scv = p.val("dvscT")
        scb = s_scT[fb][:, 0:44].unsqueeze(2).broadcast_to([128, 44, 32])

        def wv_mk(jj, fb=fb):
            return s_W[fb][:, jj::4].rearrange("p (g k) -> p g k", g=44), 44
        emit_dec_fp4(p, env, s_pk[fb][:].bitcast(U16), 1408, s_W[fb], scb, wv_mk,
                     [(f"ldp{fb}", ldv), ("dvscT", scv), (pet, p.val(pet))], "dvd", dec)
        dcv = p.val(dvd)
        for g in range(11):
            for k in range(4):
                c = 4 * g + k
                p.emit("tensor", lambda e, fb=fb, c=c, k=k: e.transpose(
                    pb[0].bitcast(BF16)[:, 128 * k:128 * (k + 1)],
                    s_W[fb][:, 128 * c:128 * (c + 1)], env["ident"][:]),
                    waits=([(dvd, dcv), (dve, p.val(dve))] if k == 0 else []),
                    inc=("pet" if k == 3 else None))
            sb_i = (f * 11 + g) % 2
            p.emit("vector", lambda e, sb_i=sb_i: e.tensor_copy(
                s_st[sb_i][:], pb[0].bitcast(BF16)[:, 0:512]),
                waits=[(pet, p.val(pet)), (f"stw{sb_i}", p.val(f"stw{sb_i}"))], inc="dve")
            dst = wdT[512 * g:512 * (g + 1), 128 * f:128 * (f + 1)].rearrange(
                "(t p) f -> p t f", p=128)
            src = s_st[sb_i][:].rearrange("p (t f) -> p t f", t=4)
            p.dma("scalar", dst, src, waits=[(dve, p.val(dve))], inc=f"stw{sb_i}")
    p.barrier()

    s_gsh = sb("s_gsh", [16, 128], F16)
    s_gsb = sb("s_gsb", [16, 128], BF16)
    s_gsT = [sb(f"s_gsT{i}", [128, 16], F32) for i in range(2)]
    stw_totals = [p.val("stw0"), p.val("stw1")]
    sta_grp = {}
    for g in range(11):
        for ci in range(4):
            c = 4 * g + ci
            for half, row0 in ((0, 128 * c), (1, 5632 + 128 * c)):
                hb2 = half
                p.dma("sync", s_pk[hb2][:, 0:256], gup[row0:row0 + 128, :],
                      waits=[(dvd, p.val(dvd) - 1)], inc=f"ldp{hb2}")
                ldv = p.val(f"ldp{hb2}")
                emit_scale_T(p, env, gus[:, row0:row0 + 128], s_gsh, s_gsb, pb[1],
                             s_gsT[hb2], 16, [("dvscT", p.val("dvscT") - 1)], "dvscT")
                scv = p.val("dvscT")
                scb = s_gsT[hb2][:, 0:16].unsqueeze(2).broadcast_to([128, 16, 32])

                def wv_mk(jj, hb2=hb2):
                    return s_W[hb2][:, jj::4][:, 0:512].rearrange(
                        "p (g k) -> p g k", g=16), 16
                emit_dec_fp4(p, env, s_pk[hb2][:, 0:256].bitcast(U16), 512,
                             s_W[hb2], scb, wv_mk,
                             [(f"ldp{hb2}", ldv), ("dvscT", scv), (pet, p.val(pet))], "dvd", dec)
                dcv = p.val(dvd)
                for grp in range(4):
                    for k in range(4):
                        ic = 4 * grp + k
                        p.emit("tensor", lambda e, hb2=hb2, ic=ic, k=k: e.transpose(
                            pb[0].bitcast(BF16)[:, 128 * k:128 * (k + 1)],
                            s_W[hb2][:, 128 * ic:128 * (ic + 1)], env["ident"][:]),
                            waits=([(dvd, dcv), (dve, p.val(dve))] if k == 0 else []),
                            inc=("pet" if k == 3 else None))
                    p.emit("vector", lambda e, hb2=hb2, grp=grp: e.tensor_copy(
                        s_wT[hb2][:].rearrange("p (ic q) -> p ic q", ic=16)[:, 4 * grp:4 * grp + 4, :],
                        pb[0].bitcast(BF16)[:, 0:512].rearrange("p (t q) -> p t q", t=4)),
                        waits=[(pet, p.val(pet)), (pem, p.val(pem) - 4)], inc="dve")
                devv = p.val(dve)
                bank = pb[4 + half]
                for tg in range(4):
                    cons = ("dvg", p.val("dvg")) if half == 0 else (dvh, p.val(dvh))
                    for ic in range(16):
                        p.emit("tensor", lambda e, hb2=hb2, ic=ic, tg=tg, bank=bank: e.matmul(
                            bank[:], s_wT[hb2][:, 128 * ic:128 * (ic + 1)],
                            xlt[:, 2048 * ic + 512 * tg:2048 * ic + 512 * (tg + 1)],
                            start=(ic == 0), stop=(ic == 15)),
                            waits=([(dve, devv), cons] if ic == 0 else []),
                            inc=("pem" if ic == 15 else None))
                    if half == 0:
                        p.emit("vector", lambda e, tg=tg: e.tensor_copy(
                            s_sg[:, 512 * tg:512 * (tg + 1)], pb[4][:]),
                            waits=[(pem, p.val(pem)), (dvh, p.val(dvh) - 3 + tg)],
                            inc="dvg")
                        p.emit("scalar", lambda e, tg=tg: e.activation(
                            s_sg[:, 512 * tg:512 * (tg + 1)],
                            s_sg[:, 512 * tg:512 * (tg + 1)], AF.Silu),
                            waits=[("dvg", p.val("dvg"))], inc="ach")
                    else:
                        p.emit("vector", lambda e, ci=ci, tg=tg: e.tensor_mul(
                            s_h[ci][:, 512 * tg:512 * (tg + 1)],
                            s_sg[:, 512 * tg:512 * (tg + 1)], pb[5][:]),
                            waits=[(pem, p.val(pem)), (ach, p.val(ach)),
                                   (pdn, p.val(pdn))],
                            inc="dvh")
        h_done = p.val(dvh)
        for ci in range(4):
            c = 4 * g + ci
            p.dma("gpsimd", s_wd[ci][:], wdT[128 * c:128 * (c + 1), :],
                  waits=[("stw0", stw_totals[0]), ("stw1", stw_totals[1]),
                         (pdn, p.val(pdn))], inc="ldd")
        lddv = p.val(ldd)
        for tc in range(16):
            for fg in range(4):
                idx = tc * 4 + fg
                bank = pb[6 + idx % 2]
                for ci in range(4):
                    p.emit("tensor", lambda e, ci=ci, tc=tc, fg=fg, bank=bank: e.matmul(
                        bank[:], s_h[ci][:, 128 * tc:128 * (tc + 1)],
                        s_wd[ci][:, 512 * fg:512 * (fg + 1)],
                        start=(ci == 0), stop=(ci == 3)),
                        waits=([(ldd, lddv), (dvh, h_done), (dvo, p.val(dvo) - 1)]
                               if ci == 0 else []),
                        inc=("pdn" if ci == 3 else None))
                yb = idx % 2
                p.emit("vector", lambda e, yb=yb, bank=bank: e.tensor_copy(
                    s_yst[yb][:], bank[:]),
                    waits=[(pdn, p.val(pdn)), (f"sta{yb}", p.val(f"sta{yb}"))], inc="dvo")
                gw = [(f"sta{yb}", sta_grp[g - 1][yb])] if g > 0 else []
                p.dma("gpsimd", yd[128 * tc:128 * (tc + 1), 512 * fg:512 * (fg + 1)],
                      s_yst[yb][:],
                      waits=[(dvo, p.val(dvo))] + gw, inc=f"sta{yb}",
                      accum_op=(ALU.add if g > 0 else ALU.bypass))
        sta_grp[g] = (p.val("sta0"), p.val("sta1"))
    p.barrier()

    s_yc = [env["ln_xc"][i][:] for i in range(2)]
    s_xc3 = [s_W[i].bitcast(F32)[:, 0:2048] for i in range(2)]
    s_sq2 = env["ln_sq"]
    xltf = xlt[:].bitcast(F32)
    s_o2 = [xltf[:, 2048 * i:2048 * (i + 1)] for i in range(2)]
    smz = {k: sb(f"fz_{k}", [128, 1], F32) for k in ("ssq", "t2e", "lnv", "rstd")}
    sta_totals = [p.val("sta0"), p.val("sta1")]
    for tc in range(16):
        b2 = tc % 2
        p.dma("sync", s_yc[b2], yd[128 * tc:128 * (tc + 1), :],
              waits=[("fch", p.val("fch") - 5), ("sta0", sta_totals[0]),
                     ("sta1", sta_totals[1])], inc=f"ldy{b2}")
        p.dma("sync", s_xc3[b2], x1[128 * tc:128 * (tc + 1), :],
              waits=[("fch", p.val("fch") - 5)], inc=f"ldy{b2}")
        ldv = p.val(f"ldy{b2}")
        p.emit("scalar", lambda e, b2=b2: e.activation(s_sq2[:], s_yc[b2], AF.Square,
                                                       accum_out=smz["ssq"][:]),
               waits=[(f"ldy{b2}", ldv), ("fch", p.val("fch"))], inc="fch")
        p.emit("vector", lambda e: e.tensor_scalar(smz["t2e"][:], smz["ssq"][:],
                                                   1.0 / D, 1e-6, ALU.mult, ALU.add),
               waits=[("fch", p.val("fch"))], inc="fch")
        p.emit("scalar", lambda e: e.activation(smz["lnv"][:], smz["t2e"][:], AF.Ln),
               waits=[("fch", p.val("fch"))], inc="fch")
        p.emit("scalar", lambda e: e.activation(smz["rstd"][:], smz["lnv"][:], AF.Exp,
                                                scale=-0.5),
               waits=[("fch", p.val("fch"))], inc="fch")
        p.emit("vector", lambda e, b2=b2: e.scalar_tensor_tensor(
            s_o2[b2], s_yc[b2], smz["rstd"][:], s_xc3[b2], ALU.mult, ALU.add),
            waits=[("fch", p.val("fch")), (f"stz{b2}", p.val(f"stz{b2}"))], inc="fch")
        p.dma("sync", x2[128 * tc:128 * (tc + 1), :], s_o2[b2],
              waits=[("fch", p.val("fch"))], inc=f"stz{b2}")

    return [("stz0", p.val("stz0")), ("stz1", p.val("stz1"))]


def common_identf(p, env):
    sb = env["sb"]
    identf = sb("c_identf", [128, 128], F32)
    env["identf"] = identf
    p.emit("gpsimd", lambda e: e.memset(identf[:], 0.0),
           waits=[("init", p.val("init"))], inc="init")
    p.emit("gpsimd", lambda e: e.affine_select(
        out=identf[:], in_=identf[:], compare_op=ALU.not_equal, fill=1.0,
        base=0, pattern=[[-1, 128]], channel_multiplier=1),
        waits=[("init", p.val("init"))], inc="init")


def emit_bias_T(p, env, bvec, s_bh, s_out, tag):
    pbk = env["pb"][1]
    p.dma("sync", s_bh[:], bvec.rearrange("(c p) -> c p", p=128),
          waits=[(tag + "bt", p.val(tag + "bt"))], inc=tag + "bl")
    p.emit("tensor", lambda e: e.transpose(pbk[:, 0:16], s_bh[:], env["identf"][0:16, 0:16]),
           waits=[(tag + "bl", p.val(tag + "bl")), (tag + "bt", p.val(tag + "bt"))],
           inc=tag + "bp")
    p.emit("vector", lambda e: e.tensor_copy(s_out[:], pbk[:, 0:16]),
           waits=[(tag + "bp", p.val(tag + "bp"))], inc=tag + "bt")


def emit_rope_tables(p, env, pos, s_sinf, s_cosf):
    sb = env["sb"]
    pb = env["pb"]
    s_dr = sb("rp_dr", [1, 128], I32)
    s_jr = sb("rp_jr", [1, 128], I32)
    s_jf = sb("rp_jf", [1, 128], F32)
    s_if = sb("rp_if", [1, 128], F32)
    s_pos = sb("rp_pos", [1, T], F32)
    s_u = sb("rp_u", [128, 512], F32)
    s_i = sb("rp_i", [128, 512], I32)
    s_f = sb("rp_f", [128, 512], F32)
    s_f2 = sb("rp_f2", [128, 512], F32)
    rc = p.sem("rc")

    def chain(eng, fn, extra=()):
        p.emit(eng, fn, waits=[("rc", p.val("rc"))] + list(extra), inc="rc")

    p.dma("sync", s_pos[:], pos, inc="rl")
    rl = p.val("rl")
    chain("gpsimd", lambda e: e.iota(s_dr[:], pattern=[[1, 128]], base=0, channel_multiplier=0))
    chain("vector", lambda e: e.tensor_scalar(s_jr[:], s_dr[:], 1, None, ALU.logical_shift_right))
    chain("vector", lambda e: e.tensor_copy(s_jf[:], s_jr[:]))
    LN1E4 = float(np.log(10000.0))
    chain("scalar", lambda e: e.activation(s_if[:], s_jf[:], AF.Exp, scale=-2.0 * LN1E4 / 128.0))
    INV2PI = float(1.0 / (2.0 * np.pi))
    TWOPI = float(2.0 * np.pi)
    for tg in range(4):
        chain("tensor", lambda e, tg=tg: e.matmul(
            pb[2][:], s_if[:], s_pos[:, 512 * tg:512 * (tg + 1)], start=True, stop=True),
            extra=[("rl", rl)])
        for phase, dst in ((0.0, s_sinf), (0.25, s_cosf)):
            chain("vector", lambda e, ph=phase: e.tensor_scalar(
                s_u[:], pb[2][:], INV2PI, ph, ALU.mult, ALU.add))
            chain("vector", lambda e: e.tensor_copy(s_i[:], s_u[:]))
            chain("vector", lambda e: e.tensor_copy(s_f[:], s_i[:]))
            chain("vector", lambda e: e.tensor_sub(s_f2[:], s_u[:], s_f[:]))
            chain("scalar", lambda e, dst=dst, tg=tg: e.activation(
                dst[:, 512 * tg:512 * (tg + 1)], s_f2[:], AF.Sin, scale=TWOPI))


def emit_psw(p, env, s_psw):
    sb = env["sb"]
    s_i1 = sb("pw_i1", [128, 128], F32)
    s_c1 = sb("pw_c1", [128, 128], F32)
    s_c2 = sb("pw_c2", [128, 128], F32)
    s_pi = sb("pw_pi", [128, 1], I32)
    s_pk = sb("pw_pk", [128, 1], I32)
    s_pkf = sb("pw_pkf", [128, 1], F32)
    s_ev = sb("pw_ev", [128, 1], F32)

    def chain(eng, fn):
        p.emit(eng, fn, waits=[("rc", p.val("rc"))], inc="rc")
    chain("gpsimd", lambda e: e.iota(s_i1[:], pattern=[[1, 128]], base=0, channel_multiplier=-1, allow_small_or_imprecise_dtypes=True))
    chain("vector", lambda e: e.tensor_single_scalar(s_c1[:], s_i1[:], 1.0, ALU.is_equal))
    chain("vector", lambda e: e.tensor_single_scalar(s_c2[:], s_i1[:], -1.0, ALU.is_equal))
    chain("gpsimd", lambda e: e.iota(s_pi[:], pattern=[[0, 1]], base=0, channel_multiplier=1))
    chain("vector", lambda e: e.tensor_single_scalar(s_pk[:], s_pi[:], 1, ALU.bitwise_and))
    chain("vector", lambda e: e.tensor_copy(s_pkf[:], s_pk[:]))
    chain("vector", lambda e: e.tensor_scalar(s_ev[:], s_pkf[:], -1.0, 1.0, ALU.mult, ALU.add))
    chain("vector", lambda e: e.tensor_scalar(s_c1[:], s_c1[:], s_ev[:], None, ALU.mult))
    chain("vector", lambda e: e.tensor_scalar(s_c2[:], s_c2[:], s_pkf[:], None, ALU.mult))
    chain("vector", lambda e: e.tensor_sub(s_psw[:], s_c1[:], s_c2[:]))


def emit_k3a(p, env):
    nc = env["nc"]; sb = env["sb"]; pb = env["pb"]; io = env["io"]
    x2, wq, wk, wv = io["x2"], io["wq"], io["wk"], io["wv"]
    bq, bk, bv, pos = io["bq"], io["bk"], io["bv"], io["pos"]
    qT, kT, vv = io["qT"], io["kT"], io["vv"]
    xlt = sb("xlt", [128, 16 * T], BF16)
    emit_ln(p, env, x2, xlt, "ln")
    p.emit("tensor", lambda e: e.nop(), waits=[("lndev", p.val("lndev"))])
    s_sinf = sb("s_sinf", [128, T], F32)
    s_cosf = sb("s_cosf", [128, T], F32)
    emit_rope_tables(p, env, pos, s_sinf, s_cosf)
    s_psw = sb("s_psw", [128, 128], F32)
    emit_psw(p, env, s_psw)
    s_pswb = sb("s_pswb", [128, 128], BF16)
    p.emit("vector", lambda e: e.tensor_copy(s_pswb[:], s_psw[:]),
           waits=[("rc", p.val("rc"))], inc="rc")
    s_bqT = sb("s_bqT", [128, 16], F32)
    s_bkT = sb("s_bkT", [128, 16], F32)
    s_bvR = sb("s_bvR", [1, D], F32)
    s_bvB = sb("s_bvB", [128, D], F32)
    s_bh = sb("s_bh", [16, 128], F32)
    emit_bias_T(p, env, bq, s_bh, s_bqT, "q")
    emit_bias_T(p, env, bk, s_bh, s_bkT, "k")
    p.dma("sync", s_bvR[:], bv.rearrange("(a d) -> a d", a=1), inc="rlb")
    rlv = p.val("rlb")
    for tg in range(4):
        p.emit("tensor", lambda e, tg=tg: e.matmul(
            pb[3][:], env["ones_row"][:], s_bvR[:, 512 * tg:512 * (tg + 1)],
            start=True, stop=True),
            waits=[("rlb", rlv), ("rc", p.val("rc"))], inc="rc")
        p.emit("vector", lambda e, tg=tg: e.tensor_copy(
            s_bvB[:, 512 * tg:512 * (tg + 1)], pb[3][:]),
            waits=[("rc", p.val("rc"))], inc="rc")
    p.barrier()

    s_w = [sb(f"wslab{i}", [128, 16 * 512], BF16) for i in range(2)]
    s_qs = [sb(f"s_qs{i}", [128, 512], BF16) for i in range(2)]
    s_m1 = sb("s_m1", [128, 512], F32)
    s_m2 = sb("s_m2", [128, 512], F32)
    s_qo = [sb(f"s_qo{i}", [128, 512], BF16) for i in range(2)]
    pes = p.sem("pes")
    aev = p.sem("aev")
    per = p.sem("per")
    dvr = p.sem("dvr")

    for mat, (wmat, bT, dst) in enumerate(((wq, s_bqT, qT), (wk, s_bkT, kT))):
        for og in range(4):
            wb = og % 2
            p.dma("gpsimd", s_w[wb][:],
                  wmat[:, 512 * og:512 * (og + 1)].rearrange("(ic p) o -> p ic o", p=128),
                  waits=[(pes, p.val(pes) - 16)], inc=f"ldw{wb}")
            ldv = p.val(f"ldw{wb}")
            for oc in range(4):
                h = 4 * og + oc
                for tg in range(4):
                    i = (h * 4 + tg) % 2
                    bank = pb[4 + tg % 2]
                    for ic in range(16):
                        p.emit("tensor", lambda e, wb=wb, ic=ic, oc=oc, tg=tg, bank=bank: e.matmul(
                            bank[:], s_w[wb][:, 512 * ic + 128 * oc:512 * ic + 128 * (oc + 1)],
                            xlt[:, 2048 * ic + 512 * tg:2048 * ic + 512 * (tg + 1)],
                            start=(ic == 0), stop=(ic == 15)),
                            waits=([(f"ldw{wb}", ldv), (aev, p.val(aev) - 1)] if ic == 0 else []),
                            inc=("pes" if ic == 15 else None))
                    p.emit("scalar", lambda e, i=i, h=h, bank=bank, bT=bT: e.activation(
                        s_qs[i][:], bank[:], AF.Identity, bias=bT[:, h:h + 1]),
                        waits=[(pes, p.val(pes)), (per, p.val(per) - 1),
                               (dvr, p.val(dvr) - 3)], inc="aev")
                    p.emit("tensor", lambda e, i=i: e.matmul(
                        pb[6][:], s_pswb[:], s_qs[i][:], start=True, stop=True),
                        waits=[(aev, p.val(aev)), (dvr, p.val(dvr) - 1)], inc="per")
                    p.emit("vector", lambda e, i=i, tg=tg: e.tensor_mul(
                        s_m1[:], s_qs[i][:], s_cosf[:, 512 * tg:512 * (tg + 1)]),
                        waits=[(aev, p.val(aev))])
                    p.emit("vector", lambda e, tg=tg: e.tensor_mul(
                        s_m2[:], pb[6][:], s_sinf[:, 512 * tg:512 * (tg + 1)]),
                        waits=[(per, p.val(per))], inc="dvr")
                    p.emit("vector", lambda e, i=i: e.tensor_add(s_qo[i][:], s_m1[:], s_m2[:]),
                           waits=[(f"stq{i}", p.val(f"stq{i}"))], inc="dvr")
                    p.dma("scalar", dst[128 * h:128 * (h + 1), 512 * tg:512 * (tg + 1)],
                          s_qo[i][:], waits=[(dvr, p.val(dvr))], inc=f"stq{i}")
    p.barrier()

    s_vo = [sb(f"s_vo{i}", [128, 512], BF16) for i in range(2)]
    for og in range(4):
        wb = og % 2
        p.dma("gpsimd", s_w[wb][:],
              wv[:, 512 * og:512 * (og + 1)].rearrange("(ic p) o -> p ic o", p=128),
              waits=[(pes, p.val(pes) - 16)], inc=f"ldw{wb}")
        ldv = p.val(f"ldw{wb}")
        for tc in range(16):
            i = (og * 16 + tc) % 2
            bank = pb[4 + tc % 2]
            for ic in range(16):
                p.emit("tensor", lambda e, wb=wb, ic=ic, tc=tc, bank=bank: e.matmul(
                    bank[:], xlt[:, 2048 * ic + 128 * tc:2048 * ic + 128 * (tc + 1)],
                    s_w[wb][:, 512 * ic:512 * (ic + 1)],
                    start=(ic == 0), stop=(ic == 15)),
                    waits=([(f"ldw{wb}", ldv), ("dvv", p.val("dvv") - 1)] if ic == 0 else []),
                    inc=("pes" if ic == 15 else None))
            p.emit("vector", lambda e, i=i, og=og, bank=bank: e.tensor_add(
                s_vo[i][:], bank[:], s_bvB[:, 512 * og:512 * (og + 1)]),
                waits=[(pes, p.val(pes)), (f"stq{i}", p.val(f"stq{i}"))], inc="dvv")
            p.dma("scalar", vv[128 * tc:128 * (tc + 1), 512 * og:512 * (og + 1)],
                  s_vo[i][:], waits=[("dvv", p.val("dvv"))], inc=f"stq{i}")

    return [("stq0", p.val("stq0")), ("stq1", p.val("stq1"))]


def emit_k3b(p, env):
    nc = env["nc"]; sb = env["sb"]; pb = env["pb"]; io = env["io"]
    qTd, kg, vg = io["qT2"], io["kg"], io["vg"]
    wo, bo, x2, x3 = io["wo"], io["bo"], io["x2b"], io["x3"]
    s_boR = sb("s_boR", [1, D], F32)
    s_boB = sb("s_boB", [128, D], F32)
    p.dma("sync", s_boR[:], bo.rearrange("(a d) -> a d", a=1), inc="rl")
    rlv = p.val("rl")
    for tg in range(4):
        p.emit("tensor", lambda e, tg=tg: e.matmul(
            pb[3][:], env["ones_row"][:], s_boR[:, 512 * tg:512 * (tg + 1)],
            start=True, stop=True),
            waits=[("rl", rlv), ("rc", p.val("rc"))], inc="rc")
        p.emit("vector", lambda e, tg=tg: e.tensor_copy(
            s_boB[:, 512 * tg:512 * (tg + 1)], pb[3][:]),
            waits=[("rc", p.val("rc"))], inc="rc")
    p.barrier()

    s_kh = [sb(f"s_kh{i}", [128, 4 * T], BF16) for i in range(2)]
    s_vh = [sb(f"s_vh{i}", [128, 4 * T], BF16) for i in range(2)]
    s_qt = [sb(f"s_qt{i}", [128, 512], BF16) for i in range(2)]
    s_pt = [sb(f"s_pt{i}", [128, 512], BF16) for i in range(4)]
    s_rs = sb("s_rs", [1, 512], F32)
    s_rsb = sb("s_rsb", [128, 512], F32)
    s_oall = sb("s_oall", [128, 16 * T], BF16)
    pes = p.sem("pes")
    ace = p.sem("ace")
    pev = p.sem("pev")
    aev = p.sem("aev")
    dvn = p.sem("dvn")

    for h in range(16):
        hb = h % 2
        for r in range(4):
            p.dma("gpsimd", s_kh[hb][:, 2048 * r:2048 * (r + 1)],
                  kg[2048 * r + 128 * h:2048 * r + 128 * (h + 1), :],
                  waits=[(pev, p.val(pev) - 512)] if h >= 2 else [], inc=f"ldk{hb}")
            p.dma("gpsimd", s_vh[hb][:, 2048 * r:2048 * (r + 1)],
                  vg[2048 * r:2048 * (r + 1), 128 * h:128 * (h + 1)].rearrange(
                      "(tc p) d -> p tc d", p=128),
                  inc=f"ldk{hb}")
        ldkv = p.val(f"ldk{hb}")
        sB = [pb[0], pb[1], pb[5], pb[6]]
        LAG = 3
        for tg in range(4):
            qb = (h * 4 + tg) % 2
            p.dma("sync", s_qt[qb][:], qTd[128 * h:128 * (h + 1), 512 * tg:512 * (tg + 1)],
                  waits=[(pes, p.val(pes) - 66)], inc=f"ldq{qb}")
            ldqv = p.val(f"ldq{qb}")
            exp_of = {}
            pv_of = {}

            def emit_pv(kc, hb=hb):
                ptb = s_pt[kc % 4]
                p.emit("tensor", lambda e, kc=kc, hb=hb, ptb=ptb: e.matmul(
                    pb[2][:], s_vh[hb][:, 128 * kc:128 * (kc + 1)], ptb[:],
                    start=(kc == 0), stop=(kc == 63), skip_group_check=True),
                    waits=[(ace, exp_of[kc])] + ([(dvn, p.val(dvn))] if kc == 0 else []),
                    inc="pev")
                p.emit("tensor", lambda e, kc=kc, ptb=ptb: e.matmul(
                    pb[3][0:1, :], env["ones_col"][:], ptb[:],
                    start=(kc == 0), stop=(kc == 63), skip_group_check=True),
                    inc="pev")
                pv_of[kc] = p.val(pev)

            for kc in range(64):
                sbank = sB[kc % 4]
                ptb = s_pt[kc % 4]
                p.emit("tensor", lambda e, kc=kc, hb=hb, qb=qb, sbank=sbank: e.matmul(
                    sbank[:], s_kh[hb][:, 128 * kc:128 * (kc + 1)], s_qt[qb][:],
                    start=True, stop=True, skip_group_check=True),
                    waits=([(f"ldk{hb}", ldkv), (f"ldq{qb}", ldqv), (ace, p.val(ace) - 1),
                            (dvn, p.val(dvn))] if kc <= 3 else
                           [(ace, exp_of.get(kc - 4, 0))]),
                    inc="pes")
                qk_done = p.val(pes)
                p.emit("scalar", lambda e, ptb=ptb, sbank=sbank: e.activation(
                    ptb[:], sbank[:], AF.Exp, scale=INV_SQRT_HD),
                    waits=[(pes, qk_done), (pev, pv_of.get(kc - 4, 0))], inc="ace")
                exp_of[kc] = p.val(ace)
                if kc >= LAG:
                    emit_pv(kc - LAG)
            for kc in range(64 - LAG, 64):
                emit_pv(kc)
            p.emit("vector", lambda e: e.reciprocal(s_rs[:], pb[3][0:1, :]),
                   waits=[(pev, p.val(pev))], inc="dvn")
            p.emit("tensor", lambda e: e.matmul(pb[4][:], env["ones_row"][:], s_rs[:],
                                                start=True, stop=True),
                   waits=[(dvn, p.val(dvn)), (aev, p.val(aev))], inc="pes")
            p.emit("scalar", lambda e: e.activation(s_rsb[:], pb[4][:], AF.Copy),
                   waits=[(pes, p.val(pes))], inc="aev")
            p.emit("vector", lambda e, h=h, tg=tg: e.tensor_mul(
                s_oall[:, 2048 * h + 512 * tg:2048 * h + 512 * (tg + 1)], pb[2][:], s_rsb[:]),
                   waits=[(aev, p.val(aev))], inc="dvn")
    p.barrier()

    s_w = [sb(f"wslab{i}", [128, 16 * 512], BF16) for i in range(2)]
    s_xcs = [sb(f"s_xcs{i}", [128, 512], F32) for i in range(2)]
    s_t1 = sb("s_t1", [128, 512], F32)
    s_ys = [sb(f"s_ys{i}", [128, 512], F32) for i in range(2)]
    dvy = p.sem("dvy")
    for og in range(4):
        wb = og % 2
        p.dma("gpsimd", s_w[wb][:],
              wo[:, 512 * og:512 * (og + 1)].rearrange("(ic p) o -> p ic o", p=128),
              waits=[(dvy, p.val(dvy) - 16)] if og >= 2 else [], inc=f"ldw{wb}")
        ldv = p.val(f"ldw{wb}")
        for tc in range(16):
            i = (og * 16 + tc) % 2
            bank = pb[4 + tc % 2]
            p.dma("sync", s_xcs[i][:],
                  x2[128 * tc:128 * (tc + 1), 512 * og:512 * (og + 1)],
                  waits=[(dvy, p.val(dvy) - 1)], inc=f"ldx{i}")
            ldxv = p.val(f"ldx{i}")
            for ec in range(16):
                p.emit("tensor", lambda e, wb=wb, ec=ec, tc=tc, bank=bank: e.matmul(
                    bank[:], s_oall[:, 2048 * ec + 128 * tc:2048 * ec + 128 * (tc + 1)],
                    s_w[wb][:, 512 * ec:512 * (ec + 1)],
                    start=(ec == 0), stop=(ec == 15)),
                    waits=([(f"ldw{wb}", ldv), (dvy, p.val(dvy))] if ec == 0 else []),
                    inc=("pes" if ec == 15 else None))
            p.emit("vector", lambda e, og=og, bank=bank: e.tensor_add(
                s_t1[:], bank[:], s_boB[:, 512 * og:512 * (og + 1)]),
                waits=[(pes, p.val(pes)), (f"ldx{i}", ldxv),
                       (f"sty{i}", p.val(f"sty{i}"))])
            p.emit("vector", lambda e, i=i: e.tensor_add(s_ys[i][:], s_t1[:], s_xcs[i][:]),
                   inc="dvy")
            p.dma("sync", x3[128 * tc:128 * (tc + 1), 512 * og:512 * (og + 1)],
                  s_ys[i][:], waits=[(dvy, p.val(dvy))], inc=f"sty{i}")

    return [("sty0", p.val("sty0")), ("sty1", p.val("sty1"))]


def emit_dec_nvfp4(p, env, u8v4, nC, s_W, scb, wv_maker, waits_first, inc_last, dec):
    u8c, u16, k_, sh_, b_, t_, vp_, v_ = (dec[n] for n in
                                          ("u8c", "u16", "k", "sh", "b", "t", "vp", "v"))
    p.emit("vector", lambda e: e.tensor_copy(u8c[:, :nC], u8v4[:, 0::4]), waits=waits_first)
    p.emit("vector", lambda e: e.tensor_copy(u16[:, :nC], u8c[:, :nC]))
    for (kshift, sshift), par in (((0, 12), 0), ((4, 8), 1)):
        if kshift == 0:
            p.emit("vector", lambda e: e.tensor_scalar(k_[:, :nC], u16[:, :nC], 7, None, ALU.bitwise_and))
        else:
            p.emit("vector", lambda e, ks=kshift: e.tensor_scalar(
                k_[:, :nC], u16[:, :nC], ks, 7, ALU.logical_shift_right, ALU.bitwise_and))
        p.emit("vector", lambda e, ss=sshift: e.tensor_scalar(
            sh_[:, :nC], u16[:, :nC], ss, 0x8000, ALU.logical_shift_left, ALU.bitwise_and))
        p.emit("vector", lambda e: e.tensor_scalar(b_[:, :nC], k_[:, :nC], 64.0, 16128.0, ALU.mult, ALU.add))
        p.emit("vector", lambda e: e.tensor_scalar(t_[:, :nC], b_[:, :nC].bitcast(BF16), 2.0, -1.0, ALU.mult, ALU.add))
        p.emit("vector", lambda e: e.tensor_tensor(vp_[:, :nC], t_[:, :nC], b_[:, :nC].bitcast(BF16), ALU.min))
        p.emit("vector", lambda e: e.tensor_tensor(v_[:, :nC].bitcast(U16), vp_[:, :nC].bitcast(U16), sh_[:, :nC], ALU.bitwise_xor))
        wv, ng = wv_maker(par)
        vv_ = v_[:, :nC].rearrange("p (g k) -> p g k", g=ng)
        p.emit("vector", lambda e, wv=wv, vv_=vv_: e.tensor_tensor(wv, vv_, scb, ALU.mult),
               inc=(inc_last if par == 1 else None))


def emit_k5(p, env):
    nc = env["nc"]; sb = env["sb"]; pb = env["pb"]; io = env["io"]
    x3, out, wdT = io["x3b"], io["out"], io["wdT1"]
    gWp, gs, uWp, us, dWp, dsc = (io["gWp"], io["gs"], io["uWp"], io["us"],
                                  io["dWp"], io["dsc"])
    xlt = sb("xlt", [128, 16 * T], BF16)
    emit_ln(p, env, x3, xlt, "ln")
    p.emit("tensor", lambda e: e.nop(), waits=[("lndev", p.val("lndev"))])
    p.dma("gpsimd", out, x3, inc="sti")
    sti_v = p.val("sti")

    dec = {}
    dec["u8c"] = sb("dec_u8c", [128, 1024], U8)
    for n_ in ("u16", "k", "sh", "b"):
        dec[n_] = sb(f"dec_{n_}", [128, 1024], U16)
    for n_ in ("t", "vp", "v"):
        dec[n_] = sb(f"dec_{n_}", [128, 1024], BF16)
    s_pk = [sb(f"s_pk{i}", [128, 1024], I32) for i in range(2)]
    s_sc = [sb(f"s_sc{i}", [128, 64], F32) for i in range(2)]
    s_W = [sb(f"s_W{i}", [128, 2048], BF16) for i in range(2)]
    s_wT = [sb(f"s_wT{i}", [128, 2048], BF16) for i in range(2)]
    s_gg = sb("s_gg", [128, 2048], BF16)
    s_yst = [sb(f"s_yst{i}", [128, 512], F32) for i in range(2)]
    pk5 = [sb("pk5_0", [128, 1024], I32)]
    sc5 = [sb("sc5_0", [128, 64], F32)]
    W5 = [sb("W5_0", [128, 2048], BF16)]
    st5 = [sb(f"st5_{i}", [128, 512], BF16) for i in range(2)]
    s_h = [sb(f"s_h{i}", [128, 2048], BF16) for i in range(12)]
    _xc = env["ln_xc"]; _sq = env["ln_sq"]; _tl = env["ln_tln"]
    s_wd = [
        _xc[0][:].bitcast(BF16)[:, 0:2048], _xc[0][:].bitcast(BF16)[:, 2048:4096],
        _xc[1][:].bitcast(BF16)[:, 0:2048], _xc[1][:].bitcast(BF16)[:, 2048:4096],
        _sq[:].bitcast(BF16)[:, 0:2048], _sq[:].bitcast(BF16)[:, 2048:4096],
        _tl[0][:], _tl[1][:],
    ]
    dvd = p.sem("dvd")
    pet = p.sem("pet")
    dve = p.sem("dve")
    ld5 = p.sem("ld5")
    dvd5 = p.sem("dvd5")
    pet5 = p.sem("pet5")
    pem = p.sem("pem")
    dvg = p.sem("dvg")
    ach = p.sem("ach")
    dvh = p.sem("dvh")
    ldd = p.sem("ldd")
    pdn = p.sem("pdn")
    dvo = p.sem("dvo")

    NCH = 256
    HRING = 12

    sa_stw = {}
    dcv5 = {}
    tb = {"j": 0, "ev": {-2: 0, -1: 0}}

    def tgrp_start_waits(deps):
        j = tb["j"]
        return deps + [("dve", tb["ev"][j - 2])]

    def tgrp_done(evv):
        tb["ev"][tb["j"]] = evv
        tb["j"] += 1

    def emit_5a_load(i):
        cc, f = divmod(i, 16)
        p.dma("sync", pk5[0][:], dWp[128 * f:128 * (f + 1), 1024 * cc:1024 * (cc + 1)],
              waits=[(dvd5, p.val(dvd5))], inc="ld5")
        p.dma("sync", sc5[0][:], dsc[128 * f:128 * (f + 1), 64 * cc:64 * (cc + 1)],
              inc="ld5")

    def emit_5a_dec(i):
        ldv = p.val(ld5)
        scb = sc5[0][:].unsqueeze(2).broadcast_to([128, 64, 16])

        def wv_mk(par):
            return W5[0][:, par::2].rearrange("p (g k) -> p g k", g=64), 64
        emit_dec_nvfp4(p, env, pk5[0][:].bitcast(U8), 1024, W5[0], scb, wv_mk,
                       [(ld5, ldv), (pet5, p.val(pet5))], "dvd5", dec)
        dcv5[i] = p.val(dvd5)
        if i + 1 < 128:
            emit_5a_load(i + 1)

    def emit_5a_T(i):
        cc, f = divmod(i, 16)
        dcv = dcv5[i]
        for g in range(4):
            bk = pb[tb["j"] % 2]
            for k in range(4):
                c = 4 * g + k
                p.emit("tensor", lambda e, c=c, k=k, bk=bk: e.transpose(
                    bk.bitcast(BF16)[:, 128 * k:128 * (k + 1)],
                    W5[0][:, 128 * c:128 * (c + 1)], env["ident"][:]),
                    waits=(tgrp_start_waits([(dvd5, dcv)]) if k == 0 else []),
                    inc=("pet5" if k == 3 else None))
            sb_i = g % 2
            p.emit("vector", lambda e, sb_i=sb_i, bk=bk: e.tensor_copy(
                st5[sb_i][:], bk.bitcast(BF16)[:, 0:512]),
                waits=[(pet5, p.val(pet5)), (f"stw{sb_i}", p.val(f"stw{sb_i}"))],
                inc="dve")
            tgrp_done(p.val("dve"))
            dst = wdT[2048 * cc + 512 * g:2048 * cc + 512 * (g + 1),
                      128 * f:128 * (f + 1)].rearrange("(t p) q -> p t q", p=128)
            srcv = st5[sb_i][:].rearrange("p (t q) -> p t q", t=4)
            p.dma("scalar", dst, srcv, waits=[("dve", p.val("dve"))], inc=f"stw{sb_i}")
        if f == 15:
            sa_stw[cc] = (p.val("stw0"), p.val("stw1"))

    def emit_pk_load(n):
        c, half = divmod(n, 2)
        Wp, sc = (gWp, gs) if half == 0 else (uWp, us)
        b = n % 2
        p.dma("gpsimd", s_pk[b][:], Wp[128 * c:128 * (c + 1), :],
              waits=[(dvd, p.val(dvd) - 1)], inc=f"ldp{b}")
        p.dma("gpsimd", s_sc[b][:], sc[128 * c:128 * (c + 1), :], inc=f"ldp{b}")

    def emit_dec(n):
        b = n % 2
        ldv = ld_of[n]
        scb = s_sc[b][:].unsqueeze(2).broadcast_to([128, 64, 16])

        def wv_mk(par, b=b):
            return s_W[b][:, par::2].rearrange("p (g k) -> p g k", g=64), 64
        emit_dec_nvfp4(p, env, s_pk[b][:].bitcast(U8), 1024, s_W[b], scb, wv_mk,
                       [(f"ldp{b}", ldv), (pet, p.val(pet))], "dvd", dec)
        dec_of[n] = p.val(dvd)

    def emit_T(n):
        b = n % 2
        for grp in range(4):
            bk = pb[tb["j"] % 2]
            for k in range(4):
                ic = 4 * grp + k
                p.emit("tensor", lambda e, b=b, ic=ic, k=k, bk=bk: e.transpose(
                    bk.bitcast(BF16)[:, 128 * k:128 * (k + 1)],
                    s_W[b][:, 128 * ic:128 * (ic + 1)], env["ident"][:]),
                    waits=(tgrp_start_waits([(dvd, dec_of[n])]) if k == 0 else []),
                    inc=("pet" if k == 3 else None))
            p.emit("vector", lambda e, b=b, grp=grp, bk=bk: e.tensor_copy(
                s_wT[b][:].rearrange("p (ic q) -> p ic q", ic=16)[:, 4 * grp:4 * grp + 4, :],
                bk.bitcast(BF16)[:, 0:512].rearrange("p (t q) -> p t q", t=4)),
                waits=[(pet, p.val(pet)), (pem, p.val(pem) - 4)], inc="dve")
            tgrp_done(p.val("dve"))
        T_of[n] = p.val(dve)

    down_done = {-1: 0}
    h_of = {}
    bank_free = {tg: 0 for tg in range(4)}

    def emit_M(n):
        c, half = divmod(n, 2)
        b = n % 2
        slot = c % HRING
        for tg in range(4):
            bank = pb[4 + tg]
            for ic in range(16):
                w = []
                if ic == 0:
                    w = ([(dve, T_of[n]), bank_free[tg]] if bank_free[tg]
                         else [(dve, T_of[n])])
                p.emit("tensor", lambda e, b=b, ic=ic, tg=tg, bank=bank: e.matmul(
                    bank[:], s_wT[b][:, 128 * ic:128 * (ic + 1)],
                    xlt[:, 2048 * ic + 512 * tg:2048 * ic + 512 * (tg + 1)],
                    start=(ic == 0), stop=(ic == 15)),
                    waits=w, inc=("pem" if ic == 15 else None))
            pemv = p.val(pem)
            if half == 0:
                p.emit("vector", lambda e, tg=tg, bank=bank: e.tensor_copy(
                    s_gg[:, 512 * tg:512 * (tg + 1)], bank[:]),
                    waits=[(pem, pemv)], inc="dvg")
                bank_free[tg] = ("dvg", p.val(dvg))
                p.emit("scalar", lambda e, tg=tg: e.activation(
                    s_gg[:, 512 * tg:512 * (tg + 1)],
                    s_gg[:, 512 * tg:512 * (tg + 1)], AF.Gelu_apprx_tanh),
                    waits=[("dvg", p.val(dvg))], inc="ach")
            else:
                p.emit("vector", lambda e, tg=tg, slot=slot, bank=bank: e.tensor_mul(
                    s_h[slot][:, 512 * tg:512 * (tg + 1)],
                    s_gg[:, 512 * tg:512 * (tg + 1)], bank[:]),
                    waits=[(pem, pemv), (ach, p.val(ach)),
                           (pdn, down_done.get((c - HRING) // 8, 0))],
                    inc="dvh")
                bank_free[tg] = ("dvh", p.val(dvh))
        if half == 1:
            h_of[c] = p.val(dvh)

    wd_loaded = {}

    def emit_wd_loads(g):
        for ci in range(8):
            c = 8 * g + ci
            p.dma("gpsimd", s_wd[ci], wdT[128 * c:128 * (c + 1), :],
                  waits=[("stw0", sa_stw[g // 2][0]), ("stw1", sa_stw[g // 2][1]),
                         (pdn, down_done[g - 1]),
                         ("lnch", p.val("lnch"))], inc="ldd")
        wd_loaded[g] = p.val(ldd)

    def emit_down_regions(g, rr):
        hg_done = h_of[8 * g + 7]
        for r in rr:
            tc, fg = divmod(r, 4)
            bank = pb[2 + r % 2]
            for ci in range(8):
                slot = (8 * g + ci) % HRING
                p.emit("tensor", lambda e, ci=ci, tc=tc, fg=fg, bank=bank, slot=slot: e.matmul(
                    bank[:], s_h[slot][:, 128 * tc:128 * (tc + 1)],
                    s_wd[ci][:, 512 * fg:512 * (fg + 1)],
                    start=(ci == 0), stop=(ci == 7)),
                    waits=([(ldd, wd_loaded[g]), (dvh, hg_done),
                            (dvo, p.val(dvo) - 1)] if ci == 0 else []),
                    inc=("pdn" if ci == 7 else None))
            yb = r % 2
            p.emit("vector", lambda e, yb=yb, bank=bank: e.tensor_copy(
                s_yst[yb][:], bank[:]),
                waits=[(pdn, p.val(pdn)), (f"sta{yb}", p.val(f"sta{yb}"))], inc="dvo")
            gw = [(f"sta{yb}", sta_grp[g - 1][yb])] if g > 0 else []
            p.dma("gpsimd", out[128 * tc:128 * (tc + 1), 512 * fg:512 * (fg + 1)],
                  s_yst[yb][:],
                  waits=[(dvo, p.val(dvo)), ("sti", sti_v)] + gw, inc=f"sta{yb}",
                  accum_op=ALU.add)

    ld_of = {}
    dec_of = {}
    T_of = {}
    sta_grp = {}
    emit_5a_load(0)
    for i in range(32):
        emit_5a_dec(i)
        emit_5a_T(i)
    sa_next = 32
    emit_pk_load(0)
    ld_of[0] = p.val("ldp0")
    emit_dec(0)
    for n in range(NCH):
        if n + 1 < NCH:
            emit_pk_load(n + 1)
            ld_of[n + 1] = p.val(f"ldp{(n + 1) % 2}")
        emit_T(n)
        sa_this = sa_next if (n % 2 == 0 and sa_next < 128) else None
        if sa_this is not None:
            emit_5a_dec(sa_this)
            sa_next += 1
        if n + 1 < NCH:
            emit_dec(n + 1)
        emit_M(n)
        if sa_this is not None:
            emit_5a_T(sa_this)
        g = n // 16
        if n >= 16 and (n % 16) < 8:
            gprev = g - 1
            if n % 16 == 0:
                emit_wd_loads(gprev)
            emit_down_regions(gprev, range((n % 16) * 8, (n % 16) * 8 + 8))
            if n % 16 == 7:
                down_done[gprev] = p.val(pdn)
                sta_grp[gprev] = (p.val("sta0"), p.val("sta1"))
    emit_wd_loads(15)
    emit_down_regions(15, range(64))
    down_done[15] = p.val(pdn)
    sta_grp[15] = (p.val("sta0"), p.val("sta1"))
    return [("sta0", p.val("sta0")), ("sta1", p.val("sta1"))]


EI, EO, IN = "ExternalInput", "ExternalOutput", "Internal"

DECLS_K1 = [("xin", [T, D], F32, EI), ("wqkv", [D, 3 * D], F32, EI),
            ("wout", [D, D], F32, EI)]
DECLS_K2 = [("gup", [11264, 256], I32, EI), ("gus", [16, 11264], F16, EI),
            ("dp", [D, 704], I32, EI), ("ds", [44, D], F16, EI),
            ("wdT0", [5632, D], BF16, IN), ("yd", [T, D], F32, IN)]
DECLS_K3A = [("wq", [D, D], F32, EI), ("wk", [D, D], F32, EI), ("wv", [D, D], F32, EI),
             ("bq", [D], F32, EI), ("bk", [D], F32, EI), ("bv", [D], F32, EI),
             ("pos", [1, T], F32, EI), ("qT", [D, T], BF16, EO),
             ("kT", [D, T], BF16, EO), ("vv", [T, D], BF16, EO)]
DECLS_K3B = [("qT2", [D, T], BF16, EI), ("kg", [4 * D, T], BF16, EI),
             ("vg", [4 * T, D], BF16, EI), ("wo", [D, D], F32, EI), ("bo", [D], F32, EI)]
DECLS_K5 = [("gWp", [16384, 1024], I32, EI), ("gs", [16384, 64], F32, EI),
            ("uWp", [16384, 1024], I32, EI), ("us", [16384, 64], F32, EI),
            ("dWp", [D, 8192], I32, EI), ("dsc", [D, 512], F32, EI),
            ("wdT1", [16384, D], BF16, IN), ("out", [T, D], F32, EO)]


def build_launchA():
    decls = (DECLS_K1 + [("x1", [T, D], F32, IN)] + DECLS_K2 +
             [("x2", [T, D], F32, EO)] + DECLS_K3A)
    return build_launch([emit_k1, emit_k2, emit_k3a], decls)


def build_launchB():
    decls = (DECLS_K3B + [("x2b", [T, D], F32, EI), ("x3", [T, D], F32, IN)] + DECLS_K5)
    return build_launch([emit_k3b, emit_k5], decls, aliases={"x3b": "x3"})


def build_k1():
    return build_launch([emit_k1], DECLS_K1 + [("x1", [T, D], F32, EO)])


def build_k2():
    return build_launch([emit_k2], [("x1", [T, D], F32, EI)] + DECLS_K2 +
                        [("x2", [T, D], F32, EO)])


def build_k3a():
    return build_launch([emit_k3a], [("x2", [T, D], F32, EI)] + DECLS_K3A)


def build_k3b():
    return build_launch([emit_k3b], DECLS_K3B +
                        [("x2b", [T, D], F32, EI), ("x3", [T, D], F32, EO)])


def build_k5():
    return build_launch([emit_k5], [("x3b", [T, D], F32, EI)] + DECLS_K5)


def _shard_x(x):
    out = []
    for c in range(NCORES):
        b, s = c // 4, 2048 * (c % 4)
        out.append(np.ascontiguousarray(x[b, s:s + 2048, :], dtype=np.float32))
    return out


def kernel(**inputs):
    f32 = lambda k: np.ascontiguousarray(np.asarray(inputs[k], np.float32))
    i32 = lambda k: np.ascontiguousarray(np.asarray(inputs[k], np.int32))
    f16 = lambda k: np.ascontiguousarray(np.asarray(inputs[k], np.float16))

    xs = _shard_x(f32("L0_attn_x"))
    cores = list(range(NCORES))

    ncA = build_launchA()
    in_a = []
    for c in cores:
        s = 2048 * (c % 4)
        in_a.append({
            "xin": xs[c],
            "wqkv": f32("L0_attn_w_qkv"), "wout": f32("L0_attn_w_out"),
            "gup": i32("L0_ffn_gate_up_packed"), "gus": f16("L0_ffn_gate_up_scales"),
            "dp": i32("L0_ffn_down_packed"), "ds": f16("L0_ffn_down_scales"),
            "wq": f32("L1_attn_Wq"), "wk": f32("L1_attn_Wk"), "wv": f32("L1_attn_Wv"),
            "bq": f32("L1_attn_bq"), "bk": f32("L1_attn_bk"), "bv": f32("L1_attn_bv"),
            "pos": np.arange(s, s + 2048, dtype=np.float32).reshape(1, 2048),
        })
    rA = run_bass_kernel_spmd(ncA, in_a, core_ids=cores)
    x2 = [np.asarray(rA.results[c]["x2"]) for c in cores]
    qT = [np.asarray(rA.results[c]["qT"]) for c in cores]
    kT = [np.asarray(rA.results[c]["kT"]) for c in cores]
    vv = [np.asarray(rA.results[c]["vv"]) for c in cores]

    ncB = build_launchB()
    in_b = []
    for c in cores:
        b = c // 4
        grp = [4 * b + i for i in range(4)]
        in_b.append({
            "qT2": qT[c],
            "kg": np.ascontiguousarray(np.concatenate([kT[r] for r in grp], axis=0)),
            "vg": np.ascontiguousarray(np.concatenate([vv[r] for r in grp], axis=0)),
            "wo": f32("L1_attn_Wo"), "bo": f32("L1_attn_bo"), "x2b": x2[c],
            "gWp": i32("L1_ffn_gate_W_packed"), "gs": f32("L1_ffn_gate_scales"),
            "uWp": i32("L1_ffn_up_W_packed"), "us": f32("L1_ffn_up_scales"),
            "dWp": i32("L1_ffn_down_W_packed"), "dsc": f32("L1_ffn_down_scales"),
        })
    rB = run_bass_kernel_spmd(ncB, in_b, core_ids=cores)

    out = np.empty((2, 8192, 2048), np.float32)
    for c in cores:
        b, s = c // 4, 2048 * (c % 4)
        out[b, s:s + 2048, :] = rB.results[c]["out"]
    return out


# version 07
# speedup vs baseline: 1.3850x; 1.0184x over previous
import numpy as np
from contextlib import ExitStack

import concourse.bass as bass
import concourse.mybir as mybir
from concourse.bass_utils import run_bass_kernel_spmd

F32 = mybir.dt.float32
BF16 = mybir.dt.bfloat16
F16 = mybir.dt.float16
I32 = mybir.dt.int32
U16 = mybir.dt.uint16
U8 = mybir.dt.uint8
AF = mybir.ActivationFunctionType
ALU = mybir.AluOpType
AX = mybir.AxisListType

ENGINES = ("sync", "vector", "scalar", "tensor", "gpsimd")
T = 2048
D = 2048
NCORES = 8
INV_SQRT_HD = float(1.0 / np.sqrt(128.0))


class P:

    def __init__(self):
        self.steps = {e: [] for e in ENGINES}
        self.sems = {}
        self.order = []

    def sem(self, name):
        if name not in self.sems:
            self.sems[name] = 0
            self.order.append(name)
        return name

    def val(self, name):
        self.sem(name)
        return self.sems[name]

    def emit(self, eng, fn, waits=(), inc=None, inc_n=1):
        waits = tuple((s, v) for (s, v) in waits if v > 0)
        if inc is not None:
            self.sem(inc)
            self.sems[inc] += inc_n
        self.steps[eng].append((fn, waits, inc, inc_n))

    def dma(self, eng, out, in_, waits=(), inc=None, **kw):
        self.emit(eng, lambda e, out=out, in_=in_, kw=kw: e.dma_start(out=out, in_=in_, **kw),
                  waits=waits, inc=inc, inc_n=16)

    def barrier(self, engines=ENGINES):
        bar = self.sem("bar")
        for e in engines:
            self.emit(e, lambda eng: eng.nop(), inc="bar")
        tgt = self.sems["bar"]
        for e in engines:
            self.steps[e].append((None, ((bar, tgt),), None, 0))

    def finalize(self, nc, ctx, final_wait=None):
        handles = {n: ctx.enter_context(nc.semaphore(n)) for n in self.order}
        with nc.Block() as block:
            for eng_name in ENGINES:
                if not self.steps[eng_name] and eng_name != "sync":
                    continue

                def mk(eng_name=eng_name):
                    def body(eng):
                        for fn, waits, inc, inc_n in self.steps[eng_name]:
                            for s, v in waits:
                                eng.wait_ge(handles[s], v)
                            if fn is None:
                                continue
                            inst = fn(eng)
                            if inc:
                                inst.then_inc(handles[inc], inc_n)
                        if eng_name == "sync" and final_wait is not None:
                            eng.wait_ge(handles[final_wait[0]], final_wait[1])
                    return body
                getattr(block, eng_name)(mk())



SBUF_TOP = 229344


def make_carver(nc, base):
    state = {"off": base}

    def carve(name, shape, dtype):
        nbytes = int(np.prod(shape[1:])) * mybir.dt.size(dtype)
        off = (state["off"] + 31) // 32 * 32
        t = nc.alloc_sbuf_tensor_at(name, shape, dtype, offset=off)
        state["off"] = off + nbytes
        assert state["off"] <= SBUF_TOP, f"carve overflow at {name}: {state['off']}"
        return t

    def reset():
        state["off"] = base
    return carve, reset


def build_launch(phase_emitters, dram_decls, aliases=None):
    nc = bass.Bass("TRN2", target_bir_lowering=False, debug=False)
    io = {}
    for name, shape, dtype, kind in dram_decls:
        io[name] = nc.dram_tensor(name, shape, dtype, kind=kind).ap()
    if aliases:
        for k, v in aliases.items():
            io[k] = io[v]
    ctx = ExitStack()
    env = {"nc": nc, "io": io}
    pb = [nc.alloc_psum_tensor(f"pb{i}", [128, 512], F32) for i in range(8)]
    env["pb"] = pb

    def sb_fixed(name, shape, dtype):
        return nc.alloc_sbuf_tensor(name, shape, dtype)
    env["sb"] = sb_fixed
    p = P()
    p.sem("init")
    common_small(p, env)
    common_identf(p, env)
    arena_base = SBUF_TOP - nc.sbuf_bytes_remaining
    carve, reset = make_carver(nc, arena_base)
    p.barrier()
    for i, emit in enumerate(phase_emitters):
        reset()
        pref = f"ph{i}_"
        env["sb"] = (lambda pref=pref: lambda name, shape, dtype:
                     carve(pref + name, shape, dtype))()
        waits = emit(p, env)
        for s, v in waits:
            p.emit("sync", lambda e: e.nop(), waits=[(s, v)])
        p.barrier()
    p.finalize(nc, ctx, final_wait=None)
    env["ctx"] = ctx
    return nc


def common_small(p, env):
    nc = env["nc"]
    sb = env["sb"]
    ident = sb("c_ident", [128, 128], BF16)
    m_incl = sb("c_mincl", [128, 128], BF16)
    m_strict = sb("c_mstr", [128, 128], BF16)
    ones_col = sb("c_ones", [128, 1], BF16)
    ones_row = sb("c_onesr", [1, 128], F32)
    env["ident"], env["m_incl"], env["m_strict"] = ident, m_incl, m_strict
    env["ones_col"], env["ones_row"] = ones_col, ones_row

    def ch(fn):
        p.emit("gpsimd", fn, waits=[("init", p.val("init"))], inc="init")
    ch(lambda e: e.memset(ident[:], 0.0))
    ch(lambda e: e.affine_select(
        out=ident[:], in_=ident[:], compare_op=ALU.not_equal, fill=1.0,
        base=0, pattern=[[-1, 128]], channel_multiplier=1))
    ch(lambda e: e.memset(m_incl[:], 1.0))
    ch(lambda e: e.affine_select(
        out=m_incl[:], in_=m_incl[:], compare_op=ALU.is_ge, fill=0.0,
        base=0, pattern=[[1, 128]], channel_multiplier=-1))
    ch(lambda e: e.memset(m_strict[:], 1.0))
    ch(lambda e: e.affine_select(
        out=m_strict[:], in_=m_strict[:], compare_op=ALU.is_gt, fill=0.0,
        base=0, pattern=[[1, 128]], channel_multiplier=-1))
    ch(lambda e: e.memset(ones_col[:], 1.0))
    ch(lambda e: e.memset(ones_row[:], 1.0))


def emit_ln(p, env, xin, xlt, pref):
    nc = env["nc"]
    sb = env["sb"]
    s_xc = [sb(f"{pref}_xc{i}", [128, T], F32) for i in range(2)]
    s_sq = sb(f"{pref}_sq", [128, T], F32)
    s_tln = [sb(f"{pref}_tln{i}", [128, T], BF16) for i in range(2)]
    sm = {k: sb(f"{pref}_{k}", [128, 1], F32)
          for k in ("ssq", "ssum", "nm", "t2e", "mm", "ve", "lnv", "rstd", "negmr")}
    env[pref + "_xc"] = s_xc
    env[pref + "_sq"] = s_sq
    env[pref + "_tln"] = s_tln
    ch = p.sem(pref + "ch")
    app = p.sem(pref + "app")
    ptr = p.sem(pref + "ptr")
    dev = p.sem(pref + "dev")
    ident = env["ident"]
    pT = env["pb"][0].bitcast(BF16)

    def chain(eng, fn, extra=()):
        p.emit(eng, fn, waits=[(ch, p.val(ch))] + list(extra), inc=pref + "ch")

    for ci in range(16):
        b = ci % 2
        p.dma("sync", s_xc[b][:], xin[128 * ci:128 * (ci + 1), :],
              waits=[(ch, p.val(ch) - 10)], inc=pref + f"ld{b}")
        ldv = p.val(pref + f"ld{b}")
        chain("vector", lambda e, b=b: e.reduce_sum(sm["ssum"][:], s_xc[b][:], axis=AX.X),
              extra=[(pref + f"ld{b}", ldv)])
        chain("scalar", lambda e, b=b: e.activation(s_sq[:], s_xc[b][:], AF.Square,
                                                    accum_out=sm["ssq"][:]))
        chain("vector", lambda e: e.tensor_scalar(sm["nm"][:], sm["ssum"][:], -1.0 / D, None, ALU.mult))
        chain("vector", lambda e: e.tensor_scalar(sm["t2e"][:], sm["ssq"][:], 1.0 / D, 1e-5, ALU.mult, ALU.add))
        chain("vector", lambda e: e.tensor_mul(sm["mm"][:], sm["nm"][:], sm["nm"][:]))
        chain("vector", lambda e: e.tensor_sub(sm["ve"][:], sm["t2e"][:], sm["mm"][:]))
        chain("scalar", lambda e: e.activation(sm["lnv"][:], sm["ve"][:], AF.Ln))
        chain("scalar", lambda e: e.activation(sm["rstd"][:], sm["lnv"][:], AF.Exp, scale=-0.5))
        chain("vector", lambda e: e.tensor_mul(sm["negmr"][:], sm["nm"][:], sm["rstd"][:]))
        chain("scalar", lambda e, b=b: e.activation(s_tln[b][:], s_xc[b][:], AF.Identity,
                                                    bias=sm["negmr"][:], scale=sm["rstd"][:]),
              extra=[(dev, p.val(dev) - 4)])
        for g in range(4):
            for k in range(4):
                fj = 4 * g + k
                p.emit("tensor", lambda e, b=b, fj=fj, k=k: e.transpose(
                    pT[:, 128 * k:128 * (k + 1)], s_tln[b][:, 128 * fj:128 * (fj + 1)], ident[:]),
                    waits=[(ch, p.val(ch)), (dev, p.val(dev))] if k == 0 else [],
                    inc=pref + "ptr" if k == 3 else None)
            dst = xlt[:].rearrange("p (f t) -> p f t", f=16)[:, 4 * g:4 * g + 4,
                                                            128 * ci:128 * (ci + 1)]
            src = pT[:, 0:512].rearrange("p (f t) -> p f t", f=4)
            p.emit("vector", lambda e, dst=dst, src=src: e.tensor_copy(dst, src),
                   waits=[(ptr, p.val(ptr))], inc=pref + "dev")


def emit_k1(p, env):
    nc = env["nc"]; sb = env["sb"]; pb = env["pb"]; io = env["io"]
    xin, wqkv, wout, x1 = io["xin"], io["wqkv"], io["wout"], io["x1"]
    xlt = sb("xlt", [128, 16 * T], BF16)
    emit_ln(p, env, xin, xlt, "ln")
    p.emit("tensor", lambda e: e.nop(), waits=[("lndev", p.val("lndev"))])
    p.barrier()

    s_w = [sb(f"wslab{i}", [128, 16 * 512], BF16) for i in range(2)]
    s_q = sb("s_q", [128, 16 * 512], BF16)
    s_k = sb("s_k", [128, 16 * 512], BF16)
    s_v = sb("s_v", [128, 4 * D], BF16)
    s_o = sb("s_o", [128, 16 * 512], BF16)
    s_pt = [sb(f"s_pt{i}", [128, 512], BF16) for i in range(4)]
    s_rs = sb("s_rs", [1, 512], F32)
    s_rsb = sb("s_rsb", [128, 512], F32)
    s_xcs = [env["ln_tln"][0][:].bitcast(F32)[:, 512 * i:512 * (i + 1)] for i in range(2)]
    s_ys = [env["ln_tln"][1][:].bitcast(F32)[:, 512 * i:512 * (i + 1)] for i in range(2)]

    pes = p.sem("pes")
    ace = p.sem("ace")
    dvm = p.sem("dvm")
    pev = p.sem("pev")
    aev = p.sem("aev")
    dvn = p.sem("dvn")
    dvy = p.sem("dvy")

    for hb in range(4):
        th0 = 512 * hb
        aev0 = p.val(aev)
        for sg in range(12):
            wb = sg % 2
            p.dma("gpsimd", s_w[wb][:],
                  wqkv[:, 512 * sg:512 * (sg + 1)].rearrange("(ic p) o -> p ic o", p=128),
                  waits=[(aev, aev0 + 4 * (sg - 1))] if sg >= 2 else [], inc=f"ldw{wb}")
            ldv = p.val(f"ldw{wb}")
            for u in range(4):
                bank = pb[4 + u]
                for ic in range(16):
                    if sg < 8:
                        fn = (lambda e, wb=wb, ic=ic, u=u, th0=th0, bank=bank: e.matmul(
                            bank[:], s_w[wb][:, 512 * ic + 128 * u:512 * ic + 128 * (u + 1)],
                            xlt[:, 2048 * ic + th0:2048 * ic + th0 + 512],
                            start=(ic == 0), stop=(ic == 15)))
                    else:
                        fn = (lambda e, wb=wb, ic=ic, u=u, th0=th0, bank=bank: e.matmul(
                            bank[:], xlt[:, 2048 * ic + th0 + 128 * u:2048 * ic + th0 + 128 * (u + 1)],
                            s_w[wb][:, 512 * ic:512 * (ic + 1)],
                            start=(ic == 0), stop=(ic == 15)))
                    p.emit("tensor", fn,
                           waits=([(f"ldw{wb}", ldv),
                                   (aev, aev0 + 4 * (sg - 1) + u + 1 if sg >= 1 else 0)]
                                  if ic == 0 else []),
                           inc=("pes" if ic == 15 else None))
                if sg < 8:
                    dst, off = (s_q, (sg % 4) * 4 + u) if sg < 4 else (s_k, ((sg - 4) % 4) * 4 + u)
                    dap = dst[:, 512 * off:512 * (off + 1)]
                else:
                    dap = s_v[:, 2048 * u + 512 * (sg - 8):2048 * u + 512 * (sg - 8) + 512]
                p.emit("scalar", lambda e, dap=dap, bank=bank: e.activation(dap, bank[:], AF.Copy),
                       waits=[(pes, p.val(pes))], inc="aev")
        p.barrier()

        sB = [pb[0], pb[1], pb[5], pb[6]]
        LAG = 3
        for j in range(16):
            mask_of = {}
            pv_of = {}

            def emit_pv(c, j=j, mask_of=mask_of, pv_of=pv_of):
                Tp, jp = c // 16, c % 16
                col0 = 128 * Tp
                ptb = s_pt[c % 4]
                p.emit("tensor", lambda e, c=c, Tp=Tp, jp=jp, ptb=ptb, col0=col0: e.matmul(
                    pb[2][:, col0:512],
                    s_v[:, 2048 * Tp + 128 * jp:2048 * Tp + 128 * (jp + 1)],
                    ptb[:, col0:512], start=(c == 0), stop=(c == 63),
                    skip_group_check=True),
                    waits=[(dvm, mask_of[c])] + ([(dvn, p.val(dvn))] if c == 0 else []),
                    inc="pev")
                p.emit("tensor", lambda e, c=c, ptb=ptb, col0=col0: e.matmul(
                    pb[3][0:1, col0:512], env["ones_col"][:], ptb[:, col0:512],
                    start=(c == 0), stop=(c == 63), skip_group_check=True),
                    inc="pev")
                pv_of[c] = p.val(pev)

            for c in range(64):
                Tp, jp = c // 16, c % 16
                col0 = 128 * Tp
                sbank = sB[c % 4]
                ptb = s_pt[c % 4]
                p.emit("tensor", lambda e, j=j, Tp=Tp, jp=jp, col0=col0, sbank=sbank: e.matmul(
                    sbank[:, col0:512],
                    s_k[:, 512 * jp + 128 * Tp:512 * jp + 128 * (Tp + 1)],
                    s_q[:, 512 * j + col0:512 * j + 512],
                    start=True, stop=True, skip_group_check=True),
                    waits=[(ace, p.val(ace) - 4), (dvn, p.val(dvn))] if c <= 3 else
                          [(ace, p.val(ace) - 4)],
                    inc="pes")
                qk_done = p.val(pes)
                p.emit("scalar", lambda e, ptb=ptb, col0=col0, sbank=sbank: e.activation(
                    ptb[:, col0:512], sbank[:, col0:512], AF.Exp, scale=INV_SQRT_HD),
                    waits=[(pes, qk_done), (pev, pv_of.get(c - 4, 0))], inc="ace")
                mask = env["m_incl"] if jp <= j else env["m_strict"]
                p.emit("vector", lambda e, ptb=ptb, col0=col0, mask=mask: e.tensor_mul(
                    ptb[:, col0:col0 + 128], ptb[:, col0:col0 + 128], mask[:]),
                    waits=[(ace, p.val(ace))], inc="dvm")
                mask_of[c] = p.val(dvm)
                if c >= LAG:
                    emit_pv(c - LAG)
            for c in range(64 - LAG, 64):
                emit_pv(c)
            p.emit("vector", lambda e: e.reciprocal(s_rs[:], pb[3][0:1, :]),
                   waits=[(pev, p.val(pev))], inc="dvn")
            p.emit("tensor", lambda e: e.matmul(pb[4][:], env["ones_row"][:], s_rs[:],
                                                start=True, stop=True),
                   waits=[(dvn, p.val(dvn)), (aev, p.val(aev))], inc="pes")
            p.emit("scalar", lambda e: e.activation(s_rsb[:], pb[4][:], AF.Copy),
                   waits=[(pes, p.val(pes))], inc="aev")
            p.emit("vector", lambda e, j=j: e.tensor_mul(
                s_o[:, 512 * j:512 * (j + 1)], pb[2][:], s_rsb[:]),
                   waits=[(aev, p.val(aev))], inc="dvn")
        p.barrier()

        dvy0 = p.val(dvy)
        for og in range(4):
            wb = og % 2
            p.dma("gpsimd", s_w[wb][:],
                  wout[:, 512 * og:512 * (og + 1)].rearrange("(ic p) o -> p ic o", p=128),
                  waits=[(dvy, dvy0 + 4 * (og - 1))] if og >= 2 else [], inc=f"ldw{wb}")
            ldv = p.val(f"ldw{wb}")
            for tc in range(4):
                i = (og * 4 + tc) % 2
                bank = pb[4 + tc % 4]
                p.dma("sync", s_xcs[i],
                      xin[th0 + 128 * tc:th0 + 128 * (tc + 1), 512 * og:512 * (og + 1)],
                      waits=[(dvy, p.val(dvy) - 1)], inc=f"ldx{i}")
                ldxv = p.val(f"ldx{i}")
                for ec in range(16):
                    p.emit("tensor", lambda e, wb=wb, ec=ec, tc=tc, bank=bank: e.matmul(
                        bank[:], s_o[:, 512 * ec + 128 * tc:512 * ec + 128 * (tc + 1)],
                        s_w[wb][:, 512 * ec:512 * (ec + 1)],
                        start=(ec == 0), stop=(ec == 15)),
                        waits=([(f"ldw{wb}", ldv), (dvy, p.val(dvy))] if ec == 0 else []),
                        inc=("pes" if ec == 15 else None))
                p.emit("vector", lambda e, i=i, bank=bank: e.scalar_tensor_tensor(
                    s_ys[i], bank[:], 1.0, s_xcs[i], ALU.mult, ALU.add),
                    waits=[(pes, p.val(pes)), (f"ldx{i}", ldxv),
                           (f"sty{i}", p.val(f"sty{i}"))],
                    inc="dvy")
                p.dma("sync", x1[th0 + 128 * tc:th0 + 128 * (tc + 1), 512 * og:512 * (og + 1)],
                      s_ys[i], waits=[(dvy, p.val(dvy))], inc=f"sty{i}")
        p.barrier()

    return [("sty0", p.val("sty0")), ("sty1", p.val("sty1"))]


def emit_dec_fp4(p, env, u16v, nC2, s_W, scT_bcast, wv_maker, waits_first, inc_last,
                 dec):
    k_, sh_, b_, t_, vp_, v_ = (dec[n] for n in ("k", "sh", "b", "t", "vp", "v"))
    for jj in range(4):
        w0 = waits_first if jj == 0 else ()
        if jj == 0:
            p.emit("vector", lambda e: e.tensor_scalar(
                k_[:, :nC2], u16v, 7, None, ALU.bitwise_and), waits=w0)
        else:
            p.emit("vector", lambda e, jj=jj: e.tensor_scalar(
                k_[:, :nC2], u16v, 4 * jj, 7, ALU.logical_shift_right, ALU.bitwise_and))
        if jj == 3:
            p.emit("vector", lambda e: e.tensor_scalar(
                sh_[:, :nC2], u16v, 0x8000, None, ALU.bitwise_and))
        else:
            p.emit("vector", lambda e, jj=jj: e.tensor_scalar(
                sh_[:, :nC2], u16v, 12 - 4 * jj, 0x8000,
                ALU.logical_shift_left, ALU.bitwise_and))
        p.emit("vector", lambda e: e.tensor_scalar(
            b_[:, :nC2], k_[:, :nC2], 64.0, 16128.0, ALU.mult, ALU.add))
        p.emit("vector", lambda e: e.tensor_scalar(
            t_[:, :nC2], b_[:, :nC2].bitcast(BF16), 2.0, -1.0, ALU.mult, ALU.add))
        p.emit("vector", lambda e: e.tensor_tensor(
            vp_[:, :nC2], t_[:, :nC2], b_[:, :nC2].bitcast(BF16), ALU.min))
        p.emit("vector", lambda e: e.tensor_tensor(
            v_[:, :nC2].bitcast(U16), vp_[:, :nC2].bitcast(U16), sh_[:, :nC2],
            ALU.bitwise_xor))
        wv, ng = wv_maker(jj)
        vv = v_[:, :nC2].rearrange("p (g k) -> p g k", g=ng)
        p.emit("vector", lambda e, wv=wv, vv=vv: e.tensor_tensor(wv, vv, scT_bcast, ALU.mult),
               inc=(inc_last if jj == 3 else None))


def emit_scale_T(p, env, dram_slice, s_h, s_b16, pbank, s_out, nrows, waits, inc):
    ident = env["ident"]
    p.dma("sync", s_h[0:nrows, :], dram_slice,
          waits=list(waits) + [("pesc", p.val("pesc"))], inc="ldsc")
    v = p.val("ldsc")
    p.emit("vector", lambda e: e.tensor_copy(s_b16[0:nrows, :], s_h[0:nrows, :]),
           waits=[("ldsc", v)], inc="dvsc")
    p.emit("tensor", lambda e: e.transpose(pbank.bitcast(BF16)[:, 0:nrows],
                                           s_b16[0:nrows, :], ident[0:nrows, 0:nrows]),
           waits=[("dvsc", p.val("dvsc")), (inc, p.val(inc))], inc="pesc")
    p.emit("vector", lambda e: e.tensor_copy(s_out[:, 0:nrows], pbank.bitcast(BF16)[:, 0:nrows]),
           waits=[("pesc", p.val("pesc"))], inc=inc)


def emit_k2(p, env):
    nc = env["nc"]; sb = env["sb"]; pb = env["pb"]; io = env["io"]
    x1, gup, gus, dp, ds, x2, wdT0, yd = (io["x1"], io["gup"], io["gus"], io["dp"],
                                          io["ds"], io["x2"], io["wdT0"], io["yd"])
    wdT = wdT0
    for s in ("ldsc", "dvsc", "pesc"):
        p.sem(s)
    xlt = sb("xlt", [128, 16 * T], BF16)
    emit_ln(p, env, x1, xlt, "ln")
    p.emit("tensor", lambda e: e.nop(), waits=[("lndev", p.val("lndev"))])

    dec = {n: sb(f"dec_{n}", [128, 1408], U16 if n in ("k", "sh", "b") else BF16)
           for n in ("k", "sh", "b", "t", "vp", "v")}
    s_pk = [sb(f"s_pk{i}", [128, 704], I32) for i in range(2)]
    s_sch = sb("s_sch", [44, 128], F16)
    s_scb = sb("s_scb", [44, 128], BF16)
    s_scT = [sb(f"s_scT{i}", [128, 44], F32) for i in range(2)]
    s_W = [sb(f"s_W{i}", [128, 5632], BF16) for i in range(2)]
    s_wT = [sb(f"s_wT{i}", [128, 2048], BF16) for i in range(2)]
    s_st = [sb(f"s_st{i}", [128, 512], BF16) for i in range(2)]
    s_h = [sb(f"s_h{i}", [128, 2048], BF16) for i in range(4)]
    s_sg = sb("s_sg", [128, 2048], F32)
    s_wd = [sb(f"s_wd{i}", [128, 2048], BF16) for i in range(4)]
    s_yst = [sb(f"s_yst{i}", [128, 512], F32) for i in range(2)]
    dvd = p.sem("dvd")
    pet = p.sem("pet")
    dve = p.sem("dve")
    pem = p.sem("pem")
    ach = p.sem("ach")
    dvh = p.sem("dvh")
    ldd = p.sem("ldd")
    pdn = p.sem("pdn")
    dvo = p.sem("dvo")
    p.barrier()

    tb = {"j": 0, "ev": {-2: 0, -1: 0}}

    def tgrp_start_waits(deps):
        return deps + [(dve, tb["ev"][tb["j"] - 2])]

    def tgrp_done():
        tb["ev"][tb["j"]] = p.val(dve)
        tb["j"] += 1

    for f in range(16):
        fb = f % 2
        p.dma("sync", s_pk[fb][:], dp[128 * f:128 * (f + 1), :],
              waits=[(dvd, p.val(dvd) - 1)], inc=f"ldp{fb}")
        ldv = p.val(f"ldp{fb}")
        emit_scale_T(p, env, ds[:, 128 * f:128 * (f + 1)], s_sch, s_scb, pb[1],
                     s_scT[fb], 44, [(dvd, p.val(dvd) - 1)], "dvsc2" if False else "dvscT")
        scv = p.val("dvscT")
        scb = s_scT[fb][:, 0:44].unsqueeze(2).broadcast_to([128, 44, 32])

        def wv_mk(jj, fb=fb):
            return s_W[fb][:, jj::4].rearrange("p (g k) -> p g k", g=44), 44
        emit_dec_fp4(p, env, s_pk[fb][:].bitcast(U16), 1408, s_W[fb], scb, wv_mk,
                     [(f"ldp{fb}", ldv), ("dvscT", scv), (pet, p.val(pet))], "dvd", dec)
        dcv = p.val(dvd)
        for g in range(11):
            bk = pb[2 + tb["j"] % 2]
            for k in range(4):
                c = 4 * g + k
                p.emit("tensor", lambda e, fb=fb, c=c, k=k, bk=bk: e.transpose(
                    bk.bitcast(BF16)[:, 128 * k:128 * (k + 1)],
                    s_W[fb][:, 128 * c:128 * (c + 1)], env["ident"][:]),
                    waits=(tgrp_start_waits([(dvd, dcv)]) if k == 0 else []),
                    inc=("pet" if k == 3 else None))
            sb_i = (f * 11 + g) % 2
            p.emit("vector", lambda e, sb_i=sb_i, bk=bk: e.tensor_copy(
                s_st[sb_i][:], bk.bitcast(BF16)[:, 0:512]),
                waits=[(pet, p.val(pet)), (f"stw{sb_i}", p.val(f"stw{sb_i}"))], inc="dve")
            tgrp_done()
            dst = wdT[512 * g:512 * (g + 1), 128 * f:128 * (f + 1)].rearrange(
                "(t p) f -> p t f", p=128)
            src = s_st[sb_i][:].rearrange("p (t f) -> p t f", t=4)
            p.dma("scalar", dst, src, waits=[(dve, p.val(dve))], inc=f"stw{sb_i}")
    p.barrier()

    s_gsh = sb("s_gsh", [16, 128], F16)
    s_gsb = sb("s_gsb", [16, 128], BF16)
    s_gsT = [sb(f"s_gsT{i}", [128, 16], F32) for i in range(2)]
    stw_totals = [p.val("stw0"), p.val("stw1")]
    sta_grp = {}
    chunks = [(g, ci, half) for g in range(11) for ci in range(4) for half in (0, 1)]
    dec_of2 = {}
    T2_of = {}

    def emit_load_dec(n):
        g, ci, half = chunks[n]
        c = 4 * g + ci
        row0 = 128 * c if half == 0 else 5632 + 128 * c
        hb2 = n % 2
        p.dma("sync", s_pk[hb2][:, 0:256], gup[row0:row0 + 128, :],
              waits=[(dvd, p.val(dvd) - 1)], inc=f"ldp{hb2}")
        ldv = p.val(f"ldp{hb2}")
        emit_scale_T(p, env, gus[:, row0:row0 + 128], s_gsh, s_gsb, pb[1],
                     s_gsT[hb2], 16, [("dvscT", p.val("dvscT") - 1)], "dvscT")
        scv = p.val("dvscT")
        scb = s_gsT[hb2][:, 0:16].unsqueeze(2).broadcast_to([128, 16, 32])

        def wv_mk(jj, hb2=hb2):
            return s_W[hb2][:, jj::4][:, 0:512].rearrange(
                "p (g k) -> p g k", g=16), 16
        emit_dec_fp4(p, env, s_pk[hb2][:, 0:256].bitcast(U16), 512,
                     s_W[hb2], scb, wv_mk,
                     [(f"ldp{hb2}", ldv), ("dvscT", scv), (pet, p.val(pet))], "dvd", dec)
        dec_of2[n] = p.val(dvd)

    def emit_T2(n):
        hb2 = n % 2
        for grp in range(4):
            bk = pb[2 + tb["j"] % 2]
            for k in range(4):
                ic = 4 * grp + k
                p.emit("tensor", lambda e, hb2=hb2, ic=ic, k=k, bk=bk: e.transpose(
                    bk.bitcast(BF16)[:, 128 * k:128 * (k + 1)],
                    s_W[hb2][:, 128 * ic:128 * (ic + 1)], env["ident"][:]),
                    waits=(tgrp_start_waits([(dvd, dec_of2[n])]) if k == 0 else []),
                    inc=("pet" if k == 3 else None))
            p.emit("vector", lambda e, hb2=hb2, grp=grp, bk=bk: e.tensor_copy(
                s_wT[hb2][:].rearrange("p (ic q) -> p ic q", ic=16)[:, 4 * grp:4 * grp + 4, :],
                bk.bitcast(BF16)[:, 0:512].rearrange("p (t q) -> p t q", t=4)),
                waits=[(pet, p.val(pet)), (pem, p.val(pem) - 4)], inc="dve")
            tgrp_done()
        T2_of[n] = p.val(dve)

    def emit_mm(n):
        g, ci, half = chunks[n]
        hb2 = n % 2
        devv = T2_of[n]
        bank = pb[4 + half]
        for tg in range(4):
            cons = ("dvg", p.val("dvg")) if half == 0 else (dvh, p.val(dvh))
            for ic in range(16):
                p.emit("tensor", lambda e, hb2=hb2, ic=ic, tg=tg, bank=bank: e.matmul(
                    bank[:], s_wT[hb2][:, 128 * ic:128 * (ic + 1)],
                    xlt[:, 2048 * ic + 512 * tg:2048 * ic + 512 * (tg + 1)],
                    start=(ic == 0), stop=(ic == 15)),
                    waits=([(dve, devv), cons] if ic == 0 else []),
                    inc=("pem" if ic == 15 else None))
            if half == 0:
                p.emit("vector", lambda e, tg=tg: e.tensor_copy(
                    s_sg[:, 512 * tg:512 * (tg + 1)], pb[4][:]),
                    waits=[(pem, p.val(pem)), (dvh, p.val(dvh) - 3 + tg)],
                    inc="dvg")
                p.emit("scalar", lambda e, tg=tg: e.activation(
                    s_sg[:, 512 * tg:512 * (tg + 1)],
                    s_sg[:, 512 * tg:512 * (tg + 1)], AF.Silu),
                    waits=[("dvg", p.val("dvg"))], inc="ach")
            else:
                p.emit("vector", lambda e, ci=ci, tg=tg: e.tensor_mul(
                    s_h[ci][:, 512 * tg:512 * (tg + 1)],
                    s_sg[:, 512 * tg:512 * (tg + 1)], pb[5][:]),
                    waits=[(pem, p.val(pem)), (ach, p.val(ach)),
                           (pdn, p.val(pdn))],
                    inc="dvh")

    emit_load_dec(0)
    for n in range(88):
        emit_T2(n)
        if n + 1 < 88:
            emit_load_dec(n + 1)
        emit_mm(n)
        if n % 8 != 7:
            continue
        g = n // 8
        h_done = p.val(dvh)
        for ci in range(4):
            c = 4 * g + ci
            p.dma("gpsimd", s_wd[ci][:], wdT[128 * c:128 * (c + 1), :],
                  waits=[("stw0", stw_totals[0]), ("stw1", stw_totals[1]),
                         (pdn, p.val(pdn))], inc="ldd")
        lddv = p.val(ldd)
        for tc in range(16):
            for fg in range(4):
                idx = tc * 4 + fg
                bank = pb[6 + idx % 2]
                for ci in range(4):
                    p.emit("tensor", lambda e, ci=ci, tc=tc, fg=fg, bank=bank: e.matmul(
                        bank[:], s_h[ci][:, 128 * tc:128 * (tc + 1)],
                        s_wd[ci][:, 512 * fg:512 * (fg + 1)],
                        start=(ci == 0), stop=(ci == 3)),
                        waits=([(ldd, lddv), (dvh, h_done), (dvo, p.val(dvo) - 1)]
                               if ci == 0 else []),
                        inc=("pdn" if ci == 3 else None))
                yb = idx % 2
                p.emit("vector", lambda e, yb=yb, bank=bank: e.tensor_copy(
                    s_yst[yb][:], bank[:]),
                    waits=[(pdn, p.val(pdn)), (f"sta{yb}", p.val(f"sta{yb}"))], inc="dvo")
                gw = [(f"sta{yb}", sta_grp[g - 1][yb])] if g > 0 else []
                p.dma("gpsimd", yd[128 * tc:128 * (tc + 1), 512 * fg:512 * (fg + 1)],
                      s_yst[yb][:],
                      waits=[(dvo, p.val(dvo))] + gw, inc=f"sta{yb}",
                      accum_op=(ALU.add if g > 0 else ALU.bypass))
        sta_grp[g] = (p.val("sta0"), p.val("sta1"))
    p.barrier()

    s_yc = [env["ln_xc"][i][:] for i in range(2)]
    s_xc3 = [s_W[i].bitcast(F32)[:, 0:2048] for i in range(2)]
    s_sq2 = env["ln_sq"]
    xltf = xlt[:].bitcast(F32)
    s_o2 = [xltf[:, 2048 * i:2048 * (i + 1)] for i in range(2)]
    smz = {k: sb(f"fz_{k}", [128, 1], F32) for k in ("ssq", "t2e", "lnv", "rstd")}
    sta_totals = [p.val("sta0"), p.val("sta1")]
    for tc in range(16):
        b2 = tc % 2
        p.dma("sync", s_yc[b2], yd[128 * tc:128 * (tc + 1), :],
              waits=[("fch", p.val("fch") - 5), ("sta0", sta_totals[0]),
                     ("sta1", sta_totals[1])], inc=f"ldy{b2}")
        p.dma("sync", s_xc3[b2], x1[128 * tc:128 * (tc + 1), :],
              waits=[("fch", p.val("fch") - 5)], inc=f"ldy{b2}")
        ldv = p.val(f"ldy{b2}")
        p.emit("scalar", lambda e, b2=b2: e.activation(s_sq2[:], s_yc[b2], AF.Square,
                                                       accum_out=smz["ssq"][:]),
               waits=[(f"ldy{b2}", ldv), ("fch", p.val("fch"))], inc="fch")
        p.emit("vector", lambda e: e.tensor_scalar(smz["t2e"][:], smz["ssq"][:],
                                                   1.0 / D, 1e-6, ALU.mult, ALU.add),
               waits=[("fch", p.val("fch"))], inc="fch")
        p.emit("scalar", lambda e: e.activation(smz["lnv"][:], smz["t2e"][:], AF.Ln),
               waits=[("fch", p.val("fch"))], inc="fch")
        p.emit("scalar", lambda e: e.activation(smz["rstd"][:], smz["lnv"][:], AF.Exp,
                                                scale=-0.5),
               waits=[("fch", p.val("fch"))], inc="fch")
        p.emit("vector", lambda e, b2=b2: e.scalar_tensor_tensor(
            s_o2[b2], s_yc[b2], smz["rstd"][:], s_xc3[b2], ALU.mult, ALU.add),
            waits=[("fch", p.val("fch")), (f"stz{b2}", p.val(f"stz{b2}"))], inc="fch")
        p.dma("sync", x2[128 * tc:128 * (tc + 1), :], s_o2[b2],
              waits=[("fch", p.val("fch"))], inc=f"stz{b2}")

    return [("stz0", p.val("stz0")), ("stz1", p.val("stz1"))]


def common_identf(p, env):
    sb = env["sb"]
    identf = sb("c_identf", [128, 128], F32)
    env["identf"] = identf
    p.emit("gpsimd", lambda e: e.memset(identf[:], 0.0),
           waits=[("init", p.val("init"))], inc="init")
    p.emit("gpsimd", lambda e: e.affine_select(
        out=identf[:], in_=identf[:], compare_op=ALU.not_equal, fill=1.0,
        base=0, pattern=[[-1, 128]], channel_multiplier=1),
        waits=[("init", p.val("init"))], inc="init")


def emit_bias_T(p, env, bvec, s_bh, s_out, tag):
    pbk = env["pb"][1]
    p.dma("sync", s_bh[:], bvec.rearrange("(c p) -> c p", p=128),
          waits=[(tag + "bt", p.val(tag + "bt"))], inc=tag + "bl")
    p.emit("tensor", lambda e: e.transpose(pbk[:, 0:16], s_bh[:], env["identf"][0:16, 0:16]),
           waits=[(tag + "bl", p.val(tag + "bl")), (tag + "bt", p.val(tag + "bt"))],
           inc=tag + "bp")
    p.emit("vector", lambda e: e.tensor_copy(s_out[:], pbk[:, 0:16]),
           waits=[(tag + "bp", p.val(tag + "bp"))], inc=tag + "bt")


def emit_rope_tables(p, env, pos, s_sinf, s_cosf):
    sb = env["sb"]
    pb = env["pb"]
    s_dr = sb("rp_dr", [1, 128], I32)
    s_jr = sb("rp_jr", [1, 128], I32)
    s_jf = sb("rp_jf", [1, 128], F32)
    s_if = sb("rp_if", [1, 128], F32)
    s_pos = sb("rp_pos", [1, T], F32)
    s_u = sb("rp_u", [128, 512], F32)
    s_i = sb("rp_i", [128, 512], I32)
    s_f = sb("rp_f", [128, 512], F32)
    s_f2 = sb("rp_f2", [128, 512], F32)
    rc = p.sem("rc")

    def chain(eng, fn, extra=()):
        p.emit(eng, fn, waits=[("rc", p.val("rc"))] + list(extra), inc="rc")

    p.dma("sync", s_pos[:], pos, inc="rl")
    rl = p.val("rl")
    chain("gpsimd", lambda e: e.iota(s_dr[:], pattern=[[1, 128]], base=0, channel_multiplier=0))
    chain("vector", lambda e: e.tensor_scalar(s_jr[:], s_dr[:], 1, None, ALU.logical_shift_right))
    chain("vector", lambda e: e.tensor_copy(s_jf[:], s_jr[:]))
    LN1E4 = float(np.log(10000.0))
    chain("scalar", lambda e: e.activation(s_if[:], s_jf[:], AF.Exp, scale=-2.0 * LN1E4 / 128.0))
    INV2PI = float(1.0 / (2.0 * np.pi))
    TWOPI = float(2.0 * np.pi)
    for tg in range(4):
        chain("tensor", lambda e, tg=tg: e.matmul(
            pb[2][:], s_if[:], s_pos[:, 512 * tg:512 * (tg + 1)], start=True, stop=True),
            extra=[("rl", rl)])
        for phase, dst in ((0.0, s_sinf), (0.25, s_cosf)):
            chain("vector", lambda e, ph=phase: e.tensor_scalar(
                s_u[:], pb[2][:], INV2PI, ph, ALU.mult, ALU.add))
            chain("vector", lambda e: e.tensor_copy(s_i[:], s_u[:]))
            chain("vector", lambda e: e.tensor_copy(s_f[:], s_i[:]))
            chain("vector", lambda e: e.tensor_sub(s_f2[:], s_u[:], s_f[:]))
            chain("scalar", lambda e, dst=dst, tg=tg: e.activation(
                dst[:, 512 * tg:512 * (tg + 1)], s_f2[:], AF.Sin, scale=TWOPI))


def emit_psw(p, env, s_psw):
    sb = env["sb"]
    s_i1 = sb("pw_i1", [128, 128], F32)
    s_c1 = sb("pw_c1", [128, 128], F32)
    s_c2 = sb("pw_c2", [128, 128], F32)
    s_pi = sb("pw_pi", [128, 1], I32)
    s_pk = sb("pw_pk", [128, 1], I32)
    s_pkf = sb("pw_pkf", [128, 1], F32)
    s_ev = sb("pw_ev", [128, 1], F32)

    def chain(eng, fn):
        p.emit(eng, fn, waits=[("rc", p.val("rc"))], inc="rc")
    chain("gpsimd", lambda e: e.iota(s_i1[:], pattern=[[1, 128]], base=0, channel_multiplier=-1, allow_small_or_imprecise_dtypes=True))
    chain("vector", lambda e: e.tensor_single_scalar(s_c1[:], s_i1[:], 1.0, ALU.is_equal))
    chain("vector", lambda e: e.tensor_single_scalar(s_c2[:], s_i1[:], -1.0, ALU.is_equal))
    chain("gpsimd", lambda e: e.iota(s_pi[:], pattern=[[0, 1]], base=0, channel_multiplier=1))
    chain("vector", lambda e: e.tensor_single_scalar(s_pk[:], s_pi[:], 1, ALU.bitwise_and))
    chain("vector", lambda e: e.tensor_copy(s_pkf[:], s_pk[:]))
    chain("vector", lambda e: e.tensor_scalar(s_ev[:], s_pkf[:], -1.0, 1.0, ALU.mult, ALU.add))
    chain("vector", lambda e: e.tensor_scalar(s_c1[:], s_c1[:], s_ev[:], None, ALU.mult))
    chain("vector", lambda e: e.tensor_scalar(s_c2[:], s_c2[:], s_pkf[:], None, ALU.mult))
    chain("vector", lambda e: e.tensor_sub(s_psw[:], s_c1[:], s_c2[:]))


def emit_k3a(p, env):
    nc = env["nc"]; sb = env["sb"]; pb = env["pb"]; io = env["io"]
    x2, wq, wk, wv = io["x2"], io["wq"], io["wk"], io["wv"]
    bq, bk, bv, pos = io["bq"], io["bk"], io["bv"], io["pos"]
    qT, kT, vv = io["qT"], io["kT"], io["vv"]
    xlt = sb("xlt", [128, 16 * T], BF16)
    emit_ln(p, env, x2, xlt, "ln")
    p.emit("tensor", lambda e: e.nop(), waits=[("lndev", p.val("lndev"))])
    s_sinf = sb("s_sinf", [128, T], F32)
    s_cosf = sb("s_cosf", [128, T], F32)
    emit_rope_tables(p, env, pos, s_sinf, s_cosf)
    s_psw = sb("s_psw", [128, 128], F32)
    emit_psw(p, env, s_psw)
    s_pswb = sb("s_pswb", [128, 128], BF16)
    p.emit("vector", lambda e: e.tensor_copy(s_pswb[:], s_psw[:]),
           waits=[("rc", p.val("rc"))], inc="rc")
    s_bqT = sb("s_bqT", [128, 16], F32)
    s_bkT = sb("s_bkT", [128, 16], F32)
    s_bvR = sb("s_bvR", [1, D], F32)
    s_bvB = sb("s_bvB", [128, D], F32)
    s_bh = sb("s_bh", [16, 128], F32)
    emit_bias_T(p, env, bq, s_bh, s_bqT, "q")
    emit_bias_T(p, env, bk, s_bh, s_bkT, "k")
    p.dma("sync", s_bvR[:], bv.rearrange("(a d) -> a d", a=1), inc="rlb")
    rlv = p.val("rlb")
    for tg in range(4):
        p.emit("tensor", lambda e, tg=tg: e.matmul(
            pb[3][:], env["ones_row"][:], s_bvR[:, 512 * tg:512 * (tg + 1)],
            start=True, stop=True),
            waits=[("rlb", rlv), ("rc", p.val("rc"))], inc="rc")
        p.emit("vector", lambda e, tg=tg: e.tensor_copy(
            s_bvB[:, 512 * tg:512 * (tg + 1)], pb[3][:]),
            waits=[("rc", p.val("rc"))], inc="rc")
    p.barrier()

    s_w = [sb(f"wslab{i}", [128, 16 * 512], BF16) for i in range(2)]
    s_qs = [sb(f"s_qs{i}", [128, 512], BF16) for i in range(2)]
    s_m1 = sb("s_m1", [128, 512], F32)
    s_m2 = sb("s_m2", [128, 512], F32)
    s_qo = [sb(f"s_qo{i}", [128, 512], BF16) for i in range(2)]
    pes = p.sem("pes")
    aev = p.sem("aev")
    per = p.sem("per")
    dvr = p.sem("dvr")

    for mat, (wmat, bT, dst) in enumerate(((wq, s_bqT, qT), (wk, s_bkT, kT))):
        for og in range(4):
            wb = og % 2
            p.dma("gpsimd", s_w[wb][:],
                  wmat[:, 512 * og:512 * (og + 1)].rearrange("(ic p) o -> p ic o", p=128),
                  waits=[(pes, p.val(pes) - 16)], inc=f"ldw{wb}")
            ldv = p.val(f"ldw{wb}")
            for oc in range(4):
                h = 4 * og + oc
                for tg in range(4):
                    i = (h * 4 + tg) % 2
                    bank = pb[4 + tg % 2]
                    for ic in range(16):
                        p.emit("tensor", lambda e, wb=wb, ic=ic, oc=oc, tg=tg, bank=bank: e.matmul(
                            bank[:], s_w[wb][:, 512 * ic + 128 * oc:512 * ic + 128 * (oc + 1)],
                            xlt[:, 2048 * ic + 512 * tg:2048 * ic + 512 * (tg + 1)],
                            start=(ic == 0), stop=(ic == 15)),
                            waits=([(f"ldw{wb}", ldv), (aev, p.val(aev) - 1)] if ic == 0 else []),
                            inc=("pes" if ic == 15 else None))
                    p.emit("scalar", lambda e, i=i, h=h, bank=bank, bT=bT: e.activation(
                        s_qs[i][:], bank[:], AF.Identity, bias=bT[:, h:h + 1]),
                        waits=[(pes, p.val(pes)), (per, p.val(per) - 1),
                               (dvr, p.val(dvr) - 3)], inc="aev")
                    p.emit("tensor", lambda e, i=i: e.matmul(
                        pb[6][:], s_pswb[:], s_qs[i][:], start=True, stop=True),
                        waits=[(aev, p.val(aev)), (dvr, p.val(dvr) - 1)], inc="per")
                    p.emit("vector", lambda e, i=i, tg=tg: e.tensor_mul(
                        s_m1[:], s_qs[i][:], s_cosf[:, 512 * tg:512 * (tg + 1)]),
                        waits=[(aev, p.val(aev))])
                    p.emit("vector", lambda e, tg=tg: e.tensor_mul(
                        s_m2[:], pb[6][:], s_sinf[:, 512 * tg:512 * (tg + 1)]),
                        waits=[(per, p.val(per))], inc="dvr")
                    p.emit("vector", lambda e, i=i: e.tensor_add(s_qo[i][:], s_m1[:], s_m2[:]),
                           waits=[(f"stq{i}", p.val(f"stq{i}"))], inc="dvr")
                    p.dma("scalar", dst[128 * h:128 * (h + 1), 512 * tg:512 * (tg + 1)],
                          s_qo[i][:], waits=[(dvr, p.val(dvr))], inc=f"stq{i}")
    p.barrier()

    s_vo = [sb(f"s_vo{i}", [128, 512], BF16) for i in range(2)]
    for og in range(4):
        wb = og % 2
        p.dma("gpsimd", s_w[wb][:],
              wv[:, 512 * og:512 * (og + 1)].rearrange("(ic p) o -> p ic o", p=128),
              waits=[(pes, p.val(pes) - 16)], inc=f"ldw{wb}")
        ldv = p.val(f"ldw{wb}")
        for tc in range(16):
            i = (og * 16 + tc) % 2
            bank = pb[4 + tc % 2]
            for ic in range(16):
                p.emit("tensor", lambda e, wb=wb, ic=ic, tc=tc, bank=bank: e.matmul(
                    bank[:], xlt[:, 2048 * ic + 128 * tc:2048 * ic + 128 * (tc + 1)],
                    s_w[wb][:, 512 * ic:512 * (ic + 1)],
                    start=(ic == 0), stop=(ic == 15)),
                    waits=([(f"ldw{wb}", ldv), ("dvv", p.val("dvv") - 1)] if ic == 0 else []),
                    inc=("pes" if ic == 15 else None))
            p.emit("vector", lambda e, i=i, og=og, bank=bank: e.tensor_add(
                s_vo[i][:], bank[:], s_bvB[:, 512 * og:512 * (og + 1)]),
                waits=[(pes, p.val(pes)), (f"stq{i}", p.val(f"stq{i}"))], inc="dvv")
            p.dma("scalar", vv[128 * tc:128 * (tc + 1), 512 * og:512 * (og + 1)],
                  s_vo[i][:], waits=[("dvv", p.val("dvv"))], inc=f"stq{i}")

    return [("stq0", p.val("stq0")), ("stq1", p.val("stq1"))]


def emit_k3b(p, env):
    nc = env["nc"]; sb = env["sb"]; pb = env["pb"]; io = env["io"]
    qTd, kg, vg = io["qT2"], io["kg"], io["vg"]
    wo, bo, x2, x3 = io["wo"], io["bo"], io["x2b"], io["x3"]
    s_boR = sb("s_boR", [1, D], F32)
    s_boB = sb("s_boB", [128, D], F32)
    p.dma("sync", s_boR[:], bo.rearrange("(a d) -> a d", a=1), inc="rl")
    rlv = p.val("rl")
    for tg in range(4):
        p.emit("tensor", lambda e, tg=tg: e.matmul(
            pb[3][:], env["ones_row"][:], s_boR[:, 512 * tg:512 * (tg + 1)],
            start=True, stop=True),
            waits=[("rl", rlv), ("rc", p.val("rc"))], inc="rc")
        p.emit("vector", lambda e, tg=tg: e.tensor_copy(
            s_boB[:, 512 * tg:512 * (tg + 1)], pb[3][:]),
            waits=[("rc", p.val("rc"))], inc="rc")
    p.barrier()

    s_kh = [sb(f"s_kh{i}", [128, 4 * T], BF16) for i in range(2)]
    s_vh = [sb(f"s_vh{i}", [128, 4 * T], BF16) for i in range(2)]
    s_qt = [sb(f"s_qt{i}", [128, 512], BF16) for i in range(2)]
    s_pt = [sb(f"s_pt{i}", [128, 512], BF16) for i in range(4)]
    s_rs = sb("s_rs", [1, 512], F32)
    s_rsb = sb("s_rsb", [128, 512], F32)
    s_oall = sb("s_oall", [128, 16 * T], BF16)
    pes = p.sem("pes")
    ace = p.sem("ace")
    pev = p.sem("pev")
    aev = p.sem("aev")
    dvn = p.sem("dvn")

    for h in range(16):
        hb = h % 2
        for r in range(4):
            p.dma("gpsimd", s_kh[hb][:, 2048 * r:2048 * (r + 1)],
                  kg[2048 * r + 128 * h:2048 * r + 128 * (h + 1), :],
                  waits=[(pev, p.val(pev) - 512)] if h >= 2 else [], inc=f"ldk{hb}")
            p.dma("gpsimd", s_vh[hb][:, 2048 * r:2048 * (r + 1)],
                  vg[2048 * r:2048 * (r + 1), 128 * h:128 * (h + 1)].rearrange(
                      "(tc p) d -> p tc d", p=128),
                  inc=f"ldk{hb}")
        ldkv = p.val(f"ldk{hb}")
        sB = [pb[0], pb[1], pb[5], pb[6]]
        LAG = 3
        for tg in range(4):
            qb = (h * 4 + tg) % 2
            p.dma("sync", s_qt[qb][:], qTd[128 * h:128 * (h + 1), 512 * tg:512 * (tg + 1)],
                  waits=[(pes, p.val(pes) - 66)], inc=f"ldq{qb}")
            ldqv = p.val(f"ldq{qb}")
            exp_of = {}
            pv_of = {}

            def emit_pv(kc, hb=hb):
                ptb = s_pt[kc % 4]
                p.emit("tensor", lambda e, kc=kc, hb=hb, ptb=ptb: e.matmul(
                    pb[2][:], s_vh[hb][:, 128 * kc:128 * (kc + 1)], ptb[:],
                    start=(kc == 0), stop=(kc == 63), skip_group_check=True),
                    waits=[(ace, exp_of[kc])] + ([(dvn, p.val(dvn))] if kc == 0 else []),
                    inc="pev")
                p.emit("tensor", lambda e, kc=kc, ptb=ptb: e.matmul(
                    pb[3][0:1, :], env["ones_col"][:], ptb[:],
                    start=(kc == 0), stop=(kc == 63), skip_group_check=True),
                    inc="pev")
                pv_of[kc] = p.val(pev)

            for kc in range(64):
                sbank = sB[kc % 4]
                ptb = s_pt[kc % 4]
                p.emit("tensor", lambda e, kc=kc, hb=hb, qb=qb, sbank=sbank: e.matmul(
                    sbank[:], s_kh[hb][:, 128 * kc:128 * (kc + 1)], s_qt[qb][:],
                    start=True, stop=True, skip_group_check=True),
                    waits=([(f"ldk{hb}", ldkv), (f"ldq{qb}", ldqv), (ace, p.val(ace) - 1),
                            (dvn, p.val(dvn))] if kc <= 3 else
                           [(ace, exp_of.get(kc - 4, 0))]),
                    inc="pes")
                qk_done = p.val(pes)
                p.emit("scalar", lambda e, ptb=ptb, sbank=sbank: e.activation(
                    ptb[:], sbank[:], AF.Exp, scale=INV_SQRT_HD),
                    waits=[(pes, qk_done), (pev, pv_of.get(kc - 4, 0))], inc="ace")
                exp_of[kc] = p.val(ace)
                if kc >= LAG:
                    emit_pv(kc - LAG)
            for kc in range(64 - LAG, 64):
                emit_pv(kc)
            p.emit("vector", lambda e: e.reciprocal(s_rs[:], pb[3][0:1, :]),
                   waits=[(pev, p.val(pev))], inc="dvn")
            p.emit("tensor", lambda e: e.matmul(pb[4][:], env["ones_row"][:], s_rs[:],
                                                start=True, stop=True),
                   waits=[(dvn, p.val(dvn)), (aev, p.val(aev))], inc="pes")
            p.emit("scalar", lambda e: e.activation(s_rsb[:], pb[4][:], AF.Copy),
                   waits=[(pes, p.val(pes))], inc="aev")
            p.emit("vector", lambda e, h=h, tg=tg: e.tensor_mul(
                s_oall[:, 2048 * h + 512 * tg:2048 * h + 512 * (tg + 1)], pb[2][:], s_rsb[:]),
                   waits=[(aev, p.val(aev))], inc="dvn")
    p.barrier()

    s_w = [sb(f"wslab{i}", [128, 16 * 512], BF16) for i in range(2)]
    s_xcs = [sb(f"s_xcs{i}", [128, 512], F32) for i in range(2)]
    s_t1 = sb("s_t1", [128, 512], F32)
    s_ys = [sb(f"s_ys{i}", [128, 512], F32) for i in range(2)]
    dvy = p.sem("dvy")
    for og in range(4):
        wb = og % 2
        p.dma("gpsimd", s_w[wb][:],
              wo[:, 512 * og:512 * (og + 1)].rearrange("(ic p) o -> p ic o", p=128),
              waits=[(dvy, p.val(dvy) - 16)] if og >= 2 else [], inc=f"ldw{wb}")
        ldv = p.val(f"ldw{wb}")
        for tc in range(16):
            i = (og * 16 + tc) % 2
            bank = pb[4 + tc % 2]
            p.dma("sync", s_xcs[i][:],
                  x2[128 * tc:128 * (tc + 1), 512 * og:512 * (og + 1)],
                  waits=[(dvy, p.val(dvy) - 1)], inc=f"ldx{i}")
            ldxv = p.val(f"ldx{i}")
            for ec in range(16):
                p.emit("tensor", lambda e, wb=wb, ec=ec, tc=tc, bank=bank: e.matmul(
                    bank[:], s_oall[:, 2048 * ec + 128 * tc:2048 * ec + 128 * (tc + 1)],
                    s_w[wb][:, 512 * ec:512 * (ec + 1)],
                    start=(ec == 0), stop=(ec == 15)),
                    waits=([(f"ldw{wb}", ldv), (dvy, p.val(dvy))] if ec == 0 else []),
                    inc=("pes" if ec == 15 else None))
            p.emit("vector", lambda e, og=og, bank=bank: e.tensor_add(
                s_t1[:], bank[:], s_boB[:, 512 * og:512 * (og + 1)]),
                waits=[(pes, p.val(pes)), (f"ldx{i}", ldxv),
                       (f"sty{i}", p.val(f"sty{i}"))])
            p.emit("vector", lambda e, i=i: e.tensor_add(s_ys[i][:], s_t1[:], s_xcs[i][:]),
                   inc="dvy")
            p.dma("sync", x3[128 * tc:128 * (tc + 1), 512 * og:512 * (og + 1)],
                  s_ys[i][:], waits=[(dvy, p.val(dvy))], inc=f"sty{i}")

    return [("sty0", p.val("sty0")), ("sty1", p.val("sty1"))]


def emit_dec_nvfp4(p, env, u8v4, nC, s_W, scb, wv_maker, waits_first, inc_last, dec):
    u8c, u16, k_, sh_, b_, t_, vp_, v_ = (dec[n] for n in
                                          ("u8c", "u16", "k", "sh", "b", "t", "vp", "v"))
    p.emit("vector", lambda e: e.tensor_copy(u8c[:, :nC], u8v4[:, 0::4]), waits=waits_first)
    p.emit("vector", lambda e: e.tensor_copy(u16[:, :nC], u8c[:, :nC]))
    for (kshift, sshift), par in (((0, 12), 0), ((4, 8), 1)):
        if kshift == 0:
            p.emit("vector", lambda e: e.tensor_scalar(k_[:, :nC], u16[:, :nC], 7, None, ALU.bitwise_and))
        else:
            p.emit("vector", lambda e, ks=kshift: e.tensor_scalar(
                k_[:, :nC], u16[:, :nC], ks, 7, ALU.logical_shift_right, ALU.bitwise_and))
        p.emit("vector", lambda e, ss=sshift: e.tensor_scalar(
            sh_[:, :nC], u16[:, :nC], ss, 0x8000, ALU.logical_shift_left, ALU.bitwise_and))
        p.emit("vector", lambda e: e.tensor_scalar(b_[:, :nC], k_[:, :nC], 64.0, 16128.0, ALU.mult, ALU.add))
        p.emit("vector", lambda e: e.tensor_scalar(t_[:, :nC], b_[:, :nC].bitcast(BF16), 2.0, -1.0, ALU.mult, ALU.add))
        p.emit("vector", lambda e: e.tensor_tensor(vp_[:, :nC], t_[:, :nC], b_[:, :nC].bitcast(BF16), ALU.min))
        p.emit("vector", lambda e: e.tensor_tensor(v_[:, :nC].bitcast(U16), vp_[:, :nC].bitcast(U16), sh_[:, :nC], ALU.bitwise_xor))
        wv, ng = wv_maker(par)
        vv_ = v_[:, :nC].rearrange("p (g k) -> p g k", g=ng)
        p.emit("vector", lambda e, wv=wv, vv_=vv_: e.tensor_tensor(wv, vv_, scb, ALU.mult),
               inc=(inc_last if par == 1 else None))


def emit_k5(p, env):
    nc = env["nc"]; sb = env["sb"]; pb = env["pb"]; io = env["io"]
    x3, out, wdT = io["x3b"], io["out"], io["wdT1"]
    gWp, gs, uWp, us, dWp, dsc = (io["gWp"], io["gs"], io["uWp"], io["us"],
                                  io["dWp"], io["dsc"])
    xlt = sb("xlt", [128, 16 * T], BF16)
    emit_ln(p, env, x3, xlt, "ln")
    p.emit("tensor", lambda e: e.nop(), waits=[("lndev", p.val("lndev"))])
    p.dma("gpsimd", out, x3, inc="sti")
    sti_v = p.val("sti")

    dec = {}
    dec["u8c"] = sb("dec_u8c", [128, 1024], U8)
    for n_ in ("u16", "k", "sh", "b"):
        dec[n_] = sb(f"dec_{n_}", [128, 1024], U16)
    for n_ in ("t", "vp", "v"):
        dec[n_] = sb(f"dec_{n_}", [128, 1024], BF16)
    s_pk = [sb(f"s_pk{i}", [128, 1024], I32) for i in range(2)]
    s_sc = [sb(f"s_sc{i}", [128, 64], F32) for i in range(2)]
    s_W = [sb(f"s_W{i}", [128, 2048], BF16) for i in range(2)]
    s_wT = [sb(f"s_wT{i}", [128, 2048], BF16) for i in range(2)]
    s_gg = sb("s_gg", [128, 2048], BF16)
    s_yst = [sb(f"s_yst{i}", [128, 512], F32) for i in range(2)]
    pk5 = [sb("pk5_0", [128, 1024], I32)]
    sc5 = [sb("sc5_0", [128, 64], F32)]
    W5 = [sb("W5_0", [128, 2048], BF16)]
    st5 = [sb(f"st5_{i}", [128, 512], BF16) for i in range(2)]
    s_h = [sb(f"s_h{i}", [128, 2048], BF16) for i in range(12)]
    _xc = env["ln_xc"]; _sq = env["ln_sq"]; _tl = env["ln_tln"]
    s_wd = [
        _xc[0][:].bitcast(BF16)[:, 0:2048], _xc[0][:].bitcast(BF16)[:, 2048:4096],
        _xc[1][:].bitcast(BF16)[:, 0:2048], _xc[1][:].bitcast(BF16)[:, 2048:4096],
        _sq[:].bitcast(BF16)[:, 0:2048], _sq[:].bitcast(BF16)[:, 2048:4096],
        _tl[0][:], _tl[1][:],
    ]
    dvd = p.sem("dvd")
    pet = p.sem("pet")
    dve = p.sem("dve")
    ld5 = p.sem("ld5")
    dvd5 = p.sem("dvd5")
    pet5 = p.sem("pet5")
    pem = p.sem("pem")
    dvg = p.sem("dvg")
    ach = p.sem("ach")
    dvh = p.sem("dvh")
    ldd = p.sem("ldd")
    pdn = p.sem("pdn")
    dvo = p.sem("dvo")

    NCH = 256
    HRING = 12

    sa_stw = {}
    dcv5 = {}
    tb = {"j": 0, "ev": {-2: 0, -1: 0}}

    def tgrp_start_waits(deps):
        j = tb["j"]
        return deps + [("dve", tb["ev"][j - 2])]

    def tgrp_done(evv):
        tb["ev"][tb["j"]] = evv
        tb["j"] += 1

    def emit_5a_load(i):
        cc, f = divmod(i, 16)
        p.dma("sync", pk5[0][:], dWp[128 * f:128 * (f + 1), 1024 * cc:1024 * (cc + 1)],
              waits=[(dvd5, p.val(dvd5))], inc="ld5")
        p.dma("sync", sc5[0][:], dsc[128 * f:128 * (f + 1), 64 * cc:64 * (cc + 1)],
              inc="ld5")

    def emit_5a_dec(i):
        ldv = p.val(ld5)
        scb = sc5[0][:].unsqueeze(2).broadcast_to([128, 64, 16])

        def wv_mk(par):
            return W5[0][:, par::2].rearrange("p (g k) -> p g k", g=64), 64
        emit_dec_nvfp4(p, env, pk5[0][:].bitcast(U8), 1024, W5[0], scb, wv_mk,
                       [(ld5, ldv), (pet5, p.val(pet5))], "dvd5", dec)
        dcv5[i] = p.val(dvd5)
        if i + 1 < 128:
            emit_5a_load(i + 1)

    def emit_5a_T(i):
        cc, f = divmod(i, 16)
        dcv = dcv5[i]
        for g in range(4):
            bk = pb[tb["j"] % 2]
            for k in range(4):
                c = 4 * g + k
                p.emit("tensor", lambda e, c=c, k=k, bk=bk: e.transpose(
                    bk.bitcast(BF16)[:, 128 * k:128 * (k + 1)],
                    W5[0][:, 128 * c:128 * (c + 1)], env["ident"][:]),
                    waits=(tgrp_start_waits([(dvd5, dcv)]) if k == 0 else []),
                    inc=("pet5" if k == 3 else None))
            sb_i = g % 2
            p.emit("vector", lambda e, sb_i=sb_i, bk=bk: e.tensor_copy(
                st5[sb_i][:], bk.bitcast(BF16)[:, 0:512]),
                waits=[(pet5, p.val(pet5)), (f"stw{sb_i}", p.val(f"stw{sb_i}"))],
                inc="dve")
            tgrp_done(p.val("dve"))
            dst = wdT[2048 * cc + 512 * g:2048 * cc + 512 * (g + 1),
                      128 * f:128 * (f + 1)].rearrange("(t p) q -> p t q", p=128)
            srcv = st5[sb_i][:].rearrange("p (t q) -> p t q", t=4)
            p.dma("scalar", dst, srcv, waits=[("dve", p.val("dve"))], inc=f"stw{sb_i}")
        if f == 15:
            sa_stw[cc] = (p.val("stw0"), p.val("stw1"))

    def emit_pk_load(n):
        c, half = divmod(n, 2)
        Wp, sc = (gWp, gs) if half == 0 else (uWp, us)
        b = n % 2
        p.dma("gpsimd", s_pk[b][:], Wp[128 * c:128 * (c + 1), :],
              waits=[(dvd, p.val(dvd) - 1)], inc=f"ldp{b}")
        p.dma("gpsimd", s_sc[b][:], sc[128 * c:128 * (c + 1), :], inc=f"ldp{b}")

    def emit_dec(n):
        b = n % 2
        ldv = ld_of[n]
        scb = s_sc[b][:].unsqueeze(2).broadcast_to([128, 64, 16])

        def wv_mk(par, b=b):
            return s_W[b][:, par::2].rearrange("p (g k) -> p g k", g=64), 64
        emit_dec_nvfp4(p, env, s_pk[b][:].bitcast(U8), 1024, s_W[b], scb, wv_mk,
                       [(f"ldp{b}", ldv), (pet, p.val(pet))], "dvd", dec)
        dec_of[n] = p.val(dvd)

    def emit_T(n):
        b = n % 2
        for grp in range(4):
            bk = pb[tb["j"] % 2]
            for k in range(4):
                ic = 4 * grp + k
                p.emit("tensor", lambda e, b=b, ic=ic, k=k, bk=bk: e.transpose(
                    bk.bitcast(BF16)[:, 128 * k:128 * (k + 1)],
                    s_W[b][:, 128 * ic:128 * (ic + 1)], env["ident"][:]),
                    waits=(tgrp_start_waits([(dvd, dec_of[n])]) if k == 0 else []),
                    inc=("pet" if k == 3 else None))
            p.emit("vector", lambda e, b=b, grp=grp, bk=bk: e.tensor_copy(
                s_wT[b][:].rearrange("p (ic q) -> p ic q", ic=16)[:, 4 * grp:4 * grp + 4, :],
                bk.bitcast(BF16)[:, 0:512].rearrange("p (t q) -> p t q", t=4)),
                waits=[(pet, p.val(pet)), (pem, p.val(pem) - 4)], inc="dve")
            tgrp_done(p.val("dve"))
        T_of[n] = p.val(dve)

    down_done = {-1: 0}
    h_of = {}
    bank_free = {tg: 0 for tg in range(4)}

    def emit_M(n):
        c, half = divmod(n, 2)
        b = n % 2
        slot = c % HRING
        for tg in range(4):
            bank = pb[4 + tg]
            for ic in range(16):
                w = []
                if ic == 0:
                    w = ([(dve, T_of[n]), bank_free[tg]] if bank_free[tg]
                         else [(dve, T_of[n])])
                p.emit("tensor", lambda e, b=b, ic=ic, tg=tg, bank=bank: e.matmul(
                    bank[:], s_wT[b][:, 128 * ic:128 * (ic + 1)],
                    xlt[:, 2048 * ic + 512 * tg:2048 * ic + 512 * (tg + 1)],
                    start=(ic == 0), stop=(ic == 15)),
                    waits=w, inc=("pem" if ic == 15 else None))
            pemv = p.val(pem)
            if half == 0:
                p.emit("vector", lambda e, tg=tg, bank=bank: e.tensor_copy(
                    s_gg[:, 512 * tg:512 * (tg + 1)], bank[:]),
                    waits=[(pem, pemv)], inc="dvg")
                bank_free[tg] = ("dvg", p.val(dvg))
                p.emit("scalar", lambda e, tg=tg: e.activation(
                    s_gg[:, 512 * tg:512 * (tg + 1)],
                    s_gg[:, 512 * tg:512 * (tg + 1)], AF.Gelu_apprx_tanh),
                    waits=[("dvg", p.val(dvg))], inc="ach")
            else:
                p.emit("vector", lambda e, tg=tg, slot=slot, bank=bank: e.tensor_mul(
                    s_h[slot][:, 512 * tg:512 * (tg + 1)],
                    s_gg[:, 512 * tg:512 * (tg + 1)], bank[:]),
                    waits=[(pem, pemv), (ach, p.val(ach)),
                           (pdn, down_done.get((c - HRING) // 8, 0))],
                    inc="dvh")
                bank_free[tg] = ("dvh", p.val(dvh))
        if half == 1:
            h_of[c] = p.val(dvh)

    wd_loaded = {}

    def emit_wd_loads(g):
        for ci in range(8):
            c = 8 * g + ci
            p.dma("gpsimd", s_wd[ci], wdT[128 * c:128 * (c + 1), :],
                  waits=[("stw0", sa_stw[g // 2][0]), ("stw1", sa_stw[g // 2][1]),
                         (pdn, down_done[g - 1]),
                         ("lnch", p.val("lnch"))], inc="ldd")
        wd_loaded[g] = p.val(ldd)

    def emit_down_regions(g, rr):
        hg_done = h_of[8 * g + 7]
        for r in rr:
            tc, fg = divmod(r, 4)
            bank = pb[2 + r % 2]
            for ci in range(8):
                slot = (8 * g + ci) % HRING
                p.emit("tensor", lambda e, ci=ci, tc=tc, fg=fg, bank=bank, slot=slot: e.matmul(
                    bank[:], s_h[slot][:, 128 * tc:128 * (tc + 1)],
                    s_wd[ci][:, 512 * fg:512 * (fg + 1)],
                    start=(ci == 0), stop=(ci == 7)),
                    waits=([(ldd, wd_loaded[g]), (dvh, hg_done),
                            (dvo, p.val(dvo) - 1)] if ci == 0 else []),
                    inc=("pdn" if ci == 7 else None))
            yb = r % 2
            p.emit("vector", lambda e, yb=yb, bank=bank: e.tensor_copy(
                s_yst[yb][:], bank[:]),
                waits=[(pdn, p.val(pdn)), (f"sta{yb}", p.val(f"sta{yb}"))], inc="dvo")
            gw = [(f"sta{yb}", sta_grp[g - 1][yb])] if g > 0 else []
            p.dma("gpsimd", out[128 * tc:128 * (tc + 1), 512 * fg:512 * (fg + 1)],
                  s_yst[yb][:],
                  waits=[(dvo, p.val(dvo)), ("sti", sti_v)] + gw, inc=f"sta{yb}",
                  accum_op=ALU.add)

    ld_of = {}
    dec_of = {}
    T_of = {}
    sta_grp = {}
    emit_5a_load(0)
    for i in range(32):
        emit_5a_dec(i)
        emit_5a_T(i)
    sa_next = 32
    emit_pk_load(0)
    ld_of[0] = p.val("ldp0")
    emit_dec(0)
    for n in range(NCH):
        if n + 1 < NCH:
            emit_pk_load(n + 1)
            ld_of[n + 1] = p.val(f"ldp{(n + 1) % 2}")
        emit_T(n)
        sa_this = sa_next if (n % 2 == 0 and sa_next < 128) else None
        if sa_this is not None:
            emit_5a_dec(sa_this)
            sa_next += 1
        if n + 1 < NCH:
            emit_dec(n + 1)
        emit_M(n)
        if sa_this is not None:
            emit_5a_T(sa_this)
        g = n // 16
        if n >= 16 and (n % 16) < 8:
            gprev = g - 1
            if n % 16 == 0:
                emit_wd_loads(gprev)
            emit_down_regions(gprev, range((n % 16) * 8, (n % 16) * 8 + 8))
            if n % 16 == 7:
                down_done[gprev] = p.val(pdn)
                sta_grp[gprev] = (p.val("sta0"), p.val("sta1"))
    emit_wd_loads(15)
    emit_down_regions(15, range(64))
    down_done[15] = p.val(pdn)
    sta_grp[15] = (p.val("sta0"), p.val("sta1"))
    return [("sta0", p.val("sta0")), ("sta1", p.val("sta1"))]


EI, EO, IN = "ExternalInput", "ExternalOutput", "Internal"

DECLS_K1 = [("xin", [T, D], F32, EI), ("wqkv", [D, 3 * D], F32, EI),
            ("wout", [D, D], F32, EI)]
DECLS_K2 = [("gup", [11264, 256], I32, EI), ("gus", [16, 11264], F16, EI),
            ("dp", [D, 704], I32, EI), ("ds", [44, D], F16, EI),
            ("wdT0", [5632, D], BF16, IN), ("yd", [T, D], F32, IN)]
DECLS_K3A = [("wq", [D, D], F32, EI), ("wk", [D, D], F32, EI), ("wv", [D, D], F32, EI),
             ("bq", [D], F32, EI), ("bk", [D], F32, EI), ("bv", [D], F32, EI),
             ("pos", [1, T], F32, EI), ("qT", [D, T], BF16, EO),
             ("kT", [D, T], BF16, EO), ("vv", [T, D], BF16, EO)]
DECLS_K3B = [("qT2", [D, T], BF16, EI), ("kg", [4 * D, T], BF16, EI),
             ("vg", [4 * T, D], BF16, EI), ("wo", [D, D], F32, EI), ("bo", [D], F32, EI)]
DECLS_K5 = [("gWp", [16384, 1024], I32, EI), ("gs", [16384, 64], F32, EI),
            ("uWp", [16384, 1024], I32, EI), ("us", [16384, 64], F32, EI),
            ("dWp", [D, 8192], I32, EI), ("dsc", [D, 512], F32, EI),
            ("wdT1", [16384, D], BF16, IN), ("out", [T, D], F32, EO)]


def build_launchA():
    decls = (DECLS_K1 + [("x1", [T, D], F32, IN)] + DECLS_K2 +
             [("x2", [T, D], F32, EO)] + DECLS_K3A)
    return build_launch([emit_k1, emit_k2, emit_k3a], decls)


def build_launchB():
    decls = (DECLS_K3B + [("x2b", [T, D], F32, EI), ("x3", [T, D], F32, IN)] + DECLS_K5)
    return build_launch([emit_k3b, emit_k5], decls, aliases={"x3b": "x3"})


def build_k1():
    return build_launch([emit_k1], DECLS_K1 + [("x1", [T, D], F32, EO)])


def build_k2():
    return build_launch([emit_k2], [("x1", [T, D], F32, EI)] + DECLS_K2 +
                        [("x2", [T, D], F32, EO)])


def build_k3a():
    return build_launch([emit_k3a], [("x2", [T, D], F32, EI)] + DECLS_K3A)


def build_k3b():
    return build_launch([emit_k3b], DECLS_K3B +
                        [("x2b", [T, D], F32, EI), ("x3", [T, D], F32, EO)])


def build_k5():
    return build_launch([emit_k5], [("x3b", [T, D], F32, EI)] + DECLS_K5)


def _shard_x(x):
    out = []
    for c in range(NCORES):
        b, s = c // 4, 2048 * (c % 4)
        out.append(np.ascontiguousarray(x[b, s:s + 2048, :], dtype=np.float32))
    return out


def kernel(**inputs):
    f32 = lambda k: np.ascontiguousarray(np.asarray(inputs[k], np.float32))
    i32 = lambda k: np.ascontiguousarray(np.asarray(inputs[k], np.int32))
    f16 = lambda k: np.ascontiguousarray(np.asarray(inputs[k], np.float16))

    xs = _shard_x(f32("L0_attn_x"))
    cores = list(range(NCORES))

    ncA = build_launchA()
    in_a = []
    for c in cores:
        s = 2048 * (c % 4)
        in_a.append({
            "xin": xs[c],
            "wqkv": f32("L0_attn_w_qkv"), "wout": f32("L0_attn_w_out"),
            "gup": i32("L0_ffn_gate_up_packed"), "gus": f16("L0_ffn_gate_up_scales"),
            "dp": i32("L0_ffn_down_packed"), "ds": f16("L0_ffn_down_scales"),
            "wq": f32("L1_attn_Wq"), "wk": f32("L1_attn_Wk"), "wv": f32("L1_attn_Wv"),
            "bq": f32("L1_attn_bq"), "bk": f32("L1_attn_bk"), "bv": f32("L1_attn_bv"),
            "pos": np.arange(s, s + 2048, dtype=np.float32).reshape(1, 2048),
        })
    rA = run_bass_kernel_spmd(ncA, in_a, core_ids=cores)
    x2 = [np.asarray(rA.results[c]["x2"]) for c in cores]
    qT = [np.asarray(rA.results[c]["qT"]) for c in cores]
    kT = [np.asarray(rA.results[c]["kT"]) for c in cores]
    vv = [np.asarray(rA.results[c]["vv"]) for c in cores]

    ncB = build_launchB()
    in_b = []
    for c in cores:
        b = c // 4
        grp = [4 * b + i for i in range(4)]
        in_b.append({
            "qT2": qT[c],
            "kg": np.ascontiguousarray(np.concatenate([kT[r] for r in grp], axis=0)),
            "vg": np.ascontiguousarray(np.concatenate([vv[r] for r in grp], axis=0)),
            "wo": f32("L1_attn_Wo"), "bo": f32("L1_attn_bo"), "x2b": x2[c],
            "gWp": i32("L1_ffn_gate_W_packed"), "gs": f32("L1_ffn_gate_scales"),
            "uWp": i32("L1_ffn_up_W_packed"), "us": f32("L1_ffn_up_scales"),
            "dWp": i32("L1_ffn_down_W_packed"), "dsc": f32("L1_ffn_down_scales"),
        })
    rB = run_bass_kernel_spmd(ncB, in_b, core_ids=cores)

    out = np.empty((2, 8192, 2048), np.float32)
    for c in cores:
        b, s = c // 4, 2048 * (c % 4)
        out[b, s:s + 2048, :] = rB.results[c]["out"]
    return out


# version 08 (final)
# speedup vs baseline: 1.5351x; 1.1084x over previous
import numpy as np
from contextlib import ExitStack

import concourse.bass as bass
import concourse.mybir as mybir
from concourse.bass_utils import run_bass_kernel_spmd

F32 = mybir.dt.float32
BF16 = mybir.dt.bfloat16
F16 = mybir.dt.float16
I32 = mybir.dt.int32
U16 = mybir.dt.uint16
U8 = mybir.dt.uint8
AF = mybir.ActivationFunctionType
ALU = mybir.AluOpType
AX = mybir.AxisListType

ENGINES = ("sync", "vector", "scalar", "tensor", "gpsimd")
T = 2048
D = 2048
NCORES = 8
INV_SQRT_HD = float(1.0 / np.sqrt(128.0))


class P:

    def __init__(self):
        self.steps = {e: [] for e in ENGINES}
        self.sems = {}
        self.order = []

    def sem(self, name):
        if name not in self.sems:
            self.sems[name] = 0
            self.order.append(name)
        return name

    def val(self, name):
        self.sem(name)
        return self.sems[name]

    def emit(self, eng, fn, waits=(), inc=None, inc_n=1):
        waits = tuple((s, v) for (s, v) in waits if v > 0)
        if inc is not None:
            self.sem(inc)
            self.sems[inc] += inc_n
        self.steps[eng].append((fn, waits, inc, inc_n))

    def dma(self, eng, out, in_, waits=(), inc=None, **kw):
        self.emit(eng, lambda e, out=out, in_=in_, kw=kw: e.dma_start(out=out, in_=in_, **kw),
                  waits=waits, inc=inc, inc_n=16)

    def barrier(self, engines=ENGINES):
        bar = self.sem("bar")
        for e in engines:
            self.emit(e, lambda eng: eng.nop(), inc="bar")
        tgt = self.sems["bar"]
        for e in engines:
            self.steps[e].append((None, ((bar, tgt),), None, 0))

    def finalize(self, nc, ctx, final_wait=None):
        handles = {n: ctx.enter_context(nc.semaphore(n)) for n in self.order}
        with nc.Block() as block:
            for eng_name in ENGINES:
                if not self.steps[eng_name] and eng_name != "sync":
                    continue

                def mk(eng_name=eng_name):
                    def body(eng):
                        for fn, waits, inc, inc_n in self.steps[eng_name]:
                            for s, v in waits:
                                eng.wait_ge(handles[s], v)
                            if fn is None:
                                continue
                            inst = fn(eng)
                            if inc:
                                inst.then_inc(handles[inc], inc_n)
                        if eng_name == "sync" and final_wait is not None:
                            eng.wait_ge(handles[final_wait[0]], final_wait[1])
                    return body
                getattr(block, eng_name)(mk())



SBUF_TOP = 229344


def make_carver(nc, base):
    state = {"off": base}

    def carve(name, shape, dtype):
        nbytes = int(np.prod(shape[1:])) * mybir.dt.size(dtype)
        off = (state["off"] + 31) // 32 * 32
        t = nc.alloc_sbuf_tensor_at(name, shape, dtype, offset=off)
        state["off"] = off + nbytes
        assert state["off"] <= SBUF_TOP, f"carve overflow at {name}: {state['off']}"
        return t

    def reset():
        state["off"] = base
    return carve, reset


def build_launch(phase_emitters, dram_decls, aliases=None):
    nc = bass.Bass("TRN2", target_bir_lowering=False, debug=False)
    io = {}
    for name, shape, dtype, kind in dram_decls:
        io[name] = nc.dram_tensor(name, shape, dtype, kind=kind).ap()
    if aliases:
        for k, v in aliases.items():
            io[k] = io[v]
    ctx = ExitStack()
    env = {"nc": nc, "io": io}
    pb = [nc.alloc_psum_tensor(f"pb{i}", [128, 512], F32) for i in range(8)]
    env["pb"] = pb

    def sb_fixed(name, shape, dtype):
        return nc.alloc_sbuf_tensor(name, shape, dtype)
    env["sb"] = sb_fixed
    p = P()
    p.sem("init")
    common_small(p, env)
    common_identf(p, env)
    arena_base = SBUF_TOP - nc.sbuf_bytes_remaining
    carve, reset = make_carver(nc, arena_base)
    p.barrier()
    for i, emit in enumerate(phase_emitters):
        reset()
        pref = f"ph{i}_"
        env["sb"] = (lambda pref=pref: lambda name, shape, dtype:
                     carve(pref + name, shape, dtype))()
        waits = emit(p, env)
        for s, v in waits:
            p.emit("sync", lambda e: e.nop(), waits=[(s, v)])
        p.barrier()
    p.finalize(nc, ctx, final_wait=None)
    env["ctx"] = ctx
    return nc


def common_small(p, env):
    nc = env["nc"]
    sb = env["sb"]
    ident = sb("c_ident", [128, 128], BF16)
    m_incl = sb("c_mincl", [128, 128], BF16)
    m_strict = sb("c_mstr", [128, 128], BF16)
    ones_col = sb("c_ones", [128, 1], BF16)
    ones_row = sb("c_onesr", [1, 128], F32)
    env["ident"], env["m_incl"], env["m_strict"] = ident, m_incl, m_strict
    env["ones_col"], env["ones_row"] = ones_col, ones_row

    def ch(fn):
        p.emit("gpsimd", fn, waits=[("init", p.val("init"))], inc="init")
    ch(lambda e: e.memset(ident[:], 0.0))
    ch(lambda e: e.affine_select(
        out=ident[:], in_=ident[:], compare_op=ALU.not_equal, fill=1.0,
        base=0, pattern=[[-1, 128]], channel_multiplier=1))
    ch(lambda e: e.memset(m_incl[:], 1.0))
    ch(lambda e: e.affine_select(
        out=m_incl[:], in_=m_incl[:], compare_op=ALU.is_ge, fill=0.0,
        base=0, pattern=[[1, 128]], channel_multiplier=-1))
    ch(lambda e: e.memset(m_strict[:], 1.0))
    ch(lambda e: e.affine_select(
        out=m_strict[:], in_=m_strict[:], compare_op=ALU.is_gt, fill=0.0,
        base=0, pattern=[[1, 128]], channel_multiplier=-1))
    ch(lambda e: e.memset(ones_col[:], 1.0))
    ch(lambda e: e.memset(ones_row[:], 1.0))


def emit_ln(p, env, xin, xlt, pref):
    nc = env["nc"]
    sb = env["sb"]
    s_xc = [sb(f"{pref}_xc{i}", [128, T], F32) for i in range(2)]
    s_sq = sb(f"{pref}_sq", [128, T], F32)
    s_tln = [sb(f"{pref}_tln{i}", [128, T], BF16) for i in range(2)]
    sm = {k: sb(f"{pref}_{k}", [128, 1], F32)
          for k in ("ssq", "ssum", "nm", "t2e", "mm", "ve", "lnv", "rstd", "negmr")}
    env[pref + "_xc"] = s_xc
    env[pref + "_sq"] = s_sq
    env[pref + "_tln"] = s_tln
    ch = p.sem(pref + "ch")
    app = p.sem(pref + "app")
    ptr = p.sem(pref + "ptr")
    dev = p.sem(pref + "dev")
    ident = env["ident"]
    pT = env["pb"][0].bitcast(BF16)

    def chain(eng, fn, extra=()):
        p.emit(eng, fn, waits=[(ch, p.val(ch))] + list(extra), inc=pref + "ch")

    for ci in range(16):
        b = ci % 2
        p.dma("sync", s_xc[b][:], xin[128 * ci:128 * (ci + 1), :],
              waits=[(ch, p.val(ch) - 10)], inc=pref + f"ld{b}")
        ldv = p.val(pref + f"ld{b}")
        chain("vector", lambda e, b=b: e.reduce_sum(sm["ssum"][:], s_xc[b][:], axis=AX.X),
              extra=[(pref + f"ld{b}", ldv)])
        chain("scalar", lambda e, b=b: e.activation(s_sq[:], s_xc[b][:], AF.Square,
                                                    accum_out=sm["ssq"][:]))
        chain("vector", lambda e: e.tensor_scalar(sm["nm"][:], sm["ssum"][:], -1.0 / D, None, ALU.mult))
        chain("vector", lambda e: e.tensor_scalar(sm["t2e"][:], sm["ssq"][:], 1.0 / D, 1e-5, ALU.mult, ALU.add))
        chain("vector", lambda e: e.tensor_mul(sm["mm"][:], sm["nm"][:], sm["nm"][:]))
        chain("vector", lambda e: e.tensor_sub(sm["ve"][:], sm["t2e"][:], sm["mm"][:]))
        chain("scalar", lambda e: e.activation(sm["lnv"][:], sm["ve"][:], AF.Ln))
        chain("scalar", lambda e: e.activation(sm["rstd"][:], sm["lnv"][:], AF.Exp, scale=-0.5))
        chain("vector", lambda e: e.tensor_mul(sm["negmr"][:], sm["nm"][:], sm["rstd"][:]))
        chain("scalar", lambda e, b=b: e.activation(s_tln[b][:], s_xc[b][:], AF.Identity,
                                                    bias=sm["negmr"][:], scale=sm["rstd"][:]),
              extra=[(dev, p.val(dev) - 4)])
        for g in range(4):
            for k in range(4):
                fj = 4 * g + k
                p.emit("tensor", lambda e, b=b, fj=fj, k=k: e.transpose(
                    pT[:, 128 * k:128 * (k + 1)], s_tln[b][:, 128 * fj:128 * (fj + 1)], ident[:]),
                    waits=[(ch, p.val(ch)), (dev, p.val(dev))] if k == 0 else [],
                    inc=pref + "ptr" if k == 3 else None)
            dst = xlt[:].rearrange("p (f t) -> p f t", f=16)[:, 4 * g:4 * g + 4,
                                                            128 * ci:128 * (ci + 1)]
            src = pT[:, 0:512].rearrange("p (f t) -> p f t", f=4)
            p.emit("vector", lambda e, dst=dst, src=src: e.tensor_copy(dst, src),
                   waits=[(ptr, p.val(ptr))], inc=pref + "dev")


def emit_k1(p, env):
    nc = env["nc"]; sb = env["sb"]; pb = env["pb"]; io = env["io"]
    xin, wqkv, wout, x1 = io["xin"], io["wqkv"], io["wout"], io["x1"]
    xlt = sb("xlt", [128, 16 * T], BF16)
    emit_ln(p, env, xin, xlt, "ln")
    p.emit("tensor", lambda e: e.nop(), waits=[("lndev", p.val("lndev"))])
    p.barrier()

    s_w = [sb(f"wslab{i}", [128, 16 * 512], BF16) for i in range(2)]
    s_q = sb("s_q", [128, 16 * 512], BF16)
    s_k = sb("s_k", [128, 16 * 512], BF16)
    s_v = sb("s_v", [128, 4 * D], BF16)
    s_o = sb("s_o", [128, 16 * 512], BF16)
    s_pt = [sb(f"s_pt{i}", [128, 512], BF16) for i in range(4)]
    s_rs = sb("s_rs", [1, 512], F32)
    s_rsb = sb("s_rsb", [128, 512], F32)
    s_xcs = [env["ln_tln"][0][:].bitcast(F32)[:, 512 * i:512 * (i + 1)] for i in range(2)]
    s_ys = [env["ln_tln"][1][:].bitcast(F32)[:, 512 * i:512 * (i + 1)] for i in range(2)]

    pes = p.sem("pes")
    ace = p.sem("ace")
    dvm = p.sem("dvm")
    pev = p.sem("pev")
    aev = p.sem("aev")
    dvn = p.sem("dvn")
    dvy = p.sem("dvy")

    for hb in range(4):
        th0 = 512 * hb
        aev0 = p.val(aev)
        for sg in range(12):
            wb = sg % 2
            p.dma("gpsimd", s_w[wb][:],
                  wqkv[:, 512 * sg:512 * (sg + 1)].rearrange("(ic p) o -> p ic o", p=128),
                  waits=[(aev, aev0 + 4 * (sg - 1))] if sg >= 2 else [], inc=f"ldw{wb}")
            ldv = p.val(f"ldw{wb}")
            for u in range(4):
                bank = pb[4 + u]
                for ic in range(16):
                    if sg < 8:
                        fn = (lambda e, wb=wb, ic=ic, u=u, th0=th0, bank=bank: e.matmul(
                            bank[:], s_w[wb][:, 512 * ic + 128 * u:512 * ic + 128 * (u + 1)],
                            xlt[:, 2048 * ic + th0:2048 * ic + th0 + 512],
                            start=(ic == 0), stop=(ic == 15)))
                    else:
                        fn = (lambda e, wb=wb, ic=ic, u=u, th0=th0, bank=bank: e.matmul(
                            bank[:], xlt[:, 2048 * ic + th0 + 128 * u:2048 * ic + th0 + 128 * (u + 1)],
                            s_w[wb][:, 512 * ic:512 * (ic + 1)],
                            start=(ic == 0), stop=(ic == 15)))
                    p.emit("tensor", fn,
                           waits=([(f"ldw{wb}", ldv),
                                   (aev, aev0 + 4 * (sg - 1) + u + 1 if sg >= 1 else 0)]
                                  if ic == 0 else []),
                           inc=("pes" if ic == 15 else None))
                if sg < 8:
                    dst, off = (s_q, (sg % 4) * 4 + u) if sg < 4 else (s_k, ((sg - 4) % 4) * 4 + u)
                    dap = dst[:, 512 * off:512 * (off + 1)]
                else:
                    dap = s_v[:, 2048 * u + 512 * (sg - 8):2048 * u + 512 * (sg - 8) + 512]
                p.emit("scalar", lambda e, dap=dap, bank=bank: e.activation(dap, bank[:], AF.Copy),
                       waits=[(pes, p.val(pes))], inc="aev")
        p.barrier()

        sB = [pb[0], pb[1], pb[5], pb[6]]
        LAG = 3
        for j in range(16):
            mask_of = {}
            pv_of = {}

            def emit_pv(c, j=j, mask_of=mask_of, pv_of=pv_of):
                Tp, jp = c // 16, c % 16
                col0 = 128 * Tp
                ptb = s_pt[c % 4]
                p.emit("tensor", lambda e, c=c, Tp=Tp, jp=jp, ptb=ptb, col0=col0: e.matmul(
                    pb[2][:, col0:512],
                    s_v[:, 2048 * Tp + 128 * jp:2048 * Tp + 128 * (jp + 1)],
                    ptb[:, col0:512], start=(c == 0), stop=(c == 63),
                    skip_group_check=True),
                    waits=[(dvm, mask_of[c])] + ([(dvn, p.val(dvn))] if c == 0 else []),
                    inc="pev")
                p.emit("tensor", lambda e, c=c, ptb=ptb, col0=col0: e.matmul(
                    pb[3][0:1, col0:512], env["ones_col"][:], ptb[:, col0:512],
                    start=(c == 0), stop=(c == 63), skip_group_check=True),
                    inc="pev")
                pv_of[c] = p.val(pev)

            for c in range(64):
                Tp, jp = c // 16, c % 16
                col0 = 128 * Tp
                sbank = sB[c % 4]
                ptb = s_pt[c % 4]
                p.emit("tensor", lambda e, j=j, Tp=Tp, jp=jp, col0=col0, sbank=sbank: e.matmul(
                    sbank[:, col0:512],
                    s_k[:, 512 * jp + 128 * Tp:512 * jp + 128 * (Tp + 1)],
                    s_q[:, 512 * j + col0:512 * j + 512],
                    start=True, stop=True, skip_group_check=True),
                    waits=[(ace, p.val(ace) - 4), (dvn, p.val(dvn))] if c <= 3 else
                          [(ace, p.val(ace) - 4)],
                    inc="pes")
                qk_done = p.val(pes)
                p.emit("scalar", lambda e, ptb=ptb, col0=col0, sbank=sbank: e.activation(
                    ptb[:, col0:512], sbank[:, col0:512], AF.Exp, scale=INV_SQRT_HD),
                    waits=[(pes, qk_done), (pev, pv_of.get(c - 4, 0))], inc="ace")
                mask = env["m_incl"] if jp <= j else env["m_strict"]
                p.emit("vector", lambda e, ptb=ptb, col0=col0, mask=mask: e.tensor_mul(
                    ptb[:, col0:col0 + 128], ptb[:, col0:col0 + 128], mask[:]),
                    waits=[(ace, p.val(ace))], inc="dvm")
                mask_of[c] = p.val(dvm)
                if c >= LAG:
                    emit_pv(c - LAG)
            for c in range(64 - LAG, 64):
                emit_pv(c)
            p.emit("vector", lambda e: e.reciprocal(s_rs[:], pb[3][0:1, :]),
                   waits=[(pev, p.val(pev))], inc="dvn")
            p.emit("tensor", lambda e: e.matmul(pb[4][:], env["ones_row"][:], s_rs[:],
                                                start=True, stop=True),
                   waits=[(dvn, p.val(dvn)), (aev, p.val(aev))], inc="pes")
            p.emit("scalar", lambda e: e.activation(s_rsb[:], pb[4][:], AF.Copy),
                   waits=[(pes, p.val(pes))], inc="aev")
            p.emit("vector", lambda e, j=j: e.tensor_mul(
                s_o[:, 512 * j:512 * (j + 1)], pb[2][:], s_rsb[:]),
                   waits=[(aev, p.val(aev))], inc="dvn")
        p.barrier()

        dvy0 = p.val(dvy)
        for og in range(4):
            wb = og % 2
            p.dma("gpsimd", s_w[wb][:],
                  wout[:, 512 * og:512 * (og + 1)].rearrange("(ic p) o -> p ic o", p=128),
                  waits=[(dvy, dvy0 + 4 * (og - 1))] if og >= 2 else [], inc=f"ldw{wb}")
            ldv = p.val(f"ldw{wb}")
            for tc in range(4):
                i = (og * 4 + tc) % 2
                bank = pb[4 + tc % 4]
                p.dma("sync", s_xcs[i],
                      xin[th0 + 128 * tc:th0 + 128 * (tc + 1), 512 * og:512 * (og + 1)],
                      waits=[(dvy, p.val(dvy) - 1)], inc=f"ldx{i}")
                ldxv = p.val(f"ldx{i}")
                for ec in range(16):
                    p.emit("tensor", lambda e, wb=wb, ec=ec, tc=tc, bank=bank: e.matmul(
                        bank[:], s_o[:, 512 * ec + 128 * tc:512 * ec + 128 * (tc + 1)],
                        s_w[wb][:, 512 * ec:512 * (ec + 1)],
                        start=(ec == 0), stop=(ec == 15)),
                        waits=([(f"ldw{wb}", ldv), (dvy, p.val(dvy))] if ec == 0 else []),
                        inc=("pes" if ec == 15 else None))
                p.emit("vector", lambda e, i=i, bank=bank: e.scalar_tensor_tensor(
                    s_ys[i], bank[:], 1.0, s_xcs[i], ALU.mult, ALU.add),
                    waits=[(pes, p.val(pes)), (f"ldx{i}", ldxv),
                           (f"sty{i}", p.val(f"sty{i}"))],
                    inc="dvy")
                p.dma("sync", x1[th0 + 128 * tc:th0 + 128 * (tc + 1), 512 * og:512 * (og + 1)],
                      s_ys[i], waits=[(dvy, p.val(dvy))], inc=f"sty{i}")
        p.barrier()

    return [("sty0", p.val("sty0")), ("sty1", p.val("sty1"))]


def emit_dec_fp4(p, env, u16v, nC2, s_W, scT_bcast, wv_maker, waits_first, inc_last,
                 dec):
    k_, sh_, b_, t_, vp_, v_ = (dec[n] for n in ("k", "sh", "b", "t", "vp", "v"))
    for jj in range(4):
        w0 = waits_first if jj == 0 else ()
        if jj == 0:
            p.emit("vector", lambda e: e.tensor_scalar(
                k_[:, :nC2], u16v, 7, None, ALU.bitwise_and), waits=w0)
        else:
            p.emit("vector", lambda e, jj=jj: e.tensor_scalar(
                k_[:, :nC2], u16v, 4 * jj, 7, ALU.logical_shift_right, ALU.bitwise_and))
        if jj == 3:
            p.emit("vector", lambda e: e.tensor_scalar(
                sh_[:, :nC2], u16v, 0x8000, None, ALU.bitwise_and))
        else:
            p.emit("vector", lambda e, jj=jj: e.tensor_scalar(
                sh_[:, :nC2], u16v, 12 - 4 * jj, 0x8000,
                ALU.logical_shift_left, ALU.bitwise_and))
        p.emit("vector", lambda e: e.tensor_scalar(
            b_[:, :nC2], k_[:, :nC2], 64.0, 16128.0, ALU.mult, ALU.add))
        p.emit("vector", lambda e: e.tensor_scalar(
            t_[:, :nC2], b_[:, :nC2].bitcast(BF16), 2.0, -1.0, ALU.mult, ALU.add))
        p.emit("vector", lambda e: e.tensor_tensor(
            vp_[:, :nC2], t_[:, :nC2], b_[:, :nC2].bitcast(BF16), ALU.min))
        p.emit("vector", lambda e: e.tensor_tensor(
            v_[:, :nC2].bitcast(U16), vp_[:, :nC2].bitcast(U16), sh_[:, :nC2],
            ALU.bitwise_xor))
        wv, ng = wv_maker(jj)
        vv = v_[:, :nC2].rearrange("p (g k) -> p g k", g=ng)
        p.emit("vector", lambda e, wv=wv, vv=vv: e.tensor_tensor(wv, vv, scT_bcast, ALU.mult),
               inc=(inc_last if jj == 3 else None))


def emit_scale_T(p, env, dram_slice, s_h, s_b16, pbank, s_out, nrows, waits, inc):
    ident = env["ident"]
    p.dma("sync", s_h[0:nrows, :], dram_slice,
          waits=list(waits) + [("pesc", p.val("pesc"))], inc="ldsc")
    v = p.val("ldsc")
    p.emit("vector", lambda e: e.tensor_copy(s_b16[0:nrows, :], s_h[0:nrows, :]),
           waits=[("ldsc", v)], inc="dvsc")
    p.emit("tensor", lambda e: e.transpose(pbank.bitcast(BF16)[:, 0:nrows],
                                           s_b16[0:nrows, :], ident[0:nrows, 0:nrows]),
           waits=[("dvsc", p.val("dvsc")), (inc, p.val(inc))], inc="pesc")
    p.emit("vector", lambda e: e.tensor_copy(s_out[:, 0:nrows], pbank.bitcast(BF16)[:, 0:nrows]),
           waits=[("pesc", p.val("pesc"))], inc=inc)


def emit_k2(p, env):
    nc = env["nc"]; sb = env["sb"]; pb = env["pb"]; io = env["io"]
    x1, gup, gus, dp, ds, x2, wdT0, yd = (io["x1"], io["gup"], io["gus"], io["dp"],
                                          io["ds"], io["x2"], io["wdT0"], io["yd"])
    wdT = wdT0
    for s in ("ldsc", "dvsc", "pesc"):
        p.sem(s)
    xlt = sb("xlt", [128, 16 * T], BF16)
    emit_ln(p, env, x1, xlt, "ln")
    p.emit("tensor", lambda e: e.nop(), waits=[("lndev", p.val("lndev"))])

    dec = {n: sb(f"dec_{n}", [128, 1408], U16 if n in ("k", "sh", "b") else BF16)
           for n in ("k", "sh", "b", "t", "vp", "v")}
    s_pk = [sb(f"s_pk{i}", [128, 704], I32) for i in range(2)]
    s_sch = sb("s_sch", [44, 128], F16)
    s_scb = sb("s_scb", [44, 128], BF16)
    s_scT = [sb(f"s_scT{i}", [128, 44], F32) for i in range(2)]
    s_W = [sb(f"s_W{i}", [128, 5632], BF16) for i in range(2)]
    s_wT = [sb(f"s_wT{i}", [128, 2048], BF16) for i in range(2)]
    s_st = [sb(f"s_st{i}", [128, 512], BF16) for i in range(2)]
    s_h = [sb(f"s_h{i}", [128, 2048], BF16) for i in range(4)]
    s_sg = sb("s_sg", [128, 2048], F32)
    s_wd = [sb(f"s_wd{i}", [128, 2048], BF16) for i in range(4)]
    s_yst = [sb(f"s_yst{i}", [128, 512], F32) for i in range(4)]
    dvd = p.sem("dvd")
    pet = p.sem("pet")
    dve = p.sem("dve")
    pem = p.sem("pem")
    ach = p.sem("ach")
    dvh = p.sem("dvh")
    ldd = p.sem("ldd")
    pdn = p.sem("pdn")
    dvo = p.sem("dvo")
    p.barrier()

    tb = {"j": 0, "ev": {-2: 0, -1: 0}}

    def tgrp_start_waits(deps):
        return deps + [(dve, tb["ev"][tb["j"] - 2])]

    def tgrp_done():
        tb["ev"][tb["j"]] = p.val(dve)
        tb["j"] += 1

    for f in range(16):
        fb = f % 2
        p.dma("sync", s_pk[fb][:], dp[128 * f:128 * (f + 1), :],
              waits=[(dvd, p.val(dvd) - 1)], inc=f"ldp{fb}")
        ldv = p.val(f"ldp{fb}")
        emit_scale_T(p, env, ds[:, 128 * f:128 * (f + 1)], s_sch, s_scb, pb[1],
                     s_scT[fb], 44, [(dvd, p.val(dvd) - 1)], "dvsc2" if False else "dvscT")
        scv = p.val("dvscT")
        scb = s_scT[fb][:, 0:44].unsqueeze(2).broadcast_to([128, 44, 32])

        def wv_mk(jj, fb=fb):
            return s_W[fb][:, jj::4].rearrange("p (g k) -> p g k", g=44), 44
        emit_dec_fp4(p, env, s_pk[fb][:].bitcast(U16), 1408, s_W[fb], scb, wv_mk,
                     [(f"ldp{fb}", ldv), ("dvscT", scv), (pet, p.val(pet))], "dvd", dec)
        dcv = p.val(dvd)
        for g in range(11):
            bk = pb[2 + tb["j"] % 2]
            for k in range(4):
                c = 4 * g + k
                p.emit("tensor", lambda e, fb=fb, c=c, k=k, bk=bk: e.transpose(
                    bk.bitcast(BF16)[:, 128 * k:128 * (k + 1)],
                    s_W[fb][:, 128 * c:128 * (c + 1)], env["ident"][:]),
                    waits=(tgrp_start_waits([(dvd, dcv)]) if k == 0 else []),
                    inc=("pet" if k == 3 else None))
            sb_i = (f * 11 + g) % 2
            p.emit("vector", lambda e, sb_i=sb_i, bk=bk: e.tensor_copy(
                s_st[sb_i][:], bk.bitcast(BF16)[:, 0:512]),
                waits=[(pet, p.val(pet)), (f"stw{sb_i}", p.val(f"stw{sb_i}"))], inc="dve")
            tgrp_done()
            dst = wdT[512 * g:512 * (g + 1), 128 * f:128 * (f + 1)].rearrange(
                "(t p) f -> p t f", p=128)
            src = s_st[sb_i][:].rearrange("p (t f) -> p t f", t=4)
            p.dma("scalar", dst, src, waits=[(dve, p.val(dve))], inc=f"stw{sb_i}")
    p.barrier()

    s_gsh = sb("s_gsh", [16, 128], F16)
    s_gsb = sb("s_gsb", [16, 128], BF16)
    s_gsT = [sb(f"s_gsT{i}", [128, 16], F32) for i in range(2)]
    stw_totals = [p.val("stw0"), p.val("stw1")]
    sta_grp = {}
    chunks = [(g, ci, half) for g in range(11) for ci in range(4) for half in (0, 1)]
    dec_of2 = {}
    T2_of = {}

    def emit_load_dec(n):
        g, ci, half = chunks[n]
        c = 4 * g + ci
        row0 = 128 * c if half == 0 else 5632 + 128 * c
        hb2 = n % 2
        p.dma("sync", s_pk[hb2][:, 0:256], gup[row0:row0 + 128, :],
              waits=[(dvd, p.val(dvd) - 1)], inc=f"ldp{hb2}")
        ldv = p.val(f"ldp{hb2}")
        emit_scale_T(p, env, gus[:, row0:row0 + 128], s_gsh, s_gsb, pb[1],
                     s_gsT[hb2], 16, [("dvscT", p.val("dvscT") - 1)], "dvscT")
        scv = p.val("dvscT")
        scb = s_gsT[hb2][:, 0:16].unsqueeze(2).broadcast_to([128, 16, 32])

        def wv_mk(jj, hb2=hb2):
            return s_W[hb2][:, jj::4][:, 0:512].rearrange(
                "p (g k) -> p g k", g=16), 16
        emit_dec_fp4(p, env, s_pk[hb2][:, 0:256].bitcast(U16), 512,
                     s_W[hb2], scb, wv_mk,
                     [(f"ldp{hb2}", ldv), ("dvscT", scv), (pet, p.val(pet))], "dvd", dec)
        dec_of2[n] = p.val(dvd)

    def emit_T2(n):
        hb2 = n % 2
        for grp in range(4):
            bk = pb[2 + tb["j"] % 2]
            for k in range(4):
                ic = 4 * grp + k
                p.emit("tensor", lambda e, hb2=hb2, ic=ic, k=k, bk=bk: e.transpose(
                    bk.bitcast(BF16)[:, 128 * k:128 * (k + 1)],
                    s_W[hb2][:, 128 * ic:128 * (ic + 1)], env["ident"][:]),
                    waits=(tgrp_start_waits([(dvd, dec_of2[n])]) if k == 0 else []),
                    inc=("pet" if k == 3 else None))
            p.emit("vector", lambda e, hb2=hb2, grp=grp, bk=bk: e.tensor_copy(
                s_wT[hb2][:].rearrange("p (ic q) -> p ic q", ic=16)[:, 4 * grp:4 * grp + 4, :],
                bk.bitcast(BF16)[:, 0:512].rearrange("p (t q) -> p t q", t=4)),
                waits=[(pet, p.val(pet)), (pem, p.val(pem) - 4)], inc="dve")
            tgrp_done()
        T2_of[n] = p.val(dve)

    def emit_mm(n):
        g, ci, half = chunks[n]
        hb2 = n % 2
        devv = T2_of[n]
        bank = pb[4 + half]
        for tg in range(4):
            cons = ("dvg", p.val("dvg")) if half == 0 else (dvh, p.val(dvh))
            for ic in range(16):
                p.emit("tensor", lambda e, hb2=hb2, ic=ic, tg=tg, bank=bank: e.matmul(
                    bank[:], s_wT[hb2][:, 128 * ic:128 * (ic + 1)],
                    xlt[:, 2048 * ic + 512 * tg:2048 * ic + 512 * (tg + 1)],
                    start=(ic == 0), stop=(ic == 15)),
                    waits=([(dve, devv), cons] if ic == 0 else []),
                    inc=("pem" if ic == 15 else None))
            if half == 0:
                p.emit("vector", lambda e, tg=tg: e.tensor_copy(
                    s_sg[:, 512 * tg:512 * (tg + 1)], pb[4][:]),
                    waits=[(pem, p.val(pem)), (dvh, p.val(dvh) - 3 + tg)],
                    inc="dvg")
                p.emit("scalar", lambda e, tg=tg: e.activation(
                    s_sg[:, 512 * tg:512 * (tg + 1)],
                    s_sg[:, 512 * tg:512 * (tg + 1)], AF.Silu),
                    waits=[("dvg", p.val("dvg"))], inc="ach")
            else:
                p.emit("vector", lambda e, ci=ci, tg=tg: e.tensor_mul(
                    s_h[ci][:, 512 * tg:512 * (tg + 1)],
                    s_sg[:, 512 * tg:512 * (tg + 1)], pb[5][:]),
                    waits=[(pem, p.val(pem)), (ach, p.val(ach)),
                           (pdn, p.val(pdn))],
                    inc="dvh")

    emit_load_dec(0)
    for n in range(88):
        emit_T2(n)
        if n + 1 < 88:
            emit_load_dec(n + 1)
        emit_mm(n)
        if n % 8 != 7:
            continue
        g = n // 8
        h_done = p.val(dvh)
        for ci in range(4):
            c = 4 * g + ci
            p.dma("gpsimd", s_wd[ci][:], wdT[128 * c:128 * (c + 1), :],
                  waits=[("stw0", stw_totals[0]), ("stw1", stw_totals[1]),
                         (pdn, p.val(pdn))], inc="ldd")
        lddv = p.val(ldd)
        for tc in range(16):
            for fg in range(4):
                idx = tc * 4 + fg
                bank = pb[6 + idx % 2]
                for ci in range(4):
                    p.emit("tensor", lambda e, ci=ci, tc=tc, fg=fg, bank=bank: e.matmul(
                        bank[:], s_h[ci][:, 128 * tc:128 * (tc + 1)],
                        s_wd[ci][:, 512 * fg:512 * (fg + 1)],
                        start=(ci == 0), stop=(ci == 3)),
                        waits=([(ldd, lddv), (dvh, h_done), (dvo, p.val(dvo) - 1)]
                               if ci == 0 else []),
                        inc=("pdn" if ci == 3 else None))
                yb = idx % 4
                pp = idx % 2
                p.emit("vector", lambda e, yb=yb, bank=bank: e.tensor_copy(
                    s_yst[yb][:], bank[:]),
                    waits=[(pdn, p.val(pdn)),
                           (f"sta{pp}", p.val(f"sta{pp}") - 16)], inc="dvo")
                gw = [(f"sta{pp}", sta_grp[g - 1][pp])] if g > 0 else []
                p.dma("gpsimd", yd[128 * tc:128 * (tc + 1), 512 * fg:512 * (fg + 1)],
                      s_yst[yb][:],
                      waits=[(dvo, p.val(dvo))] + gw, inc=f"sta{pp}",
                      accum_op=(ALU.add if g > 0 else ALU.bypass))
        sta_grp[g] = (p.val("sta0"), p.val("sta1"))
    p.barrier()

    s_yc = [env["ln_xc"][i][:] for i in range(2)]
    s_xc3 = [s_W[i].bitcast(F32)[:, 0:2048] for i in range(2)]
    s_sq2 = env["ln_sq"]
    xltf = xlt[:].bitcast(F32)
    s_o2 = [xltf[:, 2048 * i:2048 * (i + 1)] for i in range(2)]
    smz = {k: sb(f"fz_{k}", [128, 1], F32) for k in ("ssq", "t2e", "lnv", "rstd")}
    sta_totals = [p.val("sta0"), p.val("sta1")]
    for tc in range(16):
        b2 = tc % 2
        p.dma("sync", s_yc[b2], yd[128 * tc:128 * (tc + 1), :],
              waits=[("fch", p.val("fch") - 5), ("sta0", sta_totals[0]),
                     ("sta1", sta_totals[1])], inc=f"ldy{b2}")
        p.dma("sync", s_xc3[b2], x1[128 * tc:128 * (tc + 1), :],
              waits=[("fch", p.val("fch") - 5)], inc=f"ldy{b2}")
        ldv = p.val(f"ldy{b2}")
        p.emit("scalar", lambda e, b2=b2: e.activation(s_sq2[:], s_yc[b2], AF.Square,
                                                       accum_out=smz["ssq"][:]),
               waits=[(f"ldy{b2}", ldv), ("fch", p.val("fch"))], inc="fch")
        p.emit("vector", lambda e: e.tensor_scalar(smz["t2e"][:], smz["ssq"][:],
                                                   1.0 / D, 1e-6, ALU.mult, ALU.add),
               waits=[("fch", p.val("fch"))], inc="fch")
        p.emit("scalar", lambda e: e.activation(smz["lnv"][:], smz["t2e"][:], AF.Ln),
               waits=[("fch", p.val("fch"))], inc="fch")
        p.emit("scalar", lambda e: e.activation(smz["rstd"][:], smz["lnv"][:], AF.Exp,
                                                scale=-0.5),
               waits=[("fch", p.val("fch"))], inc="fch")
        p.emit("vector", lambda e, b2=b2: e.scalar_tensor_tensor(
            s_o2[b2], s_yc[b2], smz["rstd"][:], s_xc3[b2], ALU.mult, ALU.add),
            waits=[("fch", p.val("fch")), (f"stz{b2}", p.val(f"stz{b2}"))], inc="fch")
        p.dma("sync", x2[128 * tc:128 * (tc + 1), :], s_o2[b2],
              waits=[("fch", p.val("fch"))], inc=f"stz{b2}")

    return [("stz0", p.val("stz0")), ("stz1", p.val("stz1"))]


def common_identf(p, env):
    sb = env["sb"]
    identf = sb("c_identf", [128, 128], F32)
    env["identf"] = identf
    p.emit("gpsimd", lambda e: e.memset(identf[:], 0.0),
           waits=[("init", p.val("init"))], inc="init")
    p.emit("gpsimd", lambda e: e.affine_select(
        out=identf[:], in_=identf[:], compare_op=ALU.not_equal, fill=1.0,
        base=0, pattern=[[-1, 128]], channel_multiplier=1),
        waits=[("init", p.val("init"))], inc="init")


def emit_bias_T(p, env, bvec, s_bh, s_out, tag):
    pbk = env["pb"][1]
    p.dma("sync", s_bh[:], bvec.rearrange("(c p) -> c p", p=128),
          waits=[(tag + "bt", p.val(tag + "bt"))], inc=tag + "bl")
    p.emit("tensor", lambda e: e.transpose(pbk[:, 0:16], s_bh[:], env["identf"][0:16, 0:16]),
           waits=[(tag + "bl", p.val(tag + "bl")), (tag + "bt", p.val(tag + "bt"))],
           inc=tag + "bp")
    p.emit("vector", lambda e: e.tensor_copy(s_out[:], pbk[:, 0:16]),
           waits=[(tag + "bp", p.val(tag + "bp"))], inc=tag + "bt")


def emit_rope_tables(p, env, pos, s_sinf, s_cosf):
    sb = env["sb"]
    pb = env["pb"]
    s_dr = sb("rp_dr", [1, 128], I32)
    s_jr = sb("rp_jr", [1, 128], I32)
    s_jf = sb("rp_jf", [1, 128], F32)
    s_if = sb("rp_if", [1, 128], F32)
    s_pos = sb("rp_pos", [1, T], F32)
    s_u = sb("rp_u", [128, 512], F32)
    s_i = sb("rp_i", [128, 512], I32)
    s_f = sb("rp_f", [128, 512], F32)
    s_f2 = sb("rp_f2", [128, 512], F32)
    rc = p.sem("rc")

    def chain(eng, fn, extra=()):
        p.emit(eng, fn, waits=[("rc", p.val("rc"))] + list(extra), inc="rc")

    p.dma("sync", s_pos[:], pos, inc="rl")
    rl = p.val("rl")
    chain("gpsimd", lambda e: e.iota(s_dr[:], pattern=[[1, 128]], base=0, channel_multiplier=0))
    chain("vector", lambda e: e.tensor_scalar(s_jr[:], s_dr[:], 1, None, ALU.logical_shift_right))
    chain("vector", lambda e: e.tensor_copy(s_jf[:], s_jr[:]))
    LN1E4 = float(np.log(10000.0))
    chain("scalar", lambda e: e.activation(s_if[:], s_jf[:], AF.Exp, scale=-2.0 * LN1E4 / 128.0))
    INV2PI = float(1.0 / (2.0 * np.pi))
    TWOPI = float(2.0 * np.pi)
    for tg in range(4):
        chain("tensor", lambda e, tg=tg: e.matmul(
            pb[2][:], s_if[:], s_pos[:, 512 * tg:512 * (tg + 1)], start=True, stop=True),
            extra=[("rl", rl)])
        for phase, dst in ((0.0, s_sinf), (0.25, s_cosf)):
            chain("vector", lambda e, ph=phase: e.tensor_scalar(
                s_u[:], pb[2][:], INV2PI, ph, ALU.mult, ALU.add))
            chain("vector", lambda e: e.tensor_copy(s_i[:], s_u[:]))
            chain("vector", lambda e: e.tensor_copy(s_f[:], s_i[:]))
            chain("vector", lambda e: e.tensor_sub(s_f2[:], s_u[:], s_f[:]))
            chain("scalar", lambda e, dst=dst, tg=tg: e.activation(
                dst[:, 512 * tg:512 * (tg + 1)], s_f2[:], AF.Sin, scale=TWOPI))


def emit_psw(p, env, s_psw):
    sb = env["sb"]
    s_i1 = sb("pw_i1", [128, 128], F32)
    s_c1 = sb("pw_c1", [128, 128], F32)
    s_c2 = sb("pw_c2", [128, 128], F32)
    s_pi = sb("pw_pi", [128, 1], I32)
    s_pk = sb("pw_pk", [128, 1], I32)
    s_pkf = sb("pw_pkf", [128, 1], F32)
    s_ev = sb("pw_ev", [128, 1], F32)

    def chain(eng, fn):
        p.emit(eng, fn, waits=[("rc", p.val("rc"))], inc="rc")
    chain("gpsimd", lambda e: e.iota(s_i1[:], pattern=[[1, 128]], base=0, channel_multiplier=-1, allow_small_or_imprecise_dtypes=True))
    chain("vector", lambda e: e.tensor_single_scalar(s_c1[:], s_i1[:], 1.0, ALU.is_equal))
    chain("vector", lambda e: e.tensor_single_scalar(s_c2[:], s_i1[:], -1.0, ALU.is_equal))
    chain("gpsimd", lambda e: e.iota(s_pi[:], pattern=[[0, 1]], base=0, channel_multiplier=1))
    chain("vector", lambda e: e.tensor_single_scalar(s_pk[:], s_pi[:], 1, ALU.bitwise_and))
    chain("vector", lambda e: e.tensor_copy(s_pkf[:], s_pk[:]))
    chain("vector", lambda e: e.tensor_scalar(s_ev[:], s_pkf[:], -1.0, 1.0, ALU.mult, ALU.add))
    chain("vector", lambda e: e.tensor_scalar(s_c1[:], s_c1[:], s_ev[:], None, ALU.mult))
    chain("vector", lambda e: e.tensor_scalar(s_c2[:], s_c2[:], s_pkf[:], None, ALU.mult))
    chain("vector", lambda e: e.tensor_sub(s_psw[:], s_c1[:], s_c2[:]))


def emit_k3a(p, env):
    nc = env["nc"]; sb = env["sb"]; pb = env["pb"]; io = env["io"]
    x2, wq, wk, wv = io["x2"], io["wq"], io["wk"], io["wv"]
    bq, bk, bv, pos = io["bq"], io["bk"], io["bv"], io["pos"]
    qT, kT, vv = io["qT"], io["kT"], io["vv"]
    xlt = sb("xlt", [128, 16 * T], BF16)
    emit_ln(p, env, x2, xlt, "ln")
    p.emit("tensor", lambda e: e.nop(), waits=[("lndev", p.val("lndev"))])
    s_sinf = sb("s_sinf", [128, T], F32)
    s_cosf = sb("s_cosf", [128, T], F32)
    emit_rope_tables(p, env, pos, s_sinf, s_cosf)
    s_psw = sb("s_psw", [128, 128], F32)
    emit_psw(p, env, s_psw)
    s_pswb = sb("s_pswb", [128, 128], BF16)
    p.emit("vector", lambda e: e.tensor_copy(s_pswb[:], s_psw[:]),
           waits=[("rc", p.val("rc"))], inc="rc")
    s_bqT = sb("s_bqT", [128, 16], F32)
    s_bkT = sb("s_bkT", [128, 16], F32)
    s_bvR = sb("s_bvR", [1, D], F32)
    s_bvB = sb("s_bvB", [128, D], F32)
    s_bh = sb("s_bh", [16, 128], F32)
    emit_bias_T(p, env, bq, s_bh, s_bqT, "q")
    emit_bias_T(p, env, bk, s_bh, s_bkT, "k")
    p.dma("sync", s_bvR[:], bv.rearrange("(a d) -> a d", a=1), inc="rlb")
    rlv = p.val("rlb")
    for tg in range(4):
        p.emit("tensor", lambda e, tg=tg: e.matmul(
            pb[3][:], env["ones_row"][:], s_bvR[:, 512 * tg:512 * (tg + 1)],
            start=True, stop=True),
            waits=[("rlb", rlv), ("rc", p.val("rc"))], inc="rc")
        p.emit("vector", lambda e, tg=tg: e.tensor_copy(
            s_bvB[:, 512 * tg:512 * (tg + 1)], pb[3][:]),
            waits=[("rc", p.val("rc"))], inc="rc")
    p.barrier()

    s_w = [sb(f"wslab{i}", [128, 16 * 512], BF16) for i in range(2)]
    s_qs = [sb(f"s_qs{i}", [128, 512], BF16) for i in range(2)]
    s_m1 = sb("s_m1", [128, 512], F32)
    s_m2 = sb("s_m2", [128, 512], F32)
    s_qo = [sb(f"s_qo{i}", [128, 512], BF16) for i in range(2)]
    pes = p.sem("pes")
    aev = p.sem("aev")
    per = p.sem("per")
    dvr = p.sem("dvr")

    for mat, (wmat, bT, dst) in enumerate(((wq, s_bqT, qT), (wk, s_bkT, kT))):
        for og in range(4):
            wb = og % 2
            p.dma("gpsimd", s_w[wb][:],
                  wmat[:, 512 * og:512 * (og + 1)].rearrange("(ic p) o -> p ic o", p=128),
                  waits=[(pes, p.val(pes) - 16)], inc=f"ldw{wb}")
            ldv = p.val(f"ldw{wb}")
            for oc in range(4):
                h = 4 * og + oc
                for tg in range(4):
                    i = (h * 4 + tg) % 2
                    bank = pb[4 + tg % 2]
                    for ic in range(16):
                        p.emit("tensor", lambda e, wb=wb, ic=ic, oc=oc, tg=tg, bank=bank: e.matmul(
                            bank[:], s_w[wb][:, 512 * ic + 128 * oc:512 * ic + 128 * (oc + 1)],
                            xlt[:, 2048 * ic + 512 * tg:2048 * ic + 512 * (tg + 1)],
                            start=(ic == 0), stop=(ic == 15)),
                            waits=([(f"ldw{wb}", ldv), (aev, p.val(aev) - 1)] if ic == 0 else []),
                            inc=("pes" if ic == 15 else None))
                    p.emit("scalar", lambda e, i=i, h=h, bank=bank, bT=bT: e.activation(
                        s_qs[i][:], bank[:], AF.Identity, bias=bT[:, h:h + 1]),
                        waits=[(pes, p.val(pes)), (per, p.val(per) - 1),
                               (dvr, p.val(dvr) - 3)], inc="aev")
                    p.emit("tensor", lambda e, i=i: e.matmul(
                        pb[6][:], s_pswb[:], s_qs[i][:], start=True, stop=True),
                        waits=[(aev, p.val(aev)), (dvr, p.val(dvr) - 1)], inc="per")
                    p.emit("vector", lambda e, i=i, tg=tg: e.tensor_mul(
                        s_m1[:], s_qs[i][:], s_cosf[:, 512 * tg:512 * (tg + 1)]),
                        waits=[(aev, p.val(aev))])
                    p.emit("vector", lambda e, tg=tg: e.tensor_mul(
                        s_m2[:], pb[6][:], s_sinf[:, 512 * tg:512 * (tg + 1)]),
                        waits=[(per, p.val(per))], inc="dvr")
                    p.emit("vector", lambda e, i=i: e.tensor_add(s_qo[i][:], s_m1[:], s_m2[:]),
                           waits=[(f"stq{i}", p.val(f"stq{i}"))], inc="dvr")
                    p.dma("scalar", dst[128 * h:128 * (h + 1), 512 * tg:512 * (tg + 1)],
                          s_qo[i][:], waits=[(dvr, p.val(dvr))], inc=f"stq{i}")
    p.barrier()

    s_vo = [sb(f"s_vo{i}", [128, 512], BF16) for i in range(2)]
    for og in range(4):
        wb = og % 2
        p.dma("gpsimd", s_w[wb][:],
              wv[:, 512 * og:512 * (og + 1)].rearrange("(ic p) o -> p ic o", p=128),
              waits=[(pes, p.val(pes) - 16)], inc=f"ldw{wb}")
        ldv = p.val(f"ldw{wb}")
        for tc in range(16):
            i = (og * 16 + tc) % 2
            bank = pb[4 + tc % 2]
            for ic in range(16):
                p.emit("tensor", lambda e, wb=wb, ic=ic, tc=tc, bank=bank: e.matmul(
                    bank[:], xlt[:, 2048 * ic + 128 * tc:2048 * ic + 128 * (tc + 1)],
                    s_w[wb][:, 512 * ic:512 * (ic + 1)],
                    start=(ic == 0), stop=(ic == 15)),
                    waits=([(f"ldw{wb}", ldv), ("dvv", p.val("dvv") - 1)] if ic == 0 else []),
                    inc=("pes" if ic == 15 else None))
            p.emit("vector", lambda e, i=i, og=og, bank=bank: e.tensor_add(
                s_vo[i][:], bank[:], s_bvB[:, 512 * og:512 * (og + 1)]),
                waits=[(pes, p.val(pes)), (f"stq{i}", p.val(f"stq{i}"))], inc="dvv")
            p.dma("scalar", vv[128 * tc:128 * (tc + 1), 512 * og:512 * (og + 1)],
                  s_vo[i][:], waits=[("dvv", p.val("dvv"))], inc=f"stq{i}")

    return [("stq0", p.val("stq0")), ("stq1", p.val("stq1"))]


def emit_k3b(p, env):
    nc = env["nc"]; sb = env["sb"]; pb = env["pb"]; io = env["io"]
    qTd, kg, vg = io["qT2"], io["kg"], io["vg"]
    wo, bo, x2, x3 = io["wo"], io["bo"], io["x2b"], io["x3"]
    s_boR = sb("s_boR", [1, D], F32)
    s_boB = sb("s_boB", [128, D], F32)
    p.dma("sync", s_boR[:], bo.rearrange("(a d) -> a d", a=1), inc="rl")
    rlv = p.val("rl")
    for tg in range(4):
        p.emit("tensor", lambda e, tg=tg: e.matmul(
            pb[3][:], env["ones_row"][:], s_boR[:, 512 * tg:512 * (tg + 1)],
            start=True, stop=True),
            waits=[("rl", rlv), ("rc", p.val("rc"))], inc="rc")
        p.emit("vector", lambda e, tg=tg: e.tensor_copy(
            s_boB[:, 512 * tg:512 * (tg + 1)], pb[3][:]),
            waits=[("rc", p.val("rc"))], inc="rc")
    p.barrier()

    s_kh = [sb(f"s_kh{i}", [128, 4 * T], BF16) for i in range(2)]
    s_vh = [sb(f"s_vh{i}", [128, 4 * T], BF16) for i in range(2)]
    s_qt = [sb(f"s_qt{i}", [128, 512], BF16) for i in range(2)]
    s_pt = [sb(f"s_pt{i}", [128, 512], BF16) for i in range(4)]
    s_rs = sb("s_rs", [1, 512], F32)
    s_rsb = sb("s_rsb", [128, 512], F32)
    s_oall = sb("s_oall", [128, 16 * T], BF16)
    pes = p.sem("pes")
    ace = p.sem("ace")
    pev = p.sem("pev")
    aev = p.sem("aev")
    dvn = p.sem("dvn")

    for h in range(16):
        hb = h % 2
        for r in range(4):
            p.dma("gpsimd", s_kh[hb][:, 2048 * r:2048 * (r + 1)],
                  kg[2048 * r + 128 * h:2048 * r + 128 * (h + 1), :],
                  waits=[(pev, p.val(pev) - 512)] if h >= 2 else [], inc=f"ldk{hb}")
            p.dma("gpsimd", s_vh[hb][:, 2048 * r:2048 * (r + 1)],
                  vg[2048 * r:2048 * (r + 1), 128 * h:128 * (h + 1)].rearrange(
                      "(tc p) d -> p tc d", p=128),
                  inc=f"ldk{hb}")
        ldkv = p.val(f"ldk{hb}")
        sB = [pb[0], pb[1], pb[5], pb[6]]
        LAG = 3
        for tg in range(4):
            qb = (h * 4 + tg) % 2
            p.dma("sync", s_qt[qb][:], qTd[128 * h:128 * (h + 1), 512 * tg:512 * (tg + 1)],
                  waits=[(pes, p.val(pes) - 66)], inc=f"ldq{qb}")
            ldqv = p.val(f"ldq{qb}")
            exp_of = {}
            pv_of = {}

            def emit_pv(kc, hb=hb):
                ptb = s_pt[kc % 4]
                p.emit("tensor", lambda e, kc=kc, hb=hb, ptb=ptb: e.matmul(
                    pb[2][:], s_vh[hb][:, 128 * kc:128 * (kc + 1)], ptb[:],
                    start=(kc == 0), stop=(kc == 63), skip_group_check=True),
                    waits=[(ace, exp_of[kc])] + ([(dvn, p.val(dvn))] if kc == 0 else []),
                    inc="pev")
                p.emit("tensor", lambda e, kc=kc, ptb=ptb: e.matmul(
                    pb[3][0:1, :], env["ones_col"][:], ptb[:],
                    start=(kc == 0), stop=(kc == 63), skip_group_check=True),
                    inc="pev")
                pv_of[kc] = p.val(pev)

            for kc in range(64):
                sbank = sB[kc % 4]
                ptb = s_pt[kc % 4]
                p.emit("tensor", lambda e, kc=kc, hb=hb, qb=qb, sbank=sbank: e.matmul(
                    sbank[:], s_kh[hb][:, 128 * kc:128 * (kc + 1)], s_qt[qb][:],
                    start=True, stop=True, skip_group_check=True),
                    waits=([(f"ldk{hb}", ldkv), (f"ldq{qb}", ldqv), (ace, p.val(ace) - 1),
                            (dvn, p.val(dvn))] if kc <= 3 else
                           [(ace, exp_of.get(kc - 4, 0))]),
                    inc="pes")
                qk_done = p.val(pes)
                p.emit("scalar", lambda e, ptb=ptb, sbank=sbank: e.activation(
                    ptb[:], sbank[:], AF.Exp, scale=INV_SQRT_HD),
                    waits=[(pes, qk_done), (pev, pv_of.get(kc - 4, 0))], inc="ace")
                exp_of[kc] = p.val(ace)
                if kc >= LAG:
                    emit_pv(kc - LAG)
            for kc in range(64 - LAG, 64):
                emit_pv(kc)
            p.emit("vector", lambda e: e.reciprocal(s_rs[:], pb[3][0:1, :]),
                   waits=[(pev, p.val(pev))], inc="dvn")
            p.emit("tensor", lambda e: e.matmul(pb[4][:], env["ones_row"][:], s_rs[:],
                                                start=True, stop=True),
                   waits=[(dvn, p.val(dvn)), (aev, p.val(aev))], inc="pes")
            p.emit("scalar", lambda e: e.activation(s_rsb[:], pb[4][:], AF.Copy),
                   waits=[(pes, p.val(pes))], inc="aev")
            p.emit("vector", lambda e, h=h, tg=tg: e.tensor_mul(
                s_oall[:, 2048 * h + 512 * tg:2048 * h + 512 * (tg + 1)], pb[2][:], s_rsb[:]),
                   waits=[(aev, p.val(aev))], inc="dvn")
    p.barrier()

    s_w = [sb(f"wslab{i}", [128, 16 * 512], BF16) for i in range(2)]
    s_xcs = [sb(f"s_xcs{i}", [128, 512], F32) for i in range(2)]
    s_t1 = sb("s_t1", [128, 512], F32)
    s_ys = [sb(f"s_ys{i}", [128, 512], F32) for i in range(2)]
    dvy = p.sem("dvy")
    for og in range(4):
        wb = og % 2
        p.dma("gpsimd", s_w[wb][:],
              wo[:, 512 * og:512 * (og + 1)].rearrange("(ic p) o -> p ic o", p=128),
              waits=[(dvy, p.val(dvy) - 16)] if og >= 2 else [], inc=f"ldw{wb}")
        ldv = p.val(f"ldw{wb}")
        for tc in range(16):
            i = (og * 16 + tc) % 2
            bank = pb[4 + tc % 2]
            p.dma("sync", s_xcs[i][:],
                  x2[128 * tc:128 * (tc + 1), 512 * og:512 * (og + 1)],
                  waits=[(dvy, p.val(dvy) - 1)], inc=f"ldx{i}")
            ldxv = p.val(f"ldx{i}")
            for ec in range(16):
                p.emit("tensor", lambda e, wb=wb, ec=ec, tc=tc, bank=bank: e.matmul(
                    bank[:], s_oall[:, 2048 * ec + 128 * tc:2048 * ec + 128 * (tc + 1)],
                    s_w[wb][:, 512 * ec:512 * (ec + 1)],
                    start=(ec == 0), stop=(ec == 15)),
                    waits=([(f"ldw{wb}", ldv), (dvy, p.val(dvy))] if ec == 0 else []),
                    inc=("pes" if ec == 15 else None))
            p.emit("vector", lambda e, og=og, bank=bank: e.tensor_add(
                s_t1[:], bank[:], s_boB[:, 512 * og:512 * (og + 1)]),
                waits=[(pes, p.val(pes)), (f"ldx{i}", ldxv),
                       (f"sty{i}", p.val(f"sty{i}"))])
            p.emit("vector", lambda e, i=i: e.tensor_add(s_ys[i][:], s_t1[:], s_xcs[i][:]),
                   inc="dvy")
            p.dma("sync", x3[128 * tc:128 * (tc + 1), 512 * og:512 * (og + 1)],
                  s_ys[i][:], waits=[(dvy, p.val(dvy))], inc=f"sty{i}")

    return [("sty0", p.val("sty0")), ("sty1", p.val("sty1"))]


def emit_dec_nvfp4(p, env, u8v4, nC, s_W, scb, wv_maker, waits_first, inc_last, dec):
    u8c, u16, k_, sh_, b_, t_, vp_, v_ = (dec[n] for n in
                                          ("u8c", "u16", "k", "sh", "b", "t", "vp", "v"))
    p.emit("vector", lambda e: e.tensor_copy(u8c[:, :nC], u8v4[:, 0::4]), waits=waits_first)
    p.emit("vector", lambda e: e.tensor_copy(u16[:, :nC], u8c[:, :nC]))
    for (kshift, sshift), par in (((0, 12), 0), ((4, 8), 1)):
        if kshift == 0:
            p.emit("vector", lambda e: e.tensor_scalar(k_[:, :nC], u16[:, :nC], 7, None, ALU.bitwise_and))
        else:
            p.emit("vector", lambda e, ks=kshift: e.tensor_scalar(
                k_[:, :nC], u16[:, :nC], ks, 7, ALU.logical_shift_right, ALU.bitwise_and))
        p.emit("vector", lambda e, ss=sshift: e.tensor_scalar(
            sh_[:, :nC], u16[:, :nC], ss, 0x8000, ALU.logical_shift_left, ALU.bitwise_and))
        p.emit("vector", lambda e: e.tensor_scalar(b_[:, :nC], k_[:, :nC], 64.0, 16128.0, ALU.mult, ALU.add))
        p.emit("vector", lambda e: e.tensor_scalar(t_[:, :nC], b_[:, :nC].bitcast(BF16), 2.0, -1.0, ALU.mult, ALU.add))
        p.emit("vector", lambda e: e.tensor_tensor(vp_[:, :nC], t_[:, :nC], b_[:, :nC].bitcast(BF16), ALU.min))
        p.emit("vector", lambda e: e.tensor_tensor(v_[:, :nC].bitcast(U16), vp_[:, :nC].bitcast(U16), sh_[:, :nC], ALU.bitwise_xor))
        wv, ng = wv_maker(par)
        vv_ = v_[:, :nC].rearrange("p (g k) -> p g k", g=ng)
        p.emit("vector", lambda e, wv=wv, vv_=vv_: e.tensor_tensor(wv, vv_, scb, ALU.mult),
               inc=(inc_last if par == 1 else None))


def emit_k5(p, env):
    nc = env["nc"]; sb = env["sb"]; pb = env["pb"]; io = env["io"]
    x3, out, wdT = io["x3b"], io["out"], io["wdT1"]
    gWp, gs, uWp, us, dWp, dsc = (io["gWp"], io["gs"], io["uWp"], io["us"],
                                  io["dWp"], io["dsc"])
    xlt = sb("xlt", [128, 16 * T], BF16)
    emit_ln(p, env, x3, xlt, "ln")
    p.emit("tensor", lambda e: e.nop(), waits=[("lndev", p.val("lndev"))])
    p.dma("gpsimd", out, x3, inc="sti")
    sti_v = p.val("sti")

    dec = {}
    dec["u8c"] = sb("dec_u8c", [128, 1024], U8)
    for n_ in ("u16", "k", "sh", "b"):
        dec[n_] = sb(f"dec_{n_}", [128, 1024], U16)
    for n_ in ("t", "vp", "v"):
        dec[n_] = sb(f"dec_{n_}", [128, 1024], BF16)
    s_pk = [sb(f"s_pk{i}", [128, 1024], I32) for i in range(2)]
    s_sc = [sb(f"s_sc{i}", [128, 64], F32) for i in range(2)]
    s_W = [sb(f"s_W{i}", [128, 2048], BF16) for i in range(2)]
    s_wT = [sb(f"s_wT{i}", [128, 2048], BF16) for i in range(2)]
    s_gg = sb("s_gg", [128, 2048], BF16)
    s_yst = [sb(f"s_yst{i}", [128, 512], F32) for i in range(4)]
    pk5 = [sb("pk5_0", [128, 1024], I32)]
    sc5 = [sb("sc5_0", [128, 64], F32)]
    W5 = [sb("W5_0", [128, 2048], BF16)]
    st5 = [sb(f"st5_{i}", [128, 512], BF16) for i in range(2)]
    s_h = [sb(f"s_h{i}", [128, 2048], BF16) for i in range(12)]
    _xc = env["ln_xc"]; _sq = env["ln_sq"]; _tl = env["ln_tln"]
    s_wd = [
        _xc[0][:].bitcast(BF16)[:, 0:2048], _xc[0][:].bitcast(BF16)[:, 2048:4096],
        _xc[1][:].bitcast(BF16)[:, 0:2048], _xc[1][:].bitcast(BF16)[:, 2048:4096],
        _sq[:].bitcast(BF16)[:, 0:2048], _sq[:].bitcast(BF16)[:, 2048:4096],
        _tl[0][:], _tl[1][:],
    ]
    dvd = p.sem("dvd")
    pet = p.sem("pet")
    dve = p.sem("dve")
    ld5 = p.sem("ld5")
    dvd5 = p.sem("dvd5")
    pet5 = p.sem("pet5")
    pem = p.sem("pem")
    dvg = p.sem("dvg")
    ach = p.sem("ach")
    dvh = p.sem("dvh")
    ldd = p.sem("ldd")
    pdn = p.sem("pdn")
    dvo = p.sem("dvo")

    NCH = 256
    HRING = 12

    sa_stw = {}
    dcv5 = {}
    tb = {"j": 0, "ev": {-2: 0, -1: 0}}

    def tgrp_start_waits(deps):
        j = tb["j"]
        return deps + [("dve", tb["ev"][j - 2])]

    def tgrp_done(evv):
        tb["ev"][tb["j"]] = evv
        tb["j"] += 1

    def emit_5a_load(i):
        cc, f = divmod(i, 16)
        p.dma("sync", pk5[0][:], dWp[128 * f:128 * (f + 1), 1024 * cc:1024 * (cc + 1)],
              waits=[(dvd5, p.val(dvd5))], inc="ld5")
        p.dma("sync", sc5[0][:], dsc[128 * f:128 * (f + 1), 64 * cc:64 * (cc + 1)],
              inc="ld5")

    def emit_5a_dec(i):
        ldv = p.val(ld5)
        scb = sc5[0][:].unsqueeze(2).broadcast_to([128, 64, 16])

        def wv_mk(par):
            return W5[0][:, par::2].rearrange("p (g k) -> p g k", g=64), 64
        emit_dec_nvfp4(p, env, pk5[0][:].bitcast(U8), 1024, W5[0], scb, wv_mk,
                       [(ld5, ldv), (pet5, p.val(pet5))], "dvd5", dec)
        dcv5[i] = p.val(dvd5)
        if i + 1 < 128:
            emit_5a_load(i + 1)

    def emit_5a_T(i):
        cc, f = divmod(i, 16)
        dcv = dcv5[i]
        for g in range(4):
            bk = pb[tb["j"] % 2]
            for k in range(4):
                c = 4 * g + k
                p.emit("tensor", lambda e, c=c, k=k, bk=bk: e.transpose(
                    bk.bitcast(BF16)[:, 128 * k:128 * (k + 1)],
                    W5[0][:, 128 * c:128 * (c + 1)], env["ident"][:]),
                    waits=(tgrp_start_waits([(dvd5, dcv)]) if k == 0 else []),
                    inc=("pet5" if k == 3 else None))
            sb_i = g % 2
            p.emit("vector", lambda e, sb_i=sb_i, bk=bk: e.tensor_copy(
                st5[sb_i][:], bk.bitcast(BF16)[:, 0:512]),
                waits=[(pet5, p.val(pet5)), (f"stw{sb_i}", p.val(f"stw{sb_i}"))],
                inc="dve")
            tgrp_done(p.val("dve"))
            dst = wdT[2048 * cc + 512 * g:2048 * cc + 512 * (g + 1),
                      128 * f:128 * (f + 1)].rearrange("(t p) q -> p t q", p=128)
            srcv = st5[sb_i][:].rearrange("p (t q) -> p t q", t=4)
            p.dma("scalar", dst, srcv, waits=[("dve", p.val("dve"))], inc=f"stw{sb_i}")
        if f == 15:
            sa_stw[cc] = (p.val("stw0"), p.val("stw1"))

    def emit_pk_load(n):
        c, half = divmod(n, 2)
        Wp, sc = (gWp, gs) if half == 0 else (uWp, us)
        b = n % 2
        p.dma("gpsimd", s_pk[b][:], Wp[128 * c:128 * (c + 1), :],
              waits=[(dvd, p.val(dvd) - 1)], inc=f"ldp{b}")
        p.dma("gpsimd", s_sc[b][:], sc[128 * c:128 * (c + 1), :], inc=f"ldp{b}")

    def emit_dec(n):
        b = n % 2
        ldv = ld_of[n]
        scb = s_sc[b][:].unsqueeze(2).broadcast_to([128, 64, 16])

        def wv_mk(par, b=b):
            return s_W[b][:, par::2].rearrange("p (g k) -> p g k", g=64), 64
        emit_dec_nvfp4(p, env, s_pk[b][:].bitcast(U8), 1024, s_W[b], scb, wv_mk,
                       [(f"ldp{b}", ldv), (pet, p.val(pet))], "dvd", dec)
        dec_of[n] = p.val(dvd)

    def emit_T(n):
        b = n % 2
        for grp in range(4):
            bk = pb[tb["j"] % 2]
            for k in range(4):
                ic = 4 * grp + k
                p.emit("tensor", lambda e, b=b, ic=ic, k=k, bk=bk: e.transpose(
                    bk.bitcast(BF16)[:, 128 * k:128 * (k + 1)],
                    s_W[b][:, 128 * ic:128 * (ic + 1)], env["ident"][:]),
                    waits=(tgrp_start_waits([(dvd, dec_of[n])]) if k == 0 else []),
                    inc=("pet" if k == 3 else None))
            p.emit("vector", lambda e, b=b, grp=grp, bk=bk: e.tensor_copy(
                s_wT[b][:].rearrange("p (ic q) -> p ic q", ic=16)[:, 4 * grp:4 * grp + 4, :],
                bk.bitcast(BF16)[:, 0:512].rearrange("p (t q) -> p t q", t=4)),
                waits=[(pet, p.val(pet)), (pem, p.val(pem) - 4)], inc="dve")
            tgrp_done(p.val("dve"))
        T_of[n] = p.val(dve)

    down_done = {-1: 0}
    h_of = {}
    bank_free = {tg: 0 for tg in range(4)}

    def emit_M(n):
        c, half = divmod(n, 2)
        b = n % 2
        slot = c % HRING
        for tg in range(4):
            bank = pb[4 + tg]
            for ic in range(16):
                w = []
                if ic == 0:
                    w = ([(dve, T_of[n]), bank_free[tg]] if bank_free[tg]
                         else [(dve, T_of[n])])
                p.emit("tensor", lambda e, b=b, ic=ic, tg=tg, bank=bank: e.matmul(
                    bank[:], s_wT[b][:, 128 * ic:128 * (ic + 1)],
                    xlt[:, 2048 * ic + 512 * tg:2048 * ic + 512 * (tg + 1)],
                    start=(ic == 0), stop=(ic == 15)),
                    waits=w, inc=("pem" if ic == 15 else None))
            pemv = p.val(pem)
            if half == 0:
                p.emit("vector", lambda e, tg=tg, bank=bank: e.tensor_copy(
                    s_gg[:, 512 * tg:512 * (tg + 1)], bank[:]),
                    waits=[(pem, pemv)], inc="dvg")
                bank_free[tg] = ("dvg", p.val(dvg))
                p.emit("scalar", lambda e, tg=tg: e.activation(
                    s_gg[:, 512 * tg:512 * (tg + 1)],
                    s_gg[:, 512 * tg:512 * (tg + 1)], AF.Gelu_apprx_tanh),
                    waits=[("dvg", p.val(dvg))], inc="ach")
            else:
                p.emit("vector", lambda e, tg=tg, slot=slot, bank=bank: e.tensor_mul(
                    s_h[slot][:, 512 * tg:512 * (tg + 1)],
                    s_gg[:, 512 * tg:512 * (tg + 1)], bank[:]),
                    waits=[(pem, pemv), (ach, p.val(ach)),
                           (pdn, down_done.get((c - HRING) // 8, 0))],
                    inc="dvh")
                bank_free[tg] = ("dvh", p.val(dvh))
        if half == 1:
            h_of[c] = p.val(dvh)

    wd_loaded = {}

    def emit_wd_loads(g):
        for ci in range(8):
            c = 8 * g + ci
            p.dma("gpsimd", s_wd[ci], wdT[128 * c:128 * (c + 1), :],
                  waits=[("stw0", sa_stw[g // 2][0]), ("stw1", sa_stw[g // 2][1]),
                         (pdn, down_done[g - 1]),
                         ("lnch", p.val("lnch"))], inc="ldd")
        wd_loaded[g] = p.val(ldd)

    def emit_down_regions(g, rr):
        hg_done = h_of[8 * g + 7]
        for r in rr:
            tc, fg = divmod(r, 4)
            bank = pb[2 + r % 2]
            for ci in range(8):
                slot = (8 * g + ci) % HRING
                p.emit("tensor", lambda e, ci=ci, tc=tc, fg=fg, bank=bank, slot=slot: e.matmul(
                    bank[:], s_h[slot][:, 128 * tc:128 * (tc + 1)],
                    s_wd[ci][:, 512 * fg:512 * (fg + 1)],
                    start=(ci == 0), stop=(ci == 7)),
                    waits=([(ldd, wd_loaded[g]), (dvh, hg_done),
                            (dvo, p.val(dvo) - 1)] if ci == 0 else []),
                    inc=("pdn" if ci == 7 else None))
            yb = r % 4
            pp = r % 2
            p.emit("vector", lambda e, yb=yb, bank=bank: e.tensor_copy(
                s_yst[yb][:], bank[:]),
                waits=[(pdn, p.val(pdn)), (f"sta{pp}", p.val(f"sta{pp}") - 16)],
                inc="dvo")
            gw = [(f"sta{pp}", sta_grp[g - 1][pp])] if g > 0 else []
            p.dma("gpsimd", out[128 * tc:128 * (tc + 1), 512 * fg:512 * (fg + 1)],
                  s_yst[yb][:],
                  waits=[(dvo, p.val(dvo)), ("sti", sti_v)] + gw, inc=f"sta{pp}",
                  accum_op=ALU.add)

    ld_of = {}
    dec_of = {}
    T_of = {}
    sta_grp = {}
    emit_5a_load(0)
    for i in range(32):
        emit_5a_dec(i)
        emit_5a_T(i)
    sa_next = 32
    emit_pk_load(0)
    ld_of[0] = p.val("ldp0")
    emit_dec(0)
    for n in range(NCH):
        if n + 1 < NCH:
            emit_pk_load(n + 1)
            ld_of[n + 1] = p.val(f"ldp{(n + 1) % 2}")
        emit_T(n)
        sa_this = sa_next if (n % 2 == 0 and sa_next < 128) else None
        if sa_this is not None:
            emit_5a_dec(sa_this)
            sa_next += 1
        if n + 1 < NCH:
            emit_dec(n + 1)
        emit_M(n)
        if sa_this is not None:
            emit_5a_T(sa_this)
        g = n // 16
        if n >= 16 and (n % 16) < 8:
            gprev = g - 1
            if n % 16 == 0:
                emit_wd_loads(gprev)
            emit_down_regions(gprev, range((n % 16) * 8, (n % 16) * 8 + 8))
            if n % 16 == 7:
                down_done[gprev] = p.val(pdn)
                sta_grp[gprev] = (p.val("sta0"), p.val("sta1"))
    emit_wd_loads(15)
    emit_down_regions(15, range(64))
    down_done[15] = p.val(pdn)
    sta_grp[15] = (p.val("sta0"), p.val("sta1"))
    return [("sta0", p.val("sta0")), ("sta1", p.val("sta1"))]


EI, EO, IN = "ExternalInput", "ExternalOutput", "Internal"

DECLS_K1 = [("xin", [T, D], F32, EI), ("wqkv", [D, 3 * D], F32, EI),
            ("wout", [D, D], F32, EI)]
DECLS_K2 = [("gup", [11264, 256], I32, EI), ("gus", [16, 11264], F16, EI),
            ("dp", [D, 704], I32, EI), ("ds", [44, D], F16, EI),
            ("wdT0", [5632, D], BF16, IN), ("yd", [T, D], F32, IN)]
DECLS_K3A = [("wq", [D, D], F32, EI), ("wk", [D, D], F32, EI), ("wv", [D, D], F32, EI),
             ("bq", [D], F32, EI), ("bk", [D], F32, EI), ("bv", [D], F32, EI),
             ("pos", [1, T], F32, EI), ("qT", [D, T], BF16, EO),
             ("kT", [D, T], BF16, EO), ("vv", [T, D], BF16, EO)]
DECLS_K3B = [("qT2", [D, T], BF16, EI), ("kg", [4 * D, T], BF16, EI),
             ("vg", [4 * T, D], BF16, EI), ("wo", [D, D], F32, EI), ("bo", [D], F32, EI)]
DECLS_K5 = [("gWp", [16384, 1024], I32, EI), ("gs", [16384, 64], F32, EI),
            ("uWp", [16384, 1024], I32, EI), ("us", [16384, 64], F32, EI),
            ("dWp", [D, 8192], I32, EI), ("dsc", [D, 512], F32, EI),
            ("wdT1", [16384, D], BF16, IN), ("out", [T, D], F32, EO)]


def build_launchA():
    decls = (DECLS_K1 + [("x1", [T, D], F32, IN)] + DECLS_K2 +
             [("x2", [T, D], F32, EO)] + DECLS_K3A)
    return build_launch([emit_k1, emit_k2, emit_k3a], decls)


def build_launchB():
    decls = (DECLS_K3B + [("x2b", [T, D], F32, EI), ("x3", [T, D], F32, IN)] + DECLS_K5)
    return build_launch([emit_k3b, emit_k5], decls, aliases={"x3b": "x3"})


def build_k1():
    return build_launch([emit_k1], DECLS_K1 + [("x1", [T, D], F32, EO)])


def build_k2():
    return build_launch([emit_k2], [("x1", [T, D], F32, EI)] + DECLS_K2 +
                        [("x2", [T, D], F32, EO)])


def build_k3a():
    return build_launch([emit_k3a], [("x2", [T, D], F32, EI)] + DECLS_K3A)


def build_k3b():
    return build_launch([emit_k3b], DECLS_K3B +
                        [("x2b", [T, D], F32, EI), ("x3", [T, D], F32, EO)])


def build_k5():
    return build_launch([emit_k5], [("x3b", [T, D], F32, EI)] + DECLS_K5)


def _shard_x(x):
    out = []
    for c in range(NCORES):
        b, s = c // 4, 2048 * (c % 4)
        out.append(np.ascontiguousarray(x[b, s:s + 2048, :], dtype=np.float32))
    return out


def kernel(**inputs):
    f32 = lambda k: np.ascontiguousarray(np.asarray(inputs[k], np.float32))
    i32 = lambda k: np.ascontiguousarray(np.asarray(inputs[k], np.int32))
    f16 = lambda k: np.ascontiguousarray(np.asarray(inputs[k], np.float16))

    xs = _shard_x(f32("L0_attn_x"))
    cores = list(range(NCORES))

    ncA = build_launchA()
    in_a = []
    for c in cores:
        s = 2048 * (c % 4)
        in_a.append({
            "xin": xs[c],
            "wqkv": f32("L0_attn_w_qkv"), "wout": f32("L0_attn_w_out"),
            "gup": i32("L0_ffn_gate_up_packed"), "gus": f16("L0_ffn_gate_up_scales"),
            "dp": i32("L0_ffn_down_packed"), "ds": f16("L0_ffn_down_scales"),
            "wq": f32("L1_attn_Wq"), "wk": f32("L1_attn_Wk"), "wv": f32("L1_attn_Wv"),
            "bq": f32("L1_attn_bq"), "bk": f32("L1_attn_bk"), "bv": f32("L1_attn_bv"),
            "pos": np.arange(s, s + 2048, dtype=np.float32).reshape(1, 2048),
        })
    rA = run_bass_kernel_spmd(ncA, in_a, core_ids=cores)
    x2 = [np.asarray(rA.results[c]["x2"]) for c in cores]
    qT = [np.asarray(rA.results[c]["qT"]) for c in cores]
    kT = [np.asarray(rA.results[c]["kT"]) for c in cores]
    vv = [np.asarray(rA.results[c]["vv"]) for c in cores]

    ncB = build_launchB()
    in_b = []
    for c in cores:
        b = c // 4
        grp = [4 * b + i for i in range(4)]
        in_b.append({
            "qT2": qT[c],
            "kg": np.ascontiguousarray(np.concatenate([kT[r] for r in grp], axis=0)),
            "vg": np.ascontiguousarray(np.concatenate([vv[r] for r in grp], axis=0)),
            "wo": f32("L1_attn_Wo"), "bo": f32("L1_attn_bo"), "x2b": x2[c],
            "gWp": i32("L1_ffn_gate_W_packed"), "gs": f32("L1_ffn_gate_scales"),
            "uWp": i32("L1_ffn_up_W_packed"), "us": f32("L1_ffn_up_scales"),
            "dWp": i32("L1_ffn_down_W_packed"), "dsc": f32("L1_ffn_down_scales"),
        })
    rB = run_bass_kernel_spmd(ncB, in_b, core_ids=cores)

    out = np.empty((2, 8192, 2048), np.float32)
    for c in cores:
        b, s = c // 4, 2048 * (c % 4)
        out[b, s:s + 2048, :] = rB.results[c]["out"]
    return out
